# version 23
# baseline (speedup 1.0000x reference)
import numpy as np
from contextlib import ExitStack

import concourse.bass as bass
import concourse.mybir as mybir
from concourse.bass_utils import run_bass_kernel_spmd

F32 = mybir.dt.float32
BF16 = mybir.dt.bfloat16
AFT = mybir.ActivationFunctionType
ALU = mybir.AluOpType

NCORES = 8
B, N, D = 2, 8192, 1024
T = 2048
S = 512
NS = T // S
C = 128
H = 8
NE = 8
HF = 256
EPS_LN = 1e-5
EPS_RMS = 1e-8
BIG = 1.0e9

ENGINES = ("sync", "tensor", "vector", "scalar", "gpsimd")


class Builder:
    def __init__(self, nc, sems):
        self.nc = nc
        self.sems = sems
        self.cnt = {k: 0 for k in sems}
        self.q = {e: [] for e in ENGINES}

    def wait(self, eng, sem, val):
        if not val or val <= 0:
            return
        h = self.sems[sem]
        self.q[eng].append(lambda E, h=h, v=val: E.wait_ge(h, v))

    def waitg(self, eng, guard):
        if guard:
            self.wait(eng, guard[0], guard[1])

    def op(self, eng, build, sem=None, by=1):
        tgt = None
        if sem is not None:
            self.cnt[sem] += by
            tgt = self.cnt[sem]
            h = self.sems[sem]

            def f(E, build=build, h=h, by=by):
                build(E).then_inc(h, by)
        else:
            def f(E, build=build):
                build(E)
        self.q[eng].append(f)
        return tgt

    def flush(self):
        nc = self.nc
        active = [e for e in ENGINES if self.q[e]]
        with nc.Block() as block:
            for e in active:
                lst = self.q[e]

                def body(E, lst=lst):
                    for f in lst:
                        f(E)

                getattr(block, e)(body)
        self.q = {e: [] for e in ENGINES}


def build_nc():
    nc = bass.Bass("TRN2", target_bir_lowering=False)

    dt = nc.dram_tensor
    x_h = dt("x", [T, D], F32, kind="ExternalInput").ap()
    wq_h = dt("wq", [D, D], F32, kind="ExternalInput").ap()
    wk_h = dt("wk", [D, D], F32, kind="ExternalInput").ap()
    wv_h = dt("wv", [D, D], F32, kind="ExternalInput").ap()
    wu_h = dt("wu", [D, D], F32, kind="ExternalInput").ap()
    wo_h = dt("wo", [D, D], F32, kind="ExternalInput").ap()
    wg_h = dt("wg", [D, NE], F32, kind="ExternalInput").ap()
    bqP_h = dt("bqp", [128, 8], F32, kind="ExternalInput").ap()
    bkP_h = dt("bkp", [128, 8], F32, kind="ExternalInput").ap()
    bv_h = dt("bvr", [1, D], F32, kind="ExternalInput").ap()
    bu_h = dt("bur", [1, D], F32, kind="ExternalInput").ap()
    bo_h = dt("bor", [1, D], F32, kind="ExternalInput").ap()
    scP_h = dt("scp", [128, 8], F32, kind="ExternalInput").ap()
    lns_h = dt("lns", [D, NE], F32, kind="ExternalInput").ap()
    lnb_h = dt("lnb", [D, NE], F32, kind="ExternalInput").ap()
    w1_h = dt("w1", [NE, D, HF], F32, kind="ExternalInput").ap()
    b1T_h = dt("b1t", [HF, NE], F32, kind="ExternalInput").ap()
    w2_h = dt("w2", [NE, HF, D], F32, kind="ExternalInput").ap()
    b2_h = dt("b2", [NE, D], F32, kind="ExternalInput").ap()
    id_h = dt("ident", [128, 128], F32, kind="ExternalInput").ap()
    cm_h = dt("cmask", [128, 128], F32, kind="ExternalInput").ap()
    pm_h = dt("pmask", [1, NCORES], F32, kind="ExternalInput").ap()
    out_h = dt("out", [T, D], F32, kind="ExternalOutput").ap()

    wqb_h = dt("wqbf", [2, 128, 4096], F32).ap()
    wkb_h = dt("wkbf", [2, 128, 4096], F32).ap()
    wvb_h = dt("wvbf", [2, 128, 4096], F32).ap()
    wub_h = dt("wubf", [2, 128, 4096], F32).ap()
    w12b_h = dt("w12bf", [NE, 128, 4096], F32).ap()
    qsp_h = dt("qspill", [NS, 128, H * S], F32).ap()
    osp_h = dt("ospill", [NS, 128, 4 * D], F32).ap()
    usp_h = dt("uspill", [NS, 128, 4 * D], F32).ap()
    xhsp_h = dt("xhspill", [NS, 128, 4096], F32).ap()
    agi_h = dt("agin", [128, H * 128], F32).ap()
    ago_h = dt("agout", [4 * 128, H * 128], F32).ap()

    es = ExitStack()
    with es:
        g1 = es.enter_context(nc.sbuf_tensor("g1", [128, 4096], F32))
        g2 = es.enter_context(nc.sbuf_tensor("g2", [128, 4096], F32))
        g3 = es.enter_context(nc.sbuf_tensor("g3", [128, 4096], F32))
        g4 = es.enter_context(nc.sbuf_tensor("g4", [128, 4096], F32))
        g5 = es.enter_context(nc.sbuf_tensor("g5", [128, 4096], F32))
        g6 = es.enter_context(nc.sbuf_tensor("g6", [128, 4096], F32))
        wba = es.enter_context(nc.sbuf_tensor("wba", [128, 4096], F32))
        wbb = es.enter_context(nc.sbuf_tensor("wbb", [128, 4096], F32))
        wos = es.enter_context(nc.sbuf_tensor("wos", [128, 8192], F32))
        ktm = es.enter_context(nc.sbuf_tensor("ktm", [128, 1024], F32))
        smb = es.enter_context(nc.sbuf_tensor("smb", [128, 1024], F32))
        mst = es.enter_context(nc.sbuf_tensor("mst", [128, 1024], F32))
        tmpa = es.enter_context(nc.sbuf_tensor("tmpa", [128, 512], F32))
        hbf = es.enter_context(nc.sbuf_tensor("hbf", [128, 512], F32))
        wbc = es.enter_context(nc.sbuf_tensor("wbc", [128, 2048], F32))
        wgb = es.enter_context(nc.sbuf_tensor("wgb", [128, 128], BF16))
        logit = es.enter_context(nc.sbuf_tensor("logit", [128, 128], F32))
        ident = es.enter_context(nc.sbuf_tensor("ident_s", [128, 128], F32))
        cmask = es.enter_context(nc.sbuf_tensor("cmask_s", [128, 128], F32))
        mfs = es.enter_context(nc.sbuf_tensor("mfs", [8, 512], F32))
        m1r = es.enter_context(nc.sbuf_tensor("m1r", [1, 2048], F32))
        b2s = es.enter_context(nc.sbuf_tensor("b2s", [8, 1024], F32))
        stat = es.enter_context(nc.sbuf_tensor("stat", [128, 96], F32))
        mvs = es.enter_context(nc.sbuf_tensor("mvs", [128, 16], F32))
        cols = es.enter_context(nc.sbuf_tensor("cols", [128, 32], F32))
        bqp = es.enter_context(nc.sbuf_tensor("bqp_s", [128, 8], F32))
        bkp = es.enter_context(nc.sbuf_tensor("bkp_s", [128, 8], F32))
        scp = es.enter_context(nc.sbuf_tensor("scp_s", [128, 8], F32))
        wgs = es.enter_context(nc.sbuf_tensor("wgs", [128, 64], F32))
        lnsp = es.enter_context(nc.sbuf_tensor("lnsp", [128, 64], F32))
        lnbp = es.enter_context(nc.sbuf_tensor("lnbp", [128, 64], F32))
        b1p = es.enter_context(nc.sbuf_tensor("b1p", [128, 16], F32))
        bgel = es.enter_context(nc.sbuf_tensor("bgel", [128, 16], F32))
        mrow = es.enter_context(nc.sbuf_tensor("mrow", [128, 8], F32))
        pmb = es.enter_context(nc.sbuf_tensor("pmb", [128, 8], F32))
        pmr = es.enter_context(nc.sbuf_tensor("pmr", [1, 8], F32))
        onesr = es.enter_context(nc.sbuf_tensor("onesr", [1, 128], F32))
        zrow = es.enter_context(nc.sbuf_tensor("zrow", [1, 128], F32))
        bigr = es.enter_context(nc.sbuf_tensor("bigr", [1, 128], F32))
        bor = es.enter_context(nc.sbuf_tensor("bor_s", [1, 1024], F32))
        p0 = es.enter_context(nc.psum_tensor("p0", [128, 512], F32))
        p1 = es.enter_context(nc.psum_tensor("p1", [128, 512], F32))
        p2 = es.enter_context(nc.psum_tensor("p2", [128, 512], F32))
        p3 = es.enter_context(nc.psum_tensor("p3", [128, 512], F32))
        p4 = es.enter_context(nc.psum_tensor("p4", [128, 512], F32))
        p5 = es.enter_context(nc.psum_tensor("p5", [128, 512], F32))
        p6 = es.enter_context(nc.psum_tensor("p6", [128, 512], F32))
        p7 = es.enter_context(nc.psum_tensor("p7", [128, 512], F32))
        s_lx = es.enter_context(nc.semaphore("s_lx"))
        s_wa = es.enter_context(nc.semaphore("s_wa"))
        s_wb = es.enter_context(nc.semaphore("s_wb"))
        s_lq = es.enter_context(nc.semaphore("s_lq"))
        s_sm = es.enter_context(nc.semaphore("s_sm"))
        s_sx = es.enter_context(nc.semaphore("s_sx"))
        s_lr = es.enter_context(nc.semaphore("s_lr"))
        s_lh = es.enter_context(nc.semaphore("s_lh"))
        s_sq = es.enter_context(nc.semaphore("s_sq"))
        s_wc = es.enter_context(nc.semaphore("s_wc"))
        s_wd = es.enter_context(nc.semaphore("s_wd"))
        s_st = es.enter_context(nc.semaphore("s_st"))
        s_pe = es.enter_context(nc.semaphore("s_pe"))
        s_dv = es.enter_context(nc.semaphore("s_dv"))
        s_ac = es.enter_context(nc.semaphore("s_ac"))
        s_sp = es.enter_context(nc.semaphore("s_sp"))
        s_cc = es.enter_context(nc.semaphore("s_cc"))
        sems = {"lx": s_lx, "wa": s_wa, "wb": s_wb, "lq": s_lq,
                "sm": s_sm, "sx": s_sx, "lr": s_lr, "lh": s_lh, "sq": s_sq, "wc": s_wc,
                "wd": s_wd, "st": s_st, "pe": s_pe,
                "dv": s_dv, "ac": s_ac, "sp": s_sp, "cc": s_cc}
        b = Builder(nc, sems)
        pbank = [p0, p1, p2, p3, p4, p5, p6, p7]
        tslot = [p4[:, 0:128], p7[:, 0:128]]

        p4g = [None, None]
        p4n = [0]

        for (dst, src) in [
            (ident[:, :], id_h[:, :]), (cmask[:, :], cm_h[:, :]),
            (bqp[:, :], bqP_h[:, :]), (bkp[:, :], bkP_h[:, :]),
            (scp[:, :], scP_h[:, :]), (b2s[:, :], b2_h[:, :]),
            (m1r[0:1, 0:1024], bv_h[:, :]), (m1r[0:1, 1024:2048], bu_h[:, :]),
            (bor[:, :], bo_h[:, :]), (pmr[:, :], pm_h[:, :]),
        ]:
            b.op("sync", lambda E, d=dst, s=src: E.dma_start(out=d, in_=s), "lq", 16)
        for ic in range(8):
            b.op("scalar", lambda E, ic=ic: E.dma_start(
                out=wgs[:, ic * 8:(ic + 1) * 8],
                in_=wg_h[ic * 128:(ic + 1) * 128, :]), "sm", 16)
            b.op("scalar", lambda E, ic=ic: E.dma_start(
                out=lnsp[:, ic * 8:(ic + 1) * 8],
                in_=lns_h[ic * 128:(ic + 1) * 128, :]), "sm", 16)
            b.op("scalar", lambda E, ic=ic: E.dma_start(
                out=lnbp[:, ic * 8:(ic + 1) * 8],
                in_=lnb_h[ic * 128:(ic + 1) * 128, :]), "sm", 16)
        for jc in range(2):
            b.op("scalar", lambda E, jc=jc: E.dma_start(
                out=b1p[:, jc * 8:(jc + 1) * 8],
                in_=b1T_h[jc * 128:(jc + 1) * 128, :]), "sm", 16)
        sm_consts = b.cnt["sm"]
        ld_consts = b.cnt["lq"]
        b.wait("sync", "lq", ld_consts)

        b.op("vector", lambda E: E.memset(mst[:, :], 0.0))
        b.op("vector", lambda E: E.memset(onesr[:, :], 1.0))
        b.op("vector", lambda E: E.memset(zrow[:, :], 0.0))
        dv_ones = b.op("vector", lambda E: E.memset(bigr[:, :], BIG), "dv")
        b.wait("tensor", "lq", ld_consts)
        b.wait("tensor", "dv", dv_ones)
        cbc = b.op("tensor", lambda E: E.matmul(
            p7[:, 0:8], onesr[0:1, 0:128], pmr[0:1, 0:8],
            start=True, stop=True), "pe")
        b.wait("vector", "pe", cbc)
        dv_pmb = b.op("vector", lambda E: E.tensor_copy(
            out=pmb[:, :], in_=p7[:, 0:8]), "dv")
        dv_b0 = dv_pmb
        for oh in range(2):
            b.wait("tensor", "dv", dv_b0)
            cb2 = b.op("tensor", lambda E, oh=oh: E.matmul(
                p7[0:8, :], onesr[0:1, 0:8], bor[0:1, oh * 512:(oh + 1) * 512],
                start=True, stop=True), "pe")
            b.wait("vector", "pe", cb2)
            dv_b0 = b.op("vector", lambda E, oh=oh: E.tensor_tensor(
                b2s[0:8, oh * 512:(oh + 1) * 512],
                b2s[0:8, oh * 512:(oh + 1) * 512],
                p7[0:8, :], op=ALU.add), "dv")
        b.flush()

        def ln_tile(i, src, dst, pre=()):
            st6 = stat[:, i * 12:(i + 1) * 12]
            for g in pre:
                b.waitg("vector", g)
            b.op("vector", lambda E, i=i, st6=st6, src=src: E.bn_stats(
                st6[:, 0:6], src[:, 0:512]))
            cst = b.op("vector", lambda E, i=i, st6=st6, src=src: E.bn_stats(
                st6[:, 6:12], src[:, 512:1024]), "dv")
            b.wait("vector", "dv", cst)
            cag = b.op("vector", lambda E, i=i, st6=st6: E.bn_aggr(
                mvs[:, 2 * i:2 * i + 2], st6[:, 0:12]), "dv")
            b.wait("vector", "dv", cag)
            c = b.op("vector", lambda E, i=i: E.tensor_scalar(
                cols[:, i:i + 1], mvs[:, 2 * i + 1:2 * i + 2], EPS_LN, None,
                op0=ALU.add), "dv")
            b.wait("scalar", "dv", c)
            c2 = b.op("scalar", lambda E, i=i: E.activation(
                cols[:, 8 + i:9 + i], cols[:, i:i + 1], AFT.Sqrt), "ac")
            b.wait("vector", "ac", c2)
            crc = b.op("vector", lambda E, i=i: E.reciprocal(
                cols[:, 16 + i:17 + i], cols[:, 8 + i:9 + i]), "dv")
            b.wait("vector", "dv", crc)
            c = b.op("vector", lambda E, i=i, src=src, dst=dst: E.tensor_scalar(
                dst[:, :], src[:, :], mvs[:, 2 * i:2 * i + 1],
                cols[:, 16 + i:17 + i], op0=ALU.subtract, op1=ALU.mult), "dv")
            return c

        def ln_pass(src, dst, pre_dv_guard=None):
            outs = []
            for i in range(4):
                pre = (pre_dv_guard,) if (i == 0 and pre_dv_guard) else ()
                outs.append(ln_tile(i, src(i), dst(i), pre))
            return outs

        def transpose_pass(src, dst, lnc, writer_pre, xslots=()):
            last = 0
            first = True
            dstb = dst[:, :].bitcast(BF16)
            slots = [tslot[0], tslot[1]] + [x[0] for x in xslots]
            gl = [p4g[0], p4g[1]] + [x[1] for x in xslots]
            nsl = len(slots)
            rot = 0
            for i in range(4):
                for ic in range(8):
                    pre = [("dv", lnc[i])] if ic == 0 else []
                    sl = rot % nsl
                    rot += 1
                    tb = slots[sl]
                    b.waitg("tensor", gl[sl])
                    for g in pre:
                        b.waitg("tensor", g)
                    cpe = b.op("tensor", lambda E, i=i, ic=ic, tb=tb: E.transpose(
                        tb, src(i)[:, ic * 128:(ic + 1) * 128], ident[:, :]), "pe")
                    if first:
                        for g in writer_pre:
                            b.waitg("scalar", g)
                            b.waitg("vector", g)
                        first = False
                    hi = dstb[:, ic * 512 + i * 128: ic * 512 + (i + 1) * 128]
                    lo = dstb[:, 4096 + ic * 512 + i * 128:
                              4096 + ic * 512 + (i + 1) * 128]
                    b.wait("scalar", "pe", cpe)
                    cac = b.op("scalar", lambda E, hi=hi, tb=tb: E.copy(
                        out=hi, in_=tb), "ac")
                    b.wait("vector", "ac", cac)
                    last = b.op("vector", lambda E, hi=hi, lo=lo, tb=tb:
                                E.tensor_tensor(lo, tb, hi, op=ALU.subtract), "dv")
                    gl[sl] = ("dv", last)
            p4g[0] = gl[0]
            p4g[1] = gl[1]
            return last

        WHALF = {"q": wq_h, "k": wk_h, "v": wv_h, "u": wu_h}
        wslot = [wba, wbb]

        st_qsp = [0] * NS
        st_osp = [0] * NS
        st_usp = [0] * NS
        st_xhsp = [0] * NS
        pe_xnf_reader = 0
        pe_xw_reader = 0
        wload_seq = [0]
        wbuf_free = [None, None]
        ld_w = {}

        wbc_free = [0, 0]
        wbc_par = [0]

        def land_split(pieces, dsts, scale_cols=None):
            last = 0
            for (dmas, hi_ap, lo_ap, scol) in pieces:
                hp = wbc_par[0] % 2
                wbc_par[0] += 1
                base = hp * 1024
                wsem = "wc" if hp == 0 else "wd"
                b.wait("sync", "dv", wbc_free[hp])
                cc = 0
                for ent in dmas:
                    if ent[0] == "ap3":
                        cc = b.op("sync", lambda E, soff=ent[1], sten=ent[2],
                                  base=base: E.dma_start(
                                      out=bass.AP(wbc, base,
                                                  [[2048, 128], [512, 2],
                                                   [1, 512]]),
                                      in_=bass.AP(sten, soff,
                                                  [[1024, 128], [131072, 2],
                                                   [1, 512]])), wsem, 16)
                    else:
                        (src_ap, c0, nc_) = ent
                        cc = b.op("sync", lambda E, src_ap=src_ap, c0=c0,
                                  nc_=nc_, base=base: E.dma_start(
                                      out=wbc[:, base + c0:base + c0 + nc_],
                                      in_=src_ap), wsem, 16)
                n = sum(1024 if x[0] == "ap3" else x[2] for x in dmas)
                b.wait("scalar", wsem, cc)
                if scol is not None:
                    cac = b.op("scalar", lambda E, hi_ap=hi_ap, n=n, scol=scol,
                               base=base:
                               E.activation(hi_ap, wbc[:, base:base + n],
                                            AFT.Copy, scale=scol), "ac")
                    b.wait("vector", "ac", cac)
                    last = b.op("vector", lambda E, hi_ap=hi_ap, lo_ap=lo_ap,
                                n=n, scol=scol, base=base:
                                E.scalar_tensor_tensor(
                                    lo_ap, wbc[:, base:base + n], scol, hi_ap,
                                    op0=ALU.mult, op1=ALU.subtract), "dv")
                else:
                    cac = b.op("scalar", lambda E, hi_ap=hi_ap, n=n, base=base:
                               E.copy(out=hi_ap, in_=wbc[:, base:base + n]),
                               "ac")
                    b.wait("vector", "ac", cac)
                    last = b.op("vector", lambda E, hi_ap=hi_ap, lo_ap=lo_ap,
                                n=n, base=base:
                                E.tensor_tensor(lo_ap, wbc[:, base:base + n],
                                                hi_ap, op=ALU.subtract), "dv")
                wbc_free[hp] = last
            return ("dv", last)

        wst = {}
        slot_store = [0, 0]
        WBF = {"q": wqb_h, "k": wkb_h, "v": wvb_h, "u": wub_h}

        def load_whalf(which, half, tag):
            sl = wload_seq[0] % 2
            wload_seq[0] += 1
            dst = wslot[sl][:, :].bitcast(BF16)
            if tag[0] == 0:
                b.waitg("scalar", wbuf_free[sl])
                b.waitg("vector", wbuf_free[sl])
                stq = b.cnt["lq"] if slot_store[sl] else 0
                b.wait("scalar", "lq", stq)
                b.wait("vector", "lq", stq)
                pieces = []
                for p in range(4):
                    dmas = [("ap3", (2 * p) * 128 * 1024 + half * 512,
                             WHALF[which].tensor)]
                    hi_ap = dst[:, p * 1024:(p + 1) * 1024]
                    lo_ap = dst[:, 4096 + p * 1024: 4096 + (p + 1) * 1024]
                    pieces.append((dmas, hi_ap, lo_ap, None))
                ld_w[tag] = land_split(pieces, None)
                b.wait("scalar", "dv", ld_w[tag][1])
                cst = b.op("scalar", lambda E, which=which, half=half, sl=sl:
                           E.dma_start(out=WBF[which][half],
                                       in_=wslot[sl][:, :]), "lq", 16)
                wst[(which, half)] = cst
                slot_store[sl] = cst
            else:
                b.waitg("sync", wbuf_free[sl])
                b.wait("sync", "lq", b.cnt["lq"])
                wsem = "wa" if sl == 0 else "wb"
                cc = b.op("sync", lambda E, which=which, half=half, sl=sl:
                          E.dma_start(out=wslot[sl][:, :],
                                      in_=WBF[which][half]), wsem, 16)
                ld_w[tag] = (wsem, cc)
            return sl

        pgrp = [0]
        bank_dv = {5: 0, 6: 0}
        bank_ac = {5: 0, 6: 0}
        exp_guard = [0]
        elu_prev = [0]

        def fm_half(dst, biasP, oh, sl, ldtag):
            last_dv = 0
            last_pe = 0
            if True:
                for ocl in range(4):
                    oc = oh * 4 + ocl
                    pb = 5 + (pgrp[0] % 2)
                    pgrp[0] += 1
                    ps = pbank[pb]
                    b.waitg("tensor", ld_w[ldtag])
                    b.wait("tensor", "dv", bank_dv[pb])
                    b.wait("tensor", "ac", bank_ac[pb])
                    cpe = 0
                    wb_ = wslot[sl][:, :].bitcast(BF16)
                    g2b = g2[:, :].bitcast(BF16)
                    for ic in range(8):
                        whi = wb_[:, ic * 512 + ocl * 128: ic * 512 + (ocl + 1) * 128]
                        wlo = wb_[:, 4096 + ic * 512 + ocl * 128:
                                  4096 + ic * 512 + (ocl + 1) * 128]
                        xhi = g2b[:, ic * 512:(ic + 1) * 512]
                        xlo = g2b[:, 4096 + ic * 512: 4096 + (ic + 1) * 512]
                        st0 = (ic == 0)
                        b.op("tensor", lambda E, whi=whi, xhi=xhi, ps=ps, st0=st0:
                             E.matmul(ps[:, :], whi, xhi,
                                      start=st0, stop=False), "pe")
                        b.op("tensor", lambda E, whi=whi, xlo=xlo, ps=ps:
                             E.matmul(ps[:, :], whi, xlo,
                                      start=False, stop=False), "pe")
                        cpe = b.op("tensor", lambda E, wlo=wlo, xhi=xhi, ps=ps,
                                   ic=ic: E.matmul(ps[:, :], wlo, xhi,
                                                   start=False, stop=(ic == 7)),
                                   "pe")
                    last_pe = cpe
                    ta = tmpa[:, 0:512]
                    dsl = dst[:, oc * 512:(oc + 1) * 512]
                    b.wait("vector", "pe", cpe)
                    b.wait("vector", "ac", exp_guard[0])
                    b.wait("vector", "dv", elu_prev[0])
                    b.op("vector", lambda E, ps=ps, oc=oc, ta=ta, biasP=biasP:
                         E.tensor_scalar(ta, ps[:, :], biasP[:, oc:oc + 1], 0.0,
                                         op0=ALU.add, op1=ALU.min), "dv")
                    cdv = b.op("vector", lambda E, ps=ps, oc=oc, dsl=dsl, biasP=biasP:
                               E.tensor_scalar(dsl, ps[:, :], biasP[:, oc:oc + 1], 0.0,
                                               op0=ALU.add, op1=ALU.max), "dv")
                    bank_dv[pb] = cdv
                    b.wait("scalar", "dv", cdv)
                    cac = b.op("scalar", lambda E, ta=ta:
                               E.activation(ta, ta, AFT.Exp), "ac")
                    exp_guard[0] = cac
                    b.wait("vector", "ac", cac)
                    last_dv = b.op("vector", lambda E, ta=ta, dsl=dsl:
                                   E.scalar_tensor_tensor(
                                       dsl, ta, -1.0, dsl,
                                       op0=ALU.add, op1=ALU.add), "dv")
                    elu_prev[0] = last_dv
            return last_dv, last_pe

        dv_msk = {}
        st_cpy = {}
        pe_oint = {}
        pe_smm = {}
        dv_logit = 0
        ac_oevac_prev = 0
        for stb in (p2, p3):
            b.op("tensor", lambda E, stb=stb: E.matmul(
                stb[:, :], zrow[0:1, 0:128], bor[0:1, 0:512],
                start=True, stop=True), "pe")

        c = 0
        for i in range(4):
            c = b.op("gpsimd", lambda E, i=i: E.dma_start(
                out=g1[:, i * 1024:(i + 1) * 1024],
                in_=x_h[i * 128:(i + 1) * 128, :]), "lx", 16)
        lnc_cur = [ln_tile(4 + i, g1[:, i * 1024:(i + 1) * 1024],
                           g1[:, i * 1024:(i + 1) * 1024],
                           pre=(("lx", c),)) for i in range(4)]

        slq0_next = load_whalf("q", 0, (0, "q", 0))
        slq1_next = load_whalf("q", 1, (0, "q", 1))
        b.wait("scalar", "sm", sm_consts)
        cwg = b.op("scalar", lambda E: E.copy(out=wgb[:, 0:64], in_=wgs[:, :]), "ac")
        b.wait("vector", "ac", cwg)
        dv_wg = b.op("vector", lambda E: E.tensor_tensor(
            wgb[:, 64:128], wgs[:, :], wgb[:, 0:64], op=ALU.subtract), "dv")
        for s in range(NS):
            slq0 = slq0_next
            slq1 = slq1_next

            og = ("ac", ac_oevac_prev) if s > 0 else None
            ac_xnf = transpose_pass(lambda i: g1[:, i * 1024:(i + 1) * 1024],
                                    g2, lnc_cur, [("pe", pe_xnf_reader),
                                                  ("sx", st_xhsp[s - 1] if s > 0 else 0)],
                                    xslots=[(p0[:, 0:128], og),
                                            (p1[:, 0:128], og)])
            pe_xw_reader = b.cnt["pe"]

            if s > 0:
                b.wait("vector", "st", st_osp[s - 1])
            b.wait("tensor", "dv", ac_xnf)

            def tm_half(oh, sl, ldtag, dst, brow_off):
                last_ac = 0
                cpe = 0
                wb_ = wslot[sl][:, :].bitcast(BF16)
                g2b = g2[:, :].bitcast(BF16)
                for sc in range(4):
                    pb = 5 + (pgrp[0] % 2)
                    pgrp[0] += 1
                    ps = pbank[pb]
                    b.waitg("tensor", ld_w[ldtag])
                    b.wait("tensor", "dv", bank_dv[pb])
                    b.wait("tensor", "ac", bank_ac[pb])
                    b.op("tensor", lambda E, ps=ps, oh=oh, brow_off=brow_off:
                         E.matmul(ps[:, :], onesr[0:1, 0:128],
                                  m1r[0:1, brow_off + oh * 512:
                                      brow_off + (oh + 1) * 512],
                                  start=True, stop=False), "pe")
                    for ic in range(8):
                        xhi = g2b[:, ic * 512 + sc * 128: ic * 512 + (sc + 1) * 128]
                        xlo = g2b[:, 4096 + ic * 512 + sc * 128:
                                  4096 + ic * 512 + (sc + 1) * 128]
                        whi = wb_[:, ic * 512:(ic + 1) * 512]
                        wlo = wb_[:, 4096 + ic * 512: 4096 + (ic + 1) * 512]
                        b.op("tensor", lambda E, xhi=xhi, whi=whi, ps=ps:
                             E.matmul(ps[:, :], xhi, whi,
                                      start=False, stop=False), "pe")
                        b.op("tensor", lambda E, xhi=xhi, wlo=wlo, ps=ps:
                             E.matmul(ps[:, :], xhi, wlo,
                                      start=False, stop=False), "pe")
                        cpe = b.op("tensor", lambda E, xlo=xlo, whi=whi, ps=ps,
                                   ic=ic: E.matmul(ps[:, :], xlo, whi,
                                                   start=False, stop=(ic == 7)),
                                   "pe")
                    dsl = dst[:, sc * 1024 + oh * 512: sc * 1024 + (oh + 1) * 512]
                    b.wait("scalar", "pe", cpe)
                    last_ac = b.op("scalar", lambda E, dsl=dsl, ps=ps:
                                   E.activation(dsl, ps[:, :], AFT.Silu), "ac")
                    bank_ac[pb] = last_ac
                    bank_dv[pb] = 0
                return cpe, last_ac

            def v_half(oh, sl, ldtag):
                return tm_half(oh, sl, ldtag, g5, 0)

            dv_q0, pe_q0 = fm_half(g3, bqp, 0, slq0, (s, "q", 0))
            wbuf_free[slq0] = ("pe", pe_q0)
            slk0 = load_whalf("k", 0, (s, "k", 0))
            dv_qf, pe_q1 = fm_half(g3, bqp, 1, slq1, (s, "q", 1))
            wbuf_free[slq1] = ("pe", pe_q1)
            slk1 = load_whalf("k", 1, (s, "k", 1))
            dv_k0, pe_k0 = fm_half(g4, bkp, 0, slk0, (s, "k", 0))
            wbuf_free[slk0] = ("pe", pe_k0)
            slv0 = load_whalf("v", 0, (s, "v", 0))
            dv_kf, pe_k1 = fm_half(g4, bkp, 1, slk1, (s, "k", 1))
            wbuf_free[slk1] = ("pe", pe_k1)
            slv1 = load_whalf("v", 1, (s, "v", 1))
            pe_v0, ac_v0 = v_half(0, slv0, (s, "v", 0))
            wbuf_free[slv0] = ("pe", pe_v0)
            slu0 = load_whalf("u", 0, (s, "u", 0))
            pe_v1, last_ac_v = v_half(1, slv1, (s, "v", 1))
            wbuf_free[slv1] = ("pe", pe_v1)
            slu1 = load_whalf("u", 1, (s, "u", 1))
            dv_qf = max(dv_qf, dv_q0)
            dv_kf = max(dv_kf, dv_k0)

            def u_half(oh, sl, ldtag, first):
                if first:
                    b.wait("scalar", "st", st_osp[s - 1] if s > 0 else 0)
                return tm_half(oh, sl, ldtag, g6, 1024)

            pe_u0, ac_u0 = u_half(0, slu0, (s, "u", 0), True)
            wbuf_free[slu0] = ("pe", pe_u0)
            pe_u1, ac_u1 = u_half(1, slu1, (s, "u", 1), False)
            wbuf_free[slu1] = ("pe", pe_u1)
            b.wait("scalar", "ac", ac_u1)
            st_usp[s] = b.op("scalar", lambda E, s=s: E.dma_start(
                out=usp_h[s, :, :], in_=g6[:, :]), "st", 16)

            for cl in range(4):
                gc = 4 * s + cl
                pb = 5 + (pgrp[0] % 2)
                pgrp[0] += 1
                ps = pbank[pb]
                if s == 0 and cl == 0:
                    b.wait("tensor", "dv", dv_wg)
                b.wait("tensor", "dv", bank_dv[pb])
                b.wait("tensor", "ac", bank_ac[pb])
                cpe = 0
                g2b = g2[:, :].bitcast(BF16)
                for ic in range(8):
                    xhi = g2b[:, ic * 512 + cl * 128: ic * 512 + (cl + 1) * 128]
                    xlo = g2b[:, 4096 + ic * 512 + cl * 128:
                              4096 + ic * 512 + (cl + 1) * 128]
                    whi = wgb[:, ic * 8:(ic + 1) * 8]
                    wlo = wgb[:, 64 + ic * 8: 64 + (ic + 1) * 8]
                    st0 = (ic == 0)
                    b.op("tensor", lambda E, xhi=xhi, whi=whi, ps=ps, st0=st0:
                         E.matmul(ps[:, 0:8], xhi, whi, start=st0,
                                  stop=False), "pe")
                    b.op("tensor", lambda E, xhi=xhi, wlo=wlo, ps=ps: E.matmul(
                        ps[:, 0:8], xhi, wlo, start=False, stop=False), "pe")
                    cpe = b.op("tensor", lambda E, xlo=xlo, whi=whi, ps=ps, ic=ic:
                               E.matmul(ps[:, 0:8], xlo, whi,
                                        start=False, stop=(ic == 7)), "pe")
                b.wait("vector", "pe", cpe)
                dv_logit = b.op("vector", lambda E, gc=gc, ps=ps: E.tensor_copy(
                    out=logit[:, gc * 8:(gc + 1) * 8], in_=ps[:, 0:8]), "dv")
                bank_dv[pb] = dv_logit
            pe_g2_free = b.cnt["pe"]

            lnc2 = [ln_tile(0, g1[:, 0:1024], g1[:, 0:1024],
                            pre=(("pe", pe_xw_reader),))]

            if s > 0:
                b.wait("scalar", "st", st_osp[s - 1])
            ac_oevac = ac_oevac_prev
            pairs = [(cl, h) for cl in range(4) for h in range(8)]

            def emit_TS(j):
                cl, h = pairs[j]
                gc = 4 * s + cl
                sbank = p5 if h % 2 == 0 else p6
                pre = [("dv", dv_kf)] if j == 0 else []
                sl = p4n[0] % 2
                p4n[0] += 1
                tb = tslot[sl]
                b.waitg("tensor", p4g[sl])
                for g in pre:
                    b.waitg("tensor", g)
                ct = b.op("tensor", lambda E, h=h, cl=cl, tb=tb: E.transpose(
                    tb, g4[:, h * 512 + cl * 128: h * 512 + (cl + 1) * 128],
                    ident[:, :]), "pe")
                b.wait("scalar", "pe", max(ct, pe_smm.get((gc - 1, h), 0)))
                cke = b.op("scalar", lambda E, h=h, tb=tb: E.copy(
                    out=ktm[:, h * 128:(h + 1) * 128], in_=tb), "ac")
                p4g[sl] = ("ac", cke)
                ktm_ev[(gc, h)] = cke
                if j >= 2:
                    pcl, ph = pairs[j - 2]
                    b.wait("tensor", "dv", dv_msk.get((4 * s + pcl, ph), 0))
                else:
                    b.wait("tensor", "dv", dv_msk.get((4 * s - 1, 6 + j), 0))
                    pb = 5 + (h % 2)
                    b.wait("tensor", "dv", bank_dv[pb])
                    b.wait("tensor", "ac", bank_ac[pb])
                if j == 0:
                    b.wait("tensor", "dv", max(dv_qf, dv_kf))
                    b.wait("tensor", "ac", last_ac_v)
                cs = b.op("tensor", lambda E, h=h, cl=cl, sbank=sbank: E.matmul(
                    sbank[:, 0:128],
                    g4[:, h * 512 + cl * 128: h * 512 + (cl + 1) * 128],
                    g3[:, h * 512 + cl * 128: h * 512 + (cl + 1) * 128],
                    start=True, stop=True), "pe")
                b.wait("vector", "pe", max(cs, pe_oint.get((gc - 1, h), 0)))
                cm_ = b.op("vector", lambda E, h=h, sbank=sbank: E.tensor_tensor(
                    smb[:, h * 128:(h + 1) * 128], cmask[:, :], sbank[:, 0:128],
                    op=ALU.mult), "dv")
                dv_msk[(gc, h)] = cm_

            ktm_ev = {}
            emit_TS(0)
            emit_TS(1)
            for j, (cl, h) in enumerate(pairs):
                gc = 4 * s + cl
                if j + 2 < len(pairs):
                    emit_TS(j + 2)
                po = p0 if h < 4 else p1
                off = (h % 4) * 128
                b.wait("tensor", "dv", dv_msk[(gc, h)])
                if h == 0:
                    b.wait("tensor", "ac", ac_oevac)
                b.op("tensor", lambda E, h=h, cl=cl, po=po, off=off: E.matmul(
                    po[:, off:off + 128], smb[:, h * 128:(h + 1) * 128],
                    g5[:, cl * 1024 + h * 128: cl * 1024 + (h + 1) * 128],
                    start=True, stop=False), "pe")
                b.wait("tensor", "dv", st_cpy.get(gc - 1, 0))
                coi = b.op("tensor", lambda E, h=h, cl=cl, po=po, off=off:
                           E.matmul(
                               po[:, off:off + 128],
                               g3[:, h * 512 + cl * 128: h * 512 + (cl + 1) * 128],
                               mst[:, h * 128:(h + 1) * 128],
                               start=False, stop=True), "pe")
                pe_oint[(gc, h)] = coi
                stb = p2 if h < 4 else p3
                b.wait("tensor", "ac", ktm_ev[(gc, h)])
                if h == 0:
                    b.wait("tensor", "dv", st_cpy.get(gc - 1, 0))
                csm = b.op("tensor", lambda E, h=h, cl=cl, stb=stb, off=off:
                           E.matmul(
                    stb[:, off:off + 128], ktm[:, h * 128:(h + 1) * 128],
                    g5[:, cl * 1024 + h * 128: cl * 1024 + (h + 1) * 128],
                    start=False, stop=False, skip_group_check=True), "pe")
                pe_smm[(gc, h)] = csm
                ic = h
                sl = p4n[0] % 2
                p4n[0] += 1
                tb = tslot[sl]
                b.waitg("tensor", p4g[sl])
                if ic == 0:
                    b.wait("tensor", "dv", lnc2[cl])
                ctp_xh_last = b.op("tensor", lambda E, ic=ic, cl=cl, tb=tb:
                                   E.transpose(
                                       tb, g1[:, cl * 1024 + ic * 128:
                                              cl * 1024 + (ic + 1) * 128],
                                       ident[:, :]), "pe")
                if ic == 0 and cl == 0:
                    b.wait("scalar", "pe", pe_g2_free)
                    b.wait("vector", "pe", pe_g2_free)
                g2b_ = g2[:, :].bitcast(BF16)
                xh_hi = g2b_[:, ic * 512 + cl * 128: ic * 512 + (cl + 1) * 128]
                xh_lo = g2b_[:, 4096 + ic * 512 + cl * 128:
                             4096 + ic * 512 + (cl + 1) * 128]
                b.wait("scalar", "pe", ctp_xh_last)
                cxa = b.op("scalar", lambda E, xh_hi=xh_hi, tb=tb:
                           E.copy(out=xh_hi, in_=tb), "ac")
                b.wait("vector", "ac", cxa)
                ac_ev_xh = b.op("vector", lambda E, xh_hi=xh_hi, xh_lo=xh_lo,
                                tb=tb: E.tensor_tensor(
                                    xh_lo, tb, xh_hi, op=ALU.subtract), "dv")
                p4g[sl] = ("dv", ac_ev_xh)

                if h == 7:
                    if cl == 0:
                        b.wait("scalar", "st", st_usp[s])
                    b.wait("vector", "pe", max(csm, pe_oint[(gc, 7)]))
                    b.op("vector", lambda E: E.tensor_copy(
                        out=mst[:, 0:512], in_=p2[:, :]), "dv")
                    st_cpy[gc] = b.op("vector", lambda E: E.tensor_copy(
                        out=mst[:, 512:1024], in_=p3[:, :]), "dv")
                    b.wait("scalar", "pe", pe_oint[(gc, 7)])
                    b.op("scalar", lambda E, cl=cl: E.copy(
                        out=g6[:, cl * 1024: cl * 1024 + 512], in_=p0[:, :]), "ac")
                    ac_oevac = b.op("scalar", lambda E, cl=cl: E.copy(
                        out=g6[:, cl * 1024 + 512: (cl + 1) * 1024],
                        in_=p1[:, :]), "ac")
                    b.wait("scalar", "dv", dv_qf)
                    st_qsp[s] = b.op("scalar", lambda E, s=s, cl=cl: E.dma_start(
                        out=bass.AP(qsp_h.tensor, s * 128 * 4096 + cl * 128,
                                    [[4096, 128], [512, 8], [1, 128]]),
                        in_=bass.AP(g3, cl * 128,
                                    [[4096, 128], [512, 8], [1, 128]])),
                        "st", 16)
                    b.wait("scalar", "ac", ac_oevac)
                    st_osp[s] = b.op("scalar", lambda E, s=s, cl=cl: E.dma_start(
                        out=osp_h[s, :, cl * 1024:(cl + 1) * 1024],
                        in_=g6[:, cl * 1024:(cl + 1) * 1024]), "st", 16)
                    if s < NS - 1 and cl == 1:
                        slq0_next = load_whalf("q", 0, (s + 1, "q", 0))
                    if s < NS - 1 and cl == 2:
                        slq1_next = load_whalf("q", 1, (s + 1, "q", 1))
                    if s < NS - 1:
                        b.wait("gpsimd", "pe", ctp_xh_last)
                        ldn = b.op("gpsimd", lambda E, s=s, cl=cl: E.dma_start(
                            out=g1[:, cl * 1024:(cl + 1) * 1024],
                            in_=x_h[(s + 1) * 512 + cl * 128:
                                    (s + 1) * 512 + (cl + 1) * 128, :]), "lx", 16)
                    b.wait("scalar", "dv", ac_ev_xh)
                    st_xhsp[s] = b.op("scalar", lambda E, s=s, cl=cl: E.dma_start(
                        out=bass.AP(xhsp_h.tensor, s * 128 * 4096 + cl * 64,
                                    [[4096, 128], [2048, 2], [256, 8],
                                     [1, 64]]),
                        in_=bass.AP(g2, cl * 64,
                                    [[4096, 128], [2048, 2], [256, 8],
                                     [1, 64]])),
                        "sx", 16)
                    if cl < 3:
                        lnc2.append(ln_tile(cl + 1,
                                            g1[:, (cl + 1) * 1024:(cl + 2) * 1024],
                                            g1[:, (cl + 1) * 1024:(cl + 2) * 1024]))
            ac_oevac_prev = ac_oevac

            if s < NS - 1:
                lnc_cur = [ln_tile(4 + i, g1[:, i * 1024:(i + 1) * 1024],
                                   g1[:, i * 1024:(i + 1) * 1024],
                                   pre=(("pe", ctp_xh_last), ("lx", ldn)))
                           for i in range(4)]
            bank_dv[5] = max(bank_dv[5], dv_msk[(4 * s + 3, 6)])
            bank_dv[6] = max(bank_dv[6], dv_msk[(4 * s + 3, 7)])

            pe_xnf_reader = b.cnt["pe"]

        st_spill_total = b.cnt["st"]
        sx_spill_total = b.cnt["sx"]
        pe_b1_end = b.cnt["pe"]
        ac_b1_end = b.cnt["ac"]
        dv_b1_end = b.cnt["dv"]
        pe_oint_f = pe_oint[(4 * NS - 1, 7)]
        pe_smm_f = pe_smm[(4 * NS - 1, 7)]
        dv_msk_f = dv_msk[(4 * NS - 1, 7)]
        st_cpy_f = st_cpy[4 * NS - 1]

        b.wait("gpsimd", "dv", st_cpy_f)
        b.wait("gpsimd", "pe", pe_oint_f)
        c0 = b.op("gpsimd", lambda E: E.dma_start(
            out=agi_h[:, :], in_=mst[:, :]), "sp", 16)
        b.wait("gpsimd", "sp", c0)
        ccv = b.op("gpsimd", lambda E: E.collective_compute(
            "AllGather", ALU.bypass,
            replica_groups=[[0, 1, 2, 3], [4, 5, 6, 7]],
            ins=[agi_h[:, :]], outs=[ago_h[:, :]]), "cc")
        b.wait("gpsimd", "st", st_spill_total)
        b.wait("gpsimd", "sx", sx_spill_total)
        b.wait("gpsimd", "pe", pe_b1_end)
        b.wait("gpsimd", "dv", dv_b1_end)
        b.wait("gpsimd", "ac", ac_b1_end)
        b.op("gpsimd", lambda E: E.dma_start(
            out=g3[:, :], in_=osp_h[0, :, :]), "lr", 16)
        b.op("gpsimd", lambda E: E.dma_start(
            out=g5[:, :], in_=usp_h[0, :, :]), "lr", 16)
        ldo0 = b.op("gpsimd", lambda E: E.dma_start(
            out=g6[:, :], in_=qsp_h[0, :, :]), "lr", 16)
        ldxh0 = b.op("gpsimd", lambda E: E.dma_start(
            out=g4[:, :], in_=xhsp_h[0, :, :]), "lh", 16)
        ldx0 = []
        for i in range(4):
            c = b.op("gpsimd", lambda E, i=i: E.dma_start(
                out=g1[:, i * 1024:(i + 1) * 1024],
                in_=x_h[i * 128:(i + 1) * 128, :]), "lx", 16)
            ldx0.append(c)
        b.wait("vector", "sp", c0)
        b.wait("vector", "pe", pe_oint_f)
        dv_mz = b.op("vector", lambda E: E.memset(mst[:, :], 0.0), "dv")

        dv_bg_e = {}
        wst12 = {}
        b.wait("tensor", "sm", sm_consts)
        b.wait("scalar", "sm", sm_consts)
        b.wait("vector", "sm", sm_consts)
        b.wait("scalar", "pe", pe_b1_end)
        b.wait("scalar", "sx", sx_spill_total)
        b.wait("vector", "pe", pe_b1_end)
        b.wait("vector", "sx", sx_spill_total)
        g2w = g2[:, :].bitcast(BF16)

        def split_w12(e, bvb):
            b.wait("sync", "dv", max(wbc_free[0], wbc_free[1]))
            cc1 = b.op("sync", lambda E, e=e: E.dma_start(
                out=bass.AP(wbc, 0, [[2048, 128], [256, 8], [1, 256]]),
                in_=bass.AP(w1_h.tensor, e * 262144,
                            [[256, 128], [32768, 8], [1, 256]])), "wc", 16)
            b.wait("tensor", "wc", cc1)
            if e == 0:
                b.waitg("tensor", p4g[1])
            cbv = 0
            for jc in range(2):
                for ic in range(8):
                    cbv = b.op("tensor", lambda E, jc=jc, ic=ic, e=e, bvb=bvb:
                               E.matmul(
                                   bvb[:, jc:jc + 1],
                                   wbc[:, ic * 256 + jc * 128:
                                       ic * 256 + (jc + 1) * 128],
                                   lnbp[:, ic * 8 + e: ic * 8 + e + 1],
                                   start=(ic == 0), stop=(ic == 7)), "pe")
            b.wait("vector", "pe", cbv)
            for jc in range(2):
                dv_bg_e[e] = b.op("vector", lambda E, jc=jc, e=e, bvb=bvb:
                                  E.tensor_tensor(
                                      bgel[:, jc * 8 + e: jc * 8 + e + 1],
                                      b1p[:, jc * 8 + e: jc * 8 + e + 1],
                                      bvb[:, jc:jc + 1], op=ALU.add), "dv")
            b.wait("scalar", "wc", cc1)
            b.wait("scalar", "pe", cbv)
            g2q = b.cnt["lq"]
            if e > 0:
                b.wait("scalar", "lq", g2q)
                b.wait("vector", "lq", g2q)
            last1 = 0
            for ic in range(8):
                scol = lnsp[:, ic * 8 + e: ic * 8 + e + 1]
                hi = g2w[:, ic * 256:(ic + 1) * 256]
                lo = g2w[:, 2048 + ic * 256: 2048 + (ic + 1) * 256]
                cac = b.op("scalar", lambda E, hi=hi, ic=ic, scol=scol:
                           E.activation(hi, wbc[:, ic * 256:(ic + 1) * 256],
                                        AFT.Copy, scale=scol), "ac")
                b.wait("vector", "ac", cac)
                last1 = b.op("vector", lambda E, hi=hi, lo=lo, ic=ic,
                             scol=scol: E.scalar_tensor_tensor(
                                 lo, wbc[:, ic * 256:(ic + 1) * 256], scol,
                                 hi, op0=ALU.mult, op1=ALU.subtract), "dv")
            b.wait("sync", "dv", last1)
            cc2 = 0
            for jc in range(2):
                cc2 = b.op("sync", lambda E, e=e, jc=jc: E.dma_start(
                    out=wbc[:, jc * 1024:(jc + 1) * 1024],
                    in_=w2_h[e, jc * 128:(jc + 1) * 128, :]), "wd", 16)
            b.wait("scalar", "wd", cc2)
            last2 = 0
            for jc in range(2):
                hi = g2w[:, 4096 + jc * 1024: 4096 + (jc + 1) * 1024]
                lo = g2w[:, 6144 + jc * 1024: 6144 + (jc + 1) * 1024]
                cac = b.op("scalar", lambda E, hi=hi, jc=jc: E.copy(
                    out=hi, in_=wbc[:, jc * 1024:(jc + 1) * 1024]), "ac")
                b.wait("vector", "ac", cac)
                last2 = b.op("vector", lambda E, hi=hi, lo=lo, jc=jc:
                             E.tensor_tensor(
                                 lo, wbc[:, jc * 1024:(jc + 1) * 1024],
                                 hi, op=ALU.subtract), "dv")
            wbc_free[0] = last2
            wbc_free[1] = last2
            b.wait("scalar", "dv", last2)
            wst12[e] = b.op("scalar", lambda E, e=e: E.dma_start(
                out=w12b_h[e], in_=g2[:, :]), "lq", 16)

        for e in range(4):
            split_w12(e, p7)
        wosb = wos[:, :].bitcast(BF16)
        wo_pieces = []
        for hc in range(8):
            wo_pieces.append((
                [(wo_h[hc * 128:(hc + 1) * 128, :], 0, 1024)],
                wosb[:, hc * 1024:(hc + 1) * 1024],
                wosb[:, 8192 + hc * 1024: 8192 + (hc + 1) * 1024],
                scp[:, hc:hc + 1]))
        dv_wos = land_split(wo_pieces, None)[1]

        b.wait("gpsimd", "cc", ccv)
        dv_pref = max(dv_mz, dv_msk_f)
        b.wait("gpsimd", "pe", max(pe_oint_f, pe_smm_f))
        pbuf = [smb, ktm]
        dv_slot = [dv_pref, dv_pref]
        for r in range(4):
            buf = pbuf[r % 2]
            psem = "sp" if r % 2 == 0 else "sq"
            b.wait("gpsimd", "dv", dv_slot[r % 2])
            csp = b.op("gpsimd", lambda E, r=r, buf=buf: E.dma_start(
                out=buf[:, :], in_=ago_h[r * 128:(r + 1) * 128, :]), psem, 16)
            b.wait("vector", psem, csp)
            b.wait("vector", "dv", dv_pref)
            dv_pref = b.op("vector", lambda E, r=r, buf=buf: E.scalar_tensor_tensor(
                mst[:, :], buf[:, :], pmb[:, r:r + 1], mst[:, :],
                op0=ALU.mult, op1=ALU.add), "dv")
            dv_slot[r % 2] = dv_pref
        ld_w2 = {}

        def load_w12(e, tag, first=False):
            sl = wload_seq[0] % 2
            wload_seq[0] += 1
            b.waitg("sync", wbuf_free[sl])
            b.wait("sync", "lq", b.cnt["lq"])
            wsem = "wa" if sl == 0 else "wb"
            cc = b.op("sync", lambda E, e=e, sl=sl: E.dma_start(
                out=wslot[sl][:, :], in_=w12b_h[e]), wsem, 16)
            ld_w2[tag] = (wsem, cc)
            return sl

        pe_prev_s_end = 0
        dv_g1_done = 0
        st_out = 0
        dv_fin_bank = {0: dv_b1_end, 1: dv_b1_end,
                       2: max(dv_b1_end, st_cpy_f), 3: max(dv_b1_end, st_cpy_f)}
        pe_hbf = [0]
        dv_cA_last = 0
        st_m1 = 0
        pe_at = pe_smm_f
        pe_m1_reader = pe_b1_end
        ac_sq = 0
        pe_g4_free = 0
        pe_g6_free = 0
        st_out_prev = 0

        for s in range(NS):
            st_out_prev = st_out
            if s == 0:
                ldo, ldxh, ldx = ldo0, ldxh0, ldx0
            else:
                b.wait("gpsimd", "ac", ac_sq)
                b.wait("gpsimd", "dv", dv_cA_last)
                b.op("gpsimd", lambda E, s=s: E.dma_start(
                    out=g3[:, :], in_=osp_h[s, :, :]), "lr", 16)
                b.wait("gpsimd", "dv", dv_cA_last)
                b.op("gpsimd", lambda E, s=s: E.dma_start(
                    out=g5[:, :], in_=usp_h[s, :, :]), "lr", 16)
                b.wait("gpsimd", "pe", pe_g6_free)
                ldo = b.op("gpsimd", lambda E, s=s: E.dma_start(
                    out=g6[:, :], in_=qsp_h[s, :, :]), "lr", 16)
                b.wait("gpsimd", "pe", pe_g4_free)
                ldxh = b.op("gpsimd", lambda E, s=s: E.dma_start(
                    out=g4[:, :], in_=xhsp_h[s, :, :]), "lh", 16)
                ldx = []
                b.wait("gpsimd", "dv", dv_g1_done)
                for i in range(4):
                    c = b.op("gpsimd", lambda E, s=s, i=i: E.dma_start(
                        out=g1[:, i * 1024:(i + 1) * 1024],
                        in_=x_h[s * 512 + i * 128: s * 512 + (i + 1) * 128, :]),
                        "lx", 16)
                    ldx.append(c)

            ctp_prev = 0
            dv_mfs = 0
            for cl in range(4):
                gc = 4 * s + cl
                b.wait("vector", "pe", ctp_prev)
                crm = b.op("vector", lambda E, gc=gc: E.reduce_max(
                    cols[:, 24:25], logit[:, gc * 8:(gc + 1) * 8],
                    mybir.AxisListType.X), "dv")
                b.wait("vector", "dv", crm)
                cdv = b.op("vector", lambda E, gc=gc: E.tensor_scalar(
                    mrow[:, :], logit[:, gc * 8:(gc + 1) * 8], cols[:, 24:25],
                    None, op0=ALU.is_equal), "dv")
                sl = p4n[0] % 2
                p4n[0] += 1
                tb = tslot[sl]
                b.waitg("tensor", p4g[sl])
                b.wait("tensor", "dv", cdv)
                ctp_prev = b.op("tensor", lambda E, cl=cl, tb=tb: E.transpose(
                    tb[0:8, :], mrow[:, :], ident[:, :]), "pe")
                b.wait("vector", "pe", ctp_prev)
                b.wait("vector", "sm", st_m1)
                dv_mfs = b.op("vector", lambda E, cl=cl, tb=tb: E.tensor_copy(
                    out=mfs[0:8, cl * 128:(cl + 1) * 128],
                    in_=tb[0:8, :]), "dv")
                p4g[sl] = ("dv", dv_mfs)
            ac_af = 0
            dv_corr = 0
            cA = 0
            atsl = [tslot[0], tslot[1],
                    pbank[5][:, 0:128], pbank[6][:, 0:128]]
            atg = [[p4g[0]], [p4g[1]],
                   [("dv", bank_dv[5]), ("ac", bank_ac[5])],
                   [("dv", bank_dv[6]), ("ac", bank_ac[6])]]
            atrot = [0]
            for cl in range(4):
                b.wait("tensor", "lr", ldo)
                b.wait("tensor", "dv", dv_corr)
                if s == 0 and cl == 0:
                    b.wait("tensor", "ac", ac_b1_end)
                    b.wait("tensor", "dv", dv_pref)
                cpe = 0
                for h in range(8):
                    po = p0 if h < 4 else p1
                    off = (h % 4) * 128
                    cpe = b.op("tensor", lambda E, h=h, cl=cl, po=po, off=off:
                               E.matmul(
                                   po[:, off:off + 128],
                                   g6[:, h * 512 + cl * 128: h * 512 + (cl + 1) * 128],
                                   mst[:, h * 128:(h + 1) * 128],
                                   start=True, stop=True), "pe")
                b.wait("vector", "pe", cpe)
                b.op("vector", lambda E, cl=cl: E.tensor_tensor(
                    g3[:, cl * 1024: cl * 1024 + 512],
                    g3[:, cl * 1024: cl * 1024 + 512], p0[:, :], op=ALU.add), "dv")
                dv_corr = b.op("vector", lambda E, cl=cl: E.tensor_tensor(
                    g3[:, cl * 1024 + 512: (cl + 1) * 1024],
                    g3[:, cl * 1024 + 512: (cl + 1) * 1024], p1[:, :],
                    op=ALU.add), "dv")

                b.wait("scalar", "dv", dv_corr)
                if s == 0 and cl == 0:
                    b.wait("scalar", "dv", dv_pref)
                b.wait("scalar", "ac", ac_sq)
                ac_sq = b.op("scalar", lambda E, cl=cl: E.activation(
                    smb[:, :], g3[:, cl * 1024:(cl + 1) * 1024], AFT.Square), "ac")
                b.wait("vector", "ac", ac_sq)
                dsum = b.op("vector", lambda E: E.reduce_sum(
                    cols[:, 25:26], smb[:, :], mybir.AxisListType.X), "dv")
                b.wait("scalar", "dv", dsum)
                crms = b.op("scalar", lambda E: E.activation(
                    cols[:, 26:27], cols[:, 25:26], AFT.Sqrt,
                    scale=1.0 / 1024.0), "ac")
                b.wait("vector", "ac", crms)
                cea = b.op("vector", lambda E: E.tensor_scalar(
                    cols[:, 27:28], cols[:, 26:27], EPS_RMS, None,
                    op0=ALU.add), "dv")
                b.wait("vector", "dv", cea)
                crc = b.op("vector", lambda E: E.reciprocal(
                    cols[:, 28:29], cols[:, 27:28]), "dv")
                b.wait("vector", "dv", crc)
                if cl == 0:
                    b.wait("vector", "lr", ldo)
                    if s == 0:
                        b.wait("vector", "dv", dv_pref)
                b.wait("vector", "pe", pe_at)
                cA = b.op("vector", lambda E, cl=cl: E.scalar_tensor_tensor(
                    ktm[:, :], g3[:, cl * 1024:(cl + 1) * 1024], cols[:, 28:29],
                    g5[:, cl * 1024:(cl + 1) * 1024],
                    op0=ALU.mult, op1=ALU.mult), "dv")
                g6b = g6[:, :].bitcast(BF16)
                for hc in range(8):
                    sl = atrot[0] % 4
                    atrot[0] += 1
                    tb = atsl[sl]
                    for g in atg[sl]:
                        b.waitg("tensor", g)
                    if hc == 0:
                        b.wait("tensor", "dv", cA)
                    ctp = b.op("tensor", lambda E, hc=hc, tb=tb: E.transpose(
                        tb, ktm[:, hc * 128:(hc + 1) * 128], ident[:, :]), "pe")
                    pe_at = ctp
                    ahi = g6b[:, 2 * (hc * 512 + cl * 128):
                              2 * (hc * 512 + cl * 128) + 128]
                    alo = g6b[:, 2 * (hc * 512 + cl * 128) + 128:
                              2 * (hc * 512 + cl * 128) + 256]
                    b.wait("scalar", "pe", ctp)
                    caa = b.op("scalar", lambda E, ahi=ahi, tb=tb: E.copy(
                        out=ahi, in_=tb), "ac")
                    b.wait("vector", "ac", caa)
                    ac_af = b.op("vector", lambda E, ahi=ahi, alo=alo, tb=tb:
                                 E.tensor_tensor(alo, tb, ahi,
                                                 op=ALU.subtract), "dv")
                    atg[sl] = [("dv", ac_af)]
                if s == 0:
                    split_w12(4 + cl, p2)
            if s == 0:
                lq_w12 = b.cnt["lq"]
            p4g[0] = atg[0][0]
            p4g[1] = atg[1][0]
            bank_dv[5] = max(bank_dv[5], atg[2][0][1])
            bank_dv[6] = max(bank_dv[6], atg[3][0][1])
            dv_cA_last = cA

            for th in range(2):
                sl_cur = load_w12(0, (s, th, 0), s == 0 and th == 0)
                b.wait("scalar", "dv", dv_mfs)
                b.wait("scalar", "pe", pe_m1_reader)
                st_m1 = b.op("scalar", lambda E, th=th: E.dma_start(
                    out=m1r[0:1, :], in_=mfs[0:8, th * 256:(th + 1) * 256]),
                    "sm", 16)
                b.wait("vector", "sm", st_m1)
                dv_m1 = b.op("vector", lambda E: E.tensor_scalar(
                    m1r[0:1, :], m1r[0:1, :], -1.0, None, op0=ALU.add), "dv")
                grp_last = {}
                for tl in range(2):
                    for dh in range(2):
                        pb = tl * 2 + dh
                        ps = pbank[pb]
                        cl = th * 2 + tl
                        b.wait("tensor", "dv", max(dv_fin_bank[pb], dv_corr))
                        b.wait("tensor", "dv", dv_mfs)
                        if s == 0:
                            b.wait("tensor", "dv", dv_bg_e[NE - 1])
                        b.op("tensor", lambda E, ps=ps, cl=cl, dh=dh: E.matmul(
                            ps[:, :], mfs[0:8, cl * 128:(cl + 1) * 128],
                            b2s[0:8, dh * 512:(dh + 1) * 512],
                            start=True, stop=False), "pe")

                for e in range(NE):
                    sl_next = None
                    if e < NE - 1:
                        sl_next = load_w12(e + 1, (s, th, e + 1),
                                           s == 0 and th == 0)
                    wdst = wslot[sl_cur][:, :].bitcast(BF16)
                    g4b = g4[:, :].bitcast(BF16)
                    hbb = hbf[:, :].bitcast(BF16)

                    dv_hj = {}
                    for jc in range(2):
                        pb = 5 + jc
                        b.waitg("tensor", ld_w2[(s, th, e)])
                        b.wait("tensor", "dv", max(bank_dv[pb], dv_m1))
                        b.wait("tensor", "lh", ldxh)
                        b.wait("tensor", "ac", bank_ac[pb])
                        for ic in range(8):
                            whi = wdst[:, ic * 256 + jc * 128:
                                       ic * 256 + (jc + 1) * 128]
                            wlo = wdst[:, 2048 + ic * 256 + jc * 128:
                                       2048 + ic * 256 + (jc + 1) * 128]
                            xhi = g4b[:, ic * 512 + th * 256:
                                      ic * 512 + (th + 1) * 256]
                            xlo = g4b[:, 4096 + ic * 512 + th * 256:
                                      4096 + ic * 512 + (th + 1) * 256]
                            st0 = (ic == 0)
                            b.op("tensor", lambda E, whi=whi, xhi=xhi, jc=jc,
                                 st0=st0: E.matmul(
                                     pbank[5 + jc][:, 0:256], whi, xhi,
                                     start=st0, stop=False), "pe")
                            b.op("tensor", lambda E, whi=whi, xlo=xlo, jc=jc:
                                 E.matmul(pbank[5 + jc][:, 0:256], whi, xlo,
                                          start=False, stop=False), "pe")
                            b.op("tensor", lambda E, wlo=wlo, xhi=xhi, jc=jc:
                                 E.matmul(pbank[5 + jc][:, 0:256], wlo, xhi,
                                          start=False, stop=False), "pe")
                        cg = b.op("tensor", lambda E, jc=jc, th=th, e=e: E.matmul(
                            pbank[5 + jc][:, 0:256], bigr[0:1, 0:128],
                            m1r[0:1, e * 256:(e + 1) * 256],
                            start=False, stop=True), "pe")
                        b.wait("scalar", "pe", max(cg, pe_hbf[0]))
                        if s == 0:
                            b.wait("scalar", "dv", dv_bg_e[e])
                        cge = b.op("scalar", lambda E, jc=jc, e=e:
                                   E.activation(
                                       pbank[5 + jc][:, 256:512],
                                       pbank[5 + jc][:, 0:256], AFT.Gelu,
                                       bias=bgel[:, jc * 8 + e: jc * 8 + e + 1]),
                                   "ac")
                        hhi = hbb[:, jc * 256:(jc + 1) * 256]
                        hlo = hbb[:, 512 + jc * 256: 512 + (jc + 1) * 256]
                        b.wait("scalar", "ac", cge)
                        chc = b.op("scalar", lambda E, jc=jc, hhi=hhi: E.copy(
                            out=hhi, in_=pbank[5 + jc][:, 256:512]), "ac")
                        b.wait("vector", "ac", chc)
                        dv_hj[jc] = b.op("vector", lambda E, jc=jc, hhi=hhi,
                                         hlo=hlo: E.tensor_tensor(
                                             hlo, pbank[5 + jc][:, 256:512], hhi,
                                             op=ALU.subtract), "dv")
                        bank_ac[pb] = chc
                        bank_dv[pb] = dv_hj[jc]

                    cw2 = 0
                    for jc in range(2):
                        b.wait("tensor", "dv", dv_hj[jc])
                        for tl in range(2):
                            for dh in range(2):
                                pb = tl * 2 + dh
                                hhi = hbb[:, jc * 256 + tl * 128:
                                          jc * 256 + (tl + 1) * 128]
                                hlo = hbb[:, 512 + jc * 256 + tl * 128:
                                          512 + jc * 256 + (tl + 1) * 128]
                                whi = wdst[:, 4096 + jc * 1024 + dh * 512:
                                           4096 + jc * 1024 + (dh + 1) * 512]
                                wlo = wdst[:, 6144 + jc * 1024 + dh * 512:
                                           6144 + jc * 1024 + (dh + 1) * 512]
                                b.op("tensor", lambda E, hhi=hhi, whi=whi, pb=pb:
                                     E.matmul(pbank[pb][:, :], hhi, whi,
                                              start=False, stop=False), "pe")
                                b.op("tensor", lambda E, hhi=hhi, wlo=wlo, pb=pb:
                                     E.matmul(pbank[pb][:, :], hhi, wlo,
                                              start=False, stop=False), "pe")
                                cw2 = b.op("tensor", lambda E, hlo=hlo, whi=whi,
                                           pb=pb: E.matmul(
                                               pbank[pb][:, :], hlo, whi,
                                               start=False, stop=False), "pe")
                    pe_hbf[0] = cw2
                    wbuf_free[sl_cur] = ("pe", cw2)
                    if sl_next is not None:
                        sl_cur = sl_next
                pe_m1_reader = b.cnt["pe"]
                if th == 1:
                    pe_g4_free = b.cnt["pe"]
                b.wait("tensor", "dv", max(ac_af, dv_wos))
                g6b = g6[:, :].bitcast(BF16)
                for tl in range(2):
                    for dh in range(2):
                        pb = tl * 2 + dh
                        cl = th * 2 + tl
                        for hc in range(8):
                            ahi = g6b[:, 2 * (hc * 512 + cl * 128):
                                      2 * (hc * 512 + cl * 128) + 128]
                            alo = g6b[:, 2 * (hc * 512 + cl * 128) + 128:
                                      2 * (hc * 512 + cl * 128) + 256]
                            whi = wosb[:, hc * 1024 + dh * 512:
                                       hc * 1024 + (dh + 1) * 512]
                            wlo = wosb[:, 8192 + hc * 1024 + dh * 512:
                                       8192 + hc * 1024 + (dh + 1) * 512]
                            b.op("tensor", lambda E, pb=pb, ahi=ahi, whi=whi:
                                 E.matmul(pbank[pb][:, :], ahi, whi,
                                          start=False, stop=False), "pe")
                            b.op("tensor", lambda E, pb=pb, ahi=ahi, wlo=wlo:
                                 E.matmul(pbank[pb][:, :], ahi, wlo,
                                          start=False, stop=False), "pe")
                            cwo = b.op("tensor", lambda E, pb=pb, alo=alo,
                                       whi=whi, hc=hc: E.matmul(
                                           pbank[pb][:, :], alo, whi,
                                           start=False, stop=(hc == 7)), "pe")
                        grp_last[pb] = cwo
                if th == 1:
                    pe_g6_free = b.cnt["pe"]

                for tl in range(2):
                    cl = th * 2 + tl
                    dv_last = 0
                    for dh in range(2):
                        pb = tl * 2 + dh
                        b.wait("vector", "pe", grp_last[pb])
                        b.wait("vector", "lx", ldx[3])
                        if th == 0 and tl == 0 and dh == 0:
                            b.wait("vector", "st", st_out_prev)
                            if s == 0:
                                b.wait("vector", "ac", ac_b1_end)
                                b.wait("vector", "sx", sx_spill_total)
                                b.wait("vector", "lq", lq_w12)
                        dvf = b.op("vector", lambda E, cl=cl, dh=dh, pb=pb:
                                   E.tensor_tensor(
                                       g2[:, cl * 1024 + dh * 512:
                                          cl * 1024 + (dh + 1) * 512],
                                       g1[:, cl * 1024 + dh * 512:
                                          cl * 1024 + (dh + 1) * 512],
                                       pbank[pb][:, :], op=ALU.add), "dv")
                        dv_fin_bank[pb] = dvf
                        dv_last = dvf
                    dv_g1_done = dv_last
                    b.wait("scalar", "dv", dv_last)
                    st_out = b.op("scalar", lambda E, s=s, cl=cl: E.dma_start(
                        out=out_h[s * 512 + cl * 128: s * 512 + (cl + 1) * 128, :],
                        in_=g2[:, cl * 1024:(cl + 1) * 1024]), "st", 16)

            pe_prev_s_end = b.cnt["pe"]

        b.wait("scalar", "st", b.cnt["st"])
        b.flush()

    return nc


_NC_CACHE = [None]


def kernel(**inputs):
    x = np.ascontiguousarray(np.asarray(inputs["L0_attn_x"]), dtype=np.float32)

    def f32c(a):
        return np.ascontiguousarray(np.asarray(a), dtype=np.float32)

    shared = {
        "wq": f32c(np.asarray(inputs["L0_attn_Wq"]).T),
        "wk": f32c(np.asarray(inputs["L0_attn_Wk"]).T),
        "wv": f32c(np.asarray(inputs["L0_attn_Wv"]).T),
        "wu": f32c(np.asarray(inputs["L0_attn_Wu"]).T),
        "wo": f32c(np.asarray(inputs["L0_attn_Wo"]).T),
        "wg": f32c(np.asarray(inputs["L0_ffn_Wg"]).T),
        "bqp": f32c(np.asarray(inputs["L0_attn_bq"]).reshape(8, 128).T),
        "bkp": f32c(np.asarray(inputs["L0_attn_bk"]).reshape(8, 128).T),
        "bvr": f32c(np.asarray(inputs["L0_attn_bv"]).reshape(1, 1024)),
        "bur": f32c(np.asarray(inputs["L0_attn_bu"]).reshape(1, 1024)),
        "bor": f32c(np.asarray(inputs["L0_attn_bo"]).reshape(1, 1024)),
        "scp": f32c(np.asarray(inputs["L0_attn_scale"]).reshape(8, 128).T),
        "lns": f32c(np.asarray(inputs["L0_ffn_ln_scale"]).T),
        "lnb": f32c(np.asarray(inputs["L0_ffn_ln_bias"]).T),
        "w1": f32c(inputs["L0_ffn_W1"]),
        "b1t": f32c(np.asarray(inputs["L0_ffn_b1"]).T),
        "w2": f32c(inputs["L0_ffn_W2"]),
        "b2": f32c(inputs["L0_ffn_b2"]),
        "ident": np.eye(128, dtype=np.float32),
        "cmask": np.triu(np.ones((128, 128), dtype=np.float32)),
    }

    in_maps = []
    for c in range(NCORES):
        bb, sl = c // 4, c % 4
        m = dict(shared)
        m["x"] = np.ascontiguousarray(x[bb, sl * T:(sl + 1) * T, :])
        m["pmask"] = np.array([[1.0 if r < sl else 0.0
                                for r in range(NCORES)]], dtype=np.float32)
        in_maps.append(m)

    if _NC_CACHE[0] is None:
        _NC_CACHE[0] = build_nc()
    nc = _NC_CACHE[0]

    res = run_bass_kernel_spmd(nc, in_maps, core_ids=list(range(NCORES)))
    out = np.empty((B, N, D), dtype=np.float32)
    for c in range(NCORES):
        bb, sl = c // 4, c % 4
        out[bb, sl * T:(sl + 1) * T, :] = res.results[c]["out"]
    return out


if __name__ == "__main__":
    build_nc()
    print("build ok")


# version 24
# speedup vs baseline: 1.0224x; 1.0224x over previous
import numpy as np
from contextlib import ExitStack

import concourse.bass as bass
import concourse.mybir as mybir
from concourse.bass_utils import run_bass_kernel_spmd

F32 = mybir.dt.float32
BF16 = mybir.dt.bfloat16
AFT = mybir.ActivationFunctionType
ALU = mybir.AluOpType

NCORES = 8
B, N, D = 2, 8192, 1024
T = 2048
S = 512
NS = T // S
C = 128
H = 8
NE = 8
HF = 256
EPS_LN = 1e-5
EPS_RMS = 1e-8
BIG = 1.0e9

ENGINES = ("sync", "tensor", "vector", "scalar", "gpsimd")


class Builder:
    def __init__(self, nc, sems):
        self.nc = nc
        self.sems = sems
        self.cnt = {k: 0 for k in sems}
        self.q = {e: [] for e in ENGINES}

    def wait(self, eng, sem, val):
        if not val or val <= 0:
            return
        h = self.sems[sem]
        self.q[eng].append(lambda E, h=h, v=val: E.wait_ge(h, v))

    def waitg(self, eng, guard):
        if guard:
            self.wait(eng, guard[0], guard[1])

    def op(self, eng, build, sem=None, by=1):
        tgt = None
        if sem is not None:
            self.cnt[sem] += by
            tgt = self.cnt[sem]
            h = self.sems[sem]

            def f(E, build=build, h=h, by=by):
                build(E).then_inc(h, by)
        else:
            def f(E, build=build):
                build(E)
        self.q[eng].append(f)
        return tgt

    def flush(self):
        nc = self.nc
        active = [e for e in ENGINES if self.q[e]]
        with nc.Block() as block:
            for e in active:
                lst = self.q[e]

                def body(E, lst=lst):
                    for f in lst:
                        f(E)

                getattr(block, e)(body)
        self.q = {e: [] for e in ENGINES}


def build_nc():
    nc = bass.Bass("TRN2", target_bir_lowering=False)

    dt = nc.dram_tensor
    x_h = dt("x", [T, D], F32, kind="ExternalInput").ap()
    wq_h = dt("wq", [D, D], F32, kind="ExternalInput").ap()
    wk_h = dt("wk", [D, D], F32, kind="ExternalInput").ap()
    wv_h = dt("wv", [D, D], F32, kind="ExternalInput").ap()
    wu_h = dt("wu", [D, D], F32, kind="ExternalInput").ap()
    wo_h = dt("wo", [D, D], F32, kind="ExternalInput").ap()
    wg_h = dt("wg", [D, NE], F32, kind="ExternalInput").ap()
    bqP_h = dt("bqp", [128, 8], F32, kind="ExternalInput").ap()
    bkP_h = dt("bkp", [128, 8], F32, kind="ExternalInput").ap()
    bv_h = dt("bvr", [1, D], F32, kind="ExternalInput").ap()
    bu_h = dt("bur", [1, D], F32, kind="ExternalInput").ap()
    bo_h = dt("bor", [1, D], F32, kind="ExternalInput").ap()
    scP_h = dt("scp", [128, 8], F32, kind="ExternalInput").ap()
    lns_h = dt("lns", [D, NE], F32, kind="ExternalInput").ap()
    lnb_h = dt("lnb", [D, NE], F32, kind="ExternalInput").ap()
    w1_h = dt("w1", [NE, D, HF], F32, kind="ExternalInput").ap()
    b1T_h = dt("b1t", [HF, NE], F32, kind="ExternalInput").ap()
    w2_h = dt("w2", [NE, HF, D], F32, kind="ExternalInput").ap()
    b2_h = dt("b2", [NE, D], F32, kind="ExternalInput").ap()
    id_h = dt("ident", [128, 128], F32, kind="ExternalInput").ap()
    cm_h = dt("cmask", [128, 128], F32, kind="ExternalInput").ap()
    pm_h = dt("pmask", [1, NCORES], F32, kind="ExternalInput").ap()
    out_h = dt("out", [T, D], F32, kind="ExternalOutput").ap()

    wqb_h = dt("wqbf", [2, 128, 4096], F32).ap()
    wkb_h = dt("wkbf", [2, 128, 4096], F32).ap()
    wvb_h = dt("wvbf", [2, 128, 4096], F32).ap()
    wub_h = dt("wubf", [2, 128, 4096], F32).ap()
    w12b_h = dt("w12bf", [NE, 128, 4096], F32).ap()
    qsp_h = dt("qspill", [NS, 128, H * S], F32).ap()
    osp_h = dt("ospill", [NS, 128, 4 * D], F32).ap()
    usp_h = dt("uspill", [NS, 128, 4 * D], F32).ap()
    xhsp_h = dt("xhspill", [NS, 128, 4096], F32).ap()
    agi_h = dt("agin", [128, H * 128], F32).ap()
    ago_h = dt("agout", [4 * 128, H * 128], F32).ap()

    es = ExitStack()
    with es:
        g1 = es.enter_context(nc.sbuf_tensor("g1", [128, 4096], F32))
        g2 = es.enter_context(nc.sbuf_tensor("g2", [128, 4096], F32))
        g3 = es.enter_context(nc.sbuf_tensor("g3", [128, 4096], F32))
        g4 = es.enter_context(nc.sbuf_tensor("g4", [128, 4096], F32))
        g5 = es.enter_context(nc.sbuf_tensor("g5", [128, 4096], F32))
        g6 = es.enter_context(nc.sbuf_tensor("g6", [128, 4096], F32))
        wba = es.enter_context(nc.sbuf_tensor("wba", [128, 4096], F32))
        wbb = es.enter_context(nc.sbuf_tensor("wbb", [128, 4096], F32))
        wos = es.enter_context(nc.sbuf_tensor("wos", [128, 8192], F32))
        ktm = es.enter_context(nc.sbuf_tensor("ktm", [128, 1024], F32))
        smb = es.enter_context(nc.sbuf_tensor("smb", [128, 1024], F32))
        mst = es.enter_context(nc.sbuf_tensor("mst", [128, 1024], F32))
        tmpa = es.enter_context(nc.sbuf_tensor("tmpa", [128, 512], F32))
        hbf = es.enter_context(nc.sbuf_tensor("hbf", [128, 512], F32))
        wbc = es.enter_context(nc.sbuf_tensor("wbc", [128, 2048], F32))
        wgb = es.enter_context(nc.sbuf_tensor("wgb", [128, 128], BF16))
        logit = es.enter_context(nc.sbuf_tensor("logit", [128, 128], F32))
        ident = es.enter_context(nc.sbuf_tensor("ident_s", [128, 128], F32))
        cmask = es.enter_context(nc.sbuf_tensor("cmask_s", [128, 128], F32))
        mfs = es.enter_context(nc.sbuf_tensor("mfs", [8, 512], F32))
        m1r = es.enter_context(nc.sbuf_tensor("m1r", [1, 2048], F32))
        b2s = es.enter_context(nc.sbuf_tensor("b2s", [8, 1024], F32))
        stat = es.enter_context(nc.sbuf_tensor("stat", [128, 96], F32))
        mvs = es.enter_context(nc.sbuf_tensor("mvs", [128, 16], F32))
        cols = es.enter_context(nc.sbuf_tensor("cols", [128, 32], F32))
        bqp = es.enter_context(nc.sbuf_tensor("bqp_s", [128, 8], F32))
        bkp = es.enter_context(nc.sbuf_tensor("bkp_s", [128, 8], F32))
        scp = es.enter_context(nc.sbuf_tensor("scp_s", [128, 8], F32))
        wgs = es.enter_context(nc.sbuf_tensor("wgs", [128, 64], F32))
        lnsp = es.enter_context(nc.sbuf_tensor("lnsp", [128, 64], F32))
        lnbp = es.enter_context(nc.sbuf_tensor("lnbp", [128, 64], F32))
        b1p = es.enter_context(nc.sbuf_tensor("b1p", [128, 16], F32))
        bgel = es.enter_context(nc.sbuf_tensor("bgel", [128, 16], F32))
        mrow = es.enter_context(nc.sbuf_tensor("mrow", [128, 8], F32))
        pmb = es.enter_context(nc.sbuf_tensor("pmb", [128, 8], F32))
        pmr = es.enter_context(nc.sbuf_tensor("pmr", [1, 8], F32))
        onesr = es.enter_context(nc.sbuf_tensor("onesr", [1, 128], F32))
        zrow = es.enter_context(nc.sbuf_tensor("zrow", [1, 128], F32))
        bigr = es.enter_context(nc.sbuf_tensor("bigr", [1, 128], F32))
        bor = es.enter_context(nc.sbuf_tensor("bor_s", [1, 1024], F32))
        p0 = es.enter_context(nc.psum_tensor("p0", [128, 512], F32))
        p1 = es.enter_context(nc.psum_tensor("p1", [128, 512], F32))
        p2 = es.enter_context(nc.psum_tensor("p2", [128, 512], F32))
        p3 = es.enter_context(nc.psum_tensor("p3", [128, 512], F32))
        p4 = es.enter_context(nc.psum_tensor("p4", [128, 512], F32))
        p5 = es.enter_context(nc.psum_tensor("p5", [128, 512], F32))
        p6 = es.enter_context(nc.psum_tensor("p6", [128, 512], F32))
        p7 = es.enter_context(nc.psum_tensor("p7", [128, 512], F32))
        s_lx = es.enter_context(nc.semaphore("s_lx"))
        s_wa = es.enter_context(nc.semaphore("s_wa"))
        s_wb = es.enter_context(nc.semaphore("s_wb"))
        s_lq = es.enter_context(nc.semaphore("s_lq"))
        s_sm = es.enter_context(nc.semaphore("s_sm"))
        s_sx = es.enter_context(nc.semaphore("s_sx"))
        s_lr = es.enter_context(nc.semaphore("s_lr"))
        s_lh = es.enter_context(nc.semaphore("s_lh"))
        s_sq = es.enter_context(nc.semaphore("s_sq"))
        s_wc = es.enter_context(nc.semaphore("s_wc"))
        s_wd = es.enter_context(nc.semaphore("s_wd"))
        s_st = es.enter_context(nc.semaphore("s_st"))
        s_pe = es.enter_context(nc.semaphore("s_pe"))
        s_dv = es.enter_context(nc.semaphore("s_dv"))
        s_ac = es.enter_context(nc.semaphore("s_ac"))
        s_sp = es.enter_context(nc.semaphore("s_sp"))
        s_cc = es.enter_context(nc.semaphore("s_cc"))
        sems = {"lx": s_lx, "wa": s_wa, "wb": s_wb, "lq": s_lq,
                "sm": s_sm, "sx": s_sx, "lr": s_lr, "lh": s_lh, "sq": s_sq, "wc": s_wc,
                "wd": s_wd, "st": s_st, "pe": s_pe,
                "dv": s_dv, "ac": s_ac, "sp": s_sp, "cc": s_cc}
        b = Builder(nc, sems)
        pbank = [p0, p1, p2, p3, p4, p5, p6, p7]
        tslot = [p4[:, 0:128], p7[:, 0:128]]

        p4g = [None, None]
        p4n = [0]

        for (dst, src) in [
            (ident[:, :], id_h[:, :]), (cmask[:, :], cm_h[:, :]),
            (bqp[:, :], bqP_h[:, :]), (bkp[:, :], bkP_h[:, :]),
            (scp[:, :], scP_h[:, :]), (b2s[:, :], b2_h[:, :]),
            (m1r[0:1, 0:1024], bv_h[:, :]), (m1r[0:1, 1024:2048], bu_h[:, :]),
            (bor[:, :], bo_h[:, :]), (pmr[:, :], pm_h[:, :]),
        ]:
            b.op("sync", lambda E, d=dst, s=src: E.dma_start(out=d, in_=s), "lq", 16)
        for ic in range(8):
            b.op("scalar", lambda E, ic=ic: E.dma_start(
                out=wgs[:, ic * 8:(ic + 1) * 8],
                in_=wg_h[ic * 128:(ic + 1) * 128, :]), "sm", 16)
            b.op("scalar", lambda E, ic=ic: E.dma_start(
                out=lnsp[:, ic * 8:(ic + 1) * 8],
                in_=lns_h[ic * 128:(ic + 1) * 128, :]), "sm", 16)
            b.op("scalar", lambda E, ic=ic: E.dma_start(
                out=lnbp[:, ic * 8:(ic + 1) * 8],
                in_=lnb_h[ic * 128:(ic + 1) * 128, :]), "sm", 16)
        for jc in range(2):
            b.op("scalar", lambda E, jc=jc: E.dma_start(
                out=b1p[:, jc * 8:(jc + 1) * 8],
                in_=b1T_h[jc * 128:(jc + 1) * 128, :]), "sm", 16)
        sm_consts = b.cnt["sm"]
        ld_consts = b.cnt["lq"]
        b.wait("sync", "lq", ld_consts)

        b.op("vector", lambda E: E.memset(mst[:, :], 0.0))
        b.op("vector", lambda E: E.memset(onesr[:, :], 1.0))
        b.op("vector", lambda E: E.memset(zrow[:, :], 0.0))
        dv_ones = b.op("vector", lambda E: E.memset(bigr[:, :], BIG), "dv")
        b.wait("tensor", "lq", ld_consts)
        b.wait("tensor", "dv", dv_ones)
        cbc = b.op("tensor", lambda E: E.matmul(
            p7[:, 0:8], onesr[0:1, 0:128], pmr[0:1, 0:8],
            start=True, stop=True), "pe")
        b.wait("vector", "pe", cbc)
        dv_pmb = b.op("vector", lambda E: E.tensor_copy(
            out=pmb[:, :], in_=p7[:, 0:8]), "dv")
        dv_b0 = dv_pmb
        for oh in range(2):
            b.wait("tensor", "dv", dv_b0)
            cb2 = b.op("tensor", lambda E, oh=oh: E.matmul(
                p7[0:8, :], onesr[0:1, 0:8], bor[0:1, oh * 512:(oh + 1) * 512],
                start=True, stop=True), "pe")
            b.wait("vector", "pe", cb2)
            dv_b0 = b.op("vector", lambda E, oh=oh: E.tensor_tensor(
                b2s[0:8, oh * 512:(oh + 1) * 512],
                b2s[0:8, oh * 512:(oh + 1) * 512],
                p7[0:8, :], op=ALU.add), "dv")
        b.flush()

        def ln_tile(i, src, dst, pre=()):
            st6 = stat[:, i * 12:(i + 1) * 12]
            for g in pre:
                b.waitg("vector", g)
            b.op("vector", lambda E, i=i, st6=st6, src=src: E.bn_stats(
                st6[:, 0:6], src[:, 0:512]))
            cst = b.op("vector", lambda E, i=i, st6=st6, src=src: E.bn_stats(
                st6[:, 6:12], src[:, 512:1024]), "dv")
            b.wait("vector", "dv", cst)
            cag = b.op("vector", lambda E, i=i, st6=st6: E.bn_aggr(
                mvs[:, 2 * i:2 * i + 2], st6[:, 0:12]), "dv")
            b.wait("vector", "dv", cag)
            c = b.op("vector", lambda E, i=i: E.tensor_scalar(
                cols[:, i:i + 1], mvs[:, 2 * i + 1:2 * i + 2], EPS_LN, None,
                op0=ALU.add), "dv")
            b.wait("scalar", "dv", c)
            c2 = b.op("scalar", lambda E, i=i: E.activation(
                cols[:, 8 + i:9 + i], cols[:, i:i + 1], AFT.Sqrt), "ac")
            b.wait("vector", "ac", c2)
            crc = b.op("vector", lambda E, i=i: E.reciprocal(
                cols[:, 16 + i:17 + i], cols[:, 8 + i:9 + i]), "dv")
            b.wait("vector", "dv", crc)
            c = b.op("vector", lambda E, i=i, src=src, dst=dst: E.tensor_scalar(
                dst[:, :], src[:, :], mvs[:, 2 * i:2 * i + 1],
                cols[:, 16 + i:17 + i], op0=ALU.subtract, op1=ALU.mult), "dv")
            return c

        def ln_pass(src, dst, pre_dv_guard=None):
            outs = []
            for i in range(4):
                pre = (pre_dv_guard,) if (i == 0 and pre_dv_guard) else ()
                outs.append(ln_tile(i, src(i), dst(i), pre))
            return outs

        def transpose_pass(src, dst, lnc, writer_pre, xslots=()):
            last = 0
            first = True
            dstb = dst[:, :].bitcast(BF16)
            slots = [tslot[0], tslot[1]] + [x[0] for x in xslots]
            gl = [p4g[0], p4g[1]] + [x[1] for x in xslots]
            nsl = len(slots)
            rot = 0
            for i in range(4):
                for ic in range(8):
                    pre = [("dv", lnc[i])] if ic == 0 else []
                    sl = rot % nsl
                    rot += 1
                    tb = slots[sl]
                    b.waitg("tensor", gl[sl])
                    for g in pre:
                        b.waitg("tensor", g)
                    cpe = b.op("tensor", lambda E, i=i, ic=ic, tb=tb: E.transpose(
                        tb, src(i)[:, ic * 128:(ic + 1) * 128], ident[:, :]), "pe")
                    if first:
                        for g in writer_pre:
                            b.waitg("scalar", g)
                            b.waitg("vector", g)
                        first = False
                    hi = dstb[:, ic * 512 + i * 128: ic * 512 + (i + 1) * 128]
                    lo = dstb[:, 4096 + ic * 512 + i * 128:
                              4096 + ic * 512 + (i + 1) * 128]
                    b.wait("scalar", "pe", cpe)
                    cac = b.op("scalar", lambda E, hi=hi, tb=tb: E.copy(
                        out=hi, in_=tb), "ac")
                    b.wait("vector", "ac", cac)
                    last = b.op("vector", lambda E, hi=hi, lo=lo, tb=tb:
                                E.tensor_tensor(lo, tb, hi, op=ALU.subtract), "dv")
                    gl[sl] = ("dv", last)
            p4g[0] = gl[0]
            p4g[1] = gl[1]
            return last

        WHALF = {"q": wq_h, "k": wk_h, "v": wv_h, "u": wu_h}
        wslot = [wba, wbb]

        st_qsp = [0] * NS
        st_osp = [0] * NS
        st_usp = [0] * NS
        st_xhsp = [0] * NS
        pe_xnf_reader = 0
        pe_xw_reader = 0
        wload_seq = [0]
        wbuf_free = [None, None]
        ld_w = {}

        wbc_free = [0, 0]
        wbc_par = [0]

        def land_split(pieces, dsts, scale_cols=None):
            last = 0
            for (dmas, hi_ap, lo_ap, scol) in pieces:
                hp = wbc_par[0] % 2
                wbc_par[0] += 1
                base = hp * 1024
                wsem = "wc" if hp == 0 else "wd"
                b.wait("sync", "dv", wbc_free[hp])
                cc = 0
                for ent in dmas:
                    if ent[0] == "ap3":
                        cc = b.op("sync", lambda E, soff=ent[1], sten=ent[2],
                                  base=base: E.dma_start(
                                      out=bass.AP(wbc, base,
                                                  [[2048, 128], [512, 2],
                                                   [1, 512]]),
                                      in_=bass.AP(sten, soff,
                                                  [[1024, 128], [131072, 2],
                                                   [1, 512]])), wsem, 16)
                    else:
                        (src_ap, c0, nc_) = ent
                        cc = b.op("sync", lambda E, src_ap=src_ap, c0=c0,
                                  nc_=nc_, base=base: E.dma_start(
                                      out=wbc[:, base + c0:base + c0 + nc_],
                                      in_=src_ap), wsem, 16)
                n = sum(1024 if x[0] == "ap3" else x[2] for x in dmas)
                b.wait("scalar", wsem, cc)
                if scol is not None:
                    cac = b.op("scalar", lambda E, hi_ap=hi_ap, n=n, scol=scol,
                               base=base:
                               E.activation(hi_ap, wbc[:, base:base + n],
                                            AFT.Copy, scale=scol), "ac")
                    b.wait("vector", "ac", cac)
                    last = b.op("vector", lambda E, hi_ap=hi_ap, lo_ap=lo_ap,
                                n=n, scol=scol, base=base:
                                E.scalar_tensor_tensor(
                                    lo_ap, wbc[:, base:base + n], scol, hi_ap,
                                    op0=ALU.mult, op1=ALU.subtract), "dv")
                else:
                    cac = b.op("scalar", lambda E, hi_ap=hi_ap, n=n, base=base:
                               E.copy(out=hi_ap, in_=wbc[:, base:base + n]),
                               "ac")
                    b.wait("vector", "ac", cac)
                    last = b.op("vector", lambda E, hi_ap=hi_ap, lo_ap=lo_ap,
                                n=n, base=base:
                                E.tensor_tensor(lo_ap, wbc[:, base:base + n],
                                                hi_ap, op=ALU.subtract), "dv")
                wbc_free[hp] = last
            return ("dv", last)

        wst = {}
        slot_store = [0, 0]
        WBF = {"q": wqb_h, "k": wkb_h, "v": wvb_h, "u": wub_h}

        def load_whalf(which, half, tag):
            sl = wload_seq[0] % 2
            wload_seq[0] += 1
            dst = wslot[sl][:, :].bitcast(BF16)
            if tag[0] == 0:
                b.waitg("scalar", wbuf_free[sl])
                b.waitg("vector", wbuf_free[sl])
                stq = b.cnt["lq"] if slot_store[sl] else 0
                b.wait("scalar", "lq", stq)
                b.wait("vector", "lq", stq)
                pieces = []
                for p in range(4):
                    dmas = [("ap3", (2 * p) * 128 * 1024 + half * 512,
                             WHALF[which].tensor)]
                    hi_ap = dst[:, p * 1024:(p + 1) * 1024]
                    lo_ap = dst[:, 4096 + p * 1024: 4096 + (p + 1) * 1024]
                    pieces.append((dmas, hi_ap, lo_ap, None))
                ld_w[tag] = land_split(pieces, None)
                b.wait("scalar", "dv", ld_w[tag][1])
                cst = b.op("scalar", lambda E, which=which, half=half, sl=sl:
                           E.dma_start(out=WBF[which][half],
                                       in_=wslot[sl][:, :]), "lq", 16)
                wst[(which, half)] = cst
                slot_store[sl] = cst
            else:
                b.waitg("sync", wbuf_free[sl])
                b.wait("sync", "lq", b.cnt["lq"])
                wsem = "wa" if sl == 0 else "wb"
                cc = b.op("sync", lambda E, which=which, half=half, sl=sl:
                          E.dma_start(out=wslot[sl][:, :],
                                      in_=WBF[which][half]), wsem, 16)
                ld_w[tag] = (wsem, cc)
            return sl

        pgrp = [0]
        bank_dv = {5: 0, 6: 0}
        bank_ac = {5: 0, 6: 0}
        exp_guard = [0]
        elu_prev = [0]

        def fm_half(dst, biasP, oh, sl, ldtag):
            last_dv = 0
            last_pe = 0
            if True:
                for ocl in range(4):
                    oc = oh * 4 + ocl
                    pb = 5 + (pgrp[0] % 2)
                    pgrp[0] += 1
                    ps = pbank[pb]
                    b.waitg("tensor", ld_w[ldtag])
                    b.wait("tensor", "dv", bank_dv[pb])
                    b.wait("tensor", "ac", bank_ac[pb])
                    cpe = 0
                    wb_ = wslot[sl][:, :].bitcast(BF16)
                    g2b = g2[:, :].bitcast(BF16)
                    for ic in range(8):
                        whi = wb_[:, ic * 512 + ocl * 128: ic * 512 + (ocl + 1) * 128]
                        wlo = wb_[:, 4096 + ic * 512 + ocl * 128:
                                  4096 + ic * 512 + (ocl + 1) * 128]
                        xhi = g2b[:, ic * 512:(ic + 1) * 512]
                        xlo = g2b[:, 4096 + ic * 512: 4096 + (ic + 1) * 512]
                        st0 = (ic == 0)
                        b.op("tensor", lambda E, whi=whi, xhi=xhi, ps=ps, st0=st0:
                             E.matmul(ps[:, :], whi, xhi,
                                      start=st0, stop=False), "pe")
                        b.op("tensor", lambda E, whi=whi, xlo=xlo, ps=ps:
                             E.matmul(ps[:, :], whi, xlo,
                                      start=False, stop=False), "pe")
                        cpe = b.op("tensor", lambda E, wlo=wlo, xhi=xhi, ps=ps,
                                   ic=ic: E.matmul(ps[:, :], wlo, xhi,
                                                   start=False, stop=(ic == 7)),
                                   "pe")
                    last_pe = cpe
                    ta = tmpa[:, 0:512]
                    dsl = dst[:, oc * 512:(oc + 1) * 512]
                    b.wait("vector", "pe", cpe)
                    b.wait("vector", "ac", exp_guard[0])
                    b.wait("vector", "dv", elu_prev[0])
                    b.op("vector", lambda E, ps=ps, oc=oc, ta=ta, biasP=biasP:
                         E.tensor_scalar(ta, ps[:, :], biasP[:, oc:oc + 1], 0.0,
                                         op0=ALU.add, op1=ALU.min), "dv")
                    cdv = b.op("vector", lambda E, ps=ps, oc=oc, dsl=dsl, biasP=biasP:
                               E.tensor_scalar(dsl, ps[:, :], biasP[:, oc:oc + 1], 0.0,
                                               op0=ALU.add, op1=ALU.max), "dv")
                    bank_dv[pb] = cdv
                    b.wait("scalar", "dv", cdv)
                    cac = b.op("scalar", lambda E, ta=ta:
                               E.activation(ta, ta, AFT.Exp), "ac")
                    exp_guard[0] = cac
                    b.wait("vector", "ac", cac)
                    last_dv = b.op("vector", lambda E, ta=ta, dsl=dsl:
                                   E.scalar_tensor_tensor(
                                       dsl, ta, -1.0, dsl,
                                       op0=ALU.add, op1=ALU.add), "dv")
                    elu_prev[0] = last_dv
            return last_dv, last_pe

        dv_msk = {}
        st_cpy = {}
        pe_oint = {}
        pe_smm = {}
        dv_logit = 0
        ac_oevac_prev = 0
        for stb in (p2, p3):
            b.op("tensor", lambda E, stb=stb: E.matmul(
                stb[:, :], zrow[0:1, 0:128], bor[0:1, 0:512],
                start=True, stop=True), "pe")

        c = 0
        for i in range(4):
            c = b.op("gpsimd", lambda E, i=i: E.dma_start(
                out=g1[:, i * 1024:(i + 1) * 1024],
                in_=x_h[i * 128:(i + 1) * 128, :]), "lx", 16)
        lnc_cur = [ln_tile(4 + i, g1[:, i * 1024:(i + 1) * 1024],
                           g1[:, i * 1024:(i + 1) * 1024],
                           pre=(("lx", c),)) for i in range(4)]

        slq0_next = load_whalf("q", 0, (0, "q", 0))
        slq1_next = load_whalf("q", 1, (0, "q", 1))
        b.wait("scalar", "sm", sm_consts)
        cwg = b.op("scalar", lambda E: E.copy(out=wgb[:, 0:64], in_=wgs[:, :]), "ac")
        b.wait("vector", "ac", cwg)
        dv_wg = b.op("vector", lambda E: E.tensor_tensor(
            wgb[:, 64:128], wgs[:, :], wgb[:, 0:64], op=ALU.subtract), "dv")
        for s in range(NS):
            slq0 = slq0_next
            slq1 = slq1_next

            og = ("ac", ac_oevac_prev) if s > 0 else None
            ac_xnf = transpose_pass(lambda i: g1[:, i * 1024:(i + 1) * 1024],
                                    g2, lnc_cur, [("pe", pe_xnf_reader),
                                                  ("sx", st_xhsp[s - 1] if s > 0 else 0)],
                                    xslots=[(p0[:, 0:128], og),
                                            (p1[:, 0:128], og)])
            pe_xw_reader = b.cnt["pe"]

            if s > 0:
                b.wait("vector", "st", st_osp[s - 1])
            b.wait("tensor", "dv", ac_xnf)

            def tm_half(oh, sl, ldtag, dst, brow_off):
                last_ac = 0
                cpe = 0
                wb_ = wslot[sl][:, :].bitcast(BF16)
                g2b = g2[:, :].bitcast(BF16)
                for sc in range(4):
                    pb = 5 + (pgrp[0] % 2)
                    pgrp[0] += 1
                    ps = pbank[pb]
                    b.waitg("tensor", ld_w[ldtag])
                    b.wait("tensor", "dv", bank_dv[pb])
                    b.wait("tensor", "ac", bank_ac[pb])
                    b.op("tensor", lambda E, ps=ps, oh=oh, brow_off=brow_off:
                         E.matmul(ps[:, :], onesr[0:1, 0:128],
                                  m1r[0:1, brow_off + oh * 512:
                                      brow_off + (oh + 1) * 512],
                                  start=True, stop=False), "pe")
                    for ic in range(8):
                        xhi = g2b[:, ic * 512 + sc * 128: ic * 512 + (sc + 1) * 128]
                        xlo = g2b[:, 4096 + ic * 512 + sc * 128:
                                  4096 + ic * 512 + (sc + 1) * 128]
                        whi = wb_[:, ic * 512:(ic + 1) * 512]
                        wlo = wb_[:, 4096 + ic * 512: 4096 + (ic + 1) * 512]
                        b.op("tensor", lambda E, xhi=xhi, whi=whi, ps=ps:
                             E.matmul(ps[:, :], xhi, whi,
                                      start=False, stop=False), "pe")
                        b.op("tensor", lambda E, xhi=xhi, wlo=wlo, ps=ps:
                             E.matmul(ps[:, :], xhi, wlo,
                                      start=False, stop=False), "pe")
                        cpe = b.op("tensor", lambda E, xlo=xlo, whi=whi, ps=ps,
                                   ic=ic: E.matmul(ps[:, :], xlo, whi,
                                                   start=False, stop=(ic == 7)),
                                   "pe")
                    dsl = dst[:, sc * 1024 + oh * 512: sc * 1024 + (oh + 1) * 512]
                    b.wait("scalar", "pe", cpe)
                    last_ac = b.op("scalar", lambda E, dsl=dsl, ps=ps:
                                   E.activation(dsl, ps[:, :], AFT.Silu), "ac")
                    bank_ac[pb] = last_ac
                    bank_dv[pb] = 0
                return cpe, last_ac

            def v_half(oh, sl, ldtag):
                return tm_half(oh, sl, ldtag, g5, 0)

            dv_q0, pe_q0 = fm_half(g3, bqp, 0, slq0, (s, "q", 0))
            wbuf_free[slq0] = ("pe", pe_q0)
            slk0 = load_whalf("k", 0, (s, "k", 0))
            dv_qf, pe_q1 = fm_half(g3, bqp, 1, slq1, (s, "q", 1))
            wbuf_free[slq1] = ("pe", pe_q1)
            slk1 = load_whalf("k", 1, (s, "k", 1))
            dv_k0, pe_k0 = fm_half(g4, bkp, 0, slk0, (s, "k", 0))
            wbuf_free[slk0] = ("pe", pe_k0)
            slv0 = load_whalf("v", 0, (s, "v", 0))
            dv_kf, pe_k1 = fm_half(g4, bkp, 1, slk1, (s, "k", 1))
            wbuf_free[slk1] = ("pe", pe_k1)
            slv1 = load_whalf("v", 1, (s, "v", 1))
            pe_v0, ac_v0 = v_half(0, slv0, (s, "v", 0))
            wbuf_free[slv0] = ("pe", pe_v0)
            slu0 = load_whalf("u", 0, (s, "u", 0))
            pe_v1, last_ac_v = v_half(1, slv1, (s, "v", 1))
            wbuf_free[slv1] = ("pe", pe_v1)
            slu1 = load_whalf("u", 1, (s, "u", 1))
            dv_qf = max(dv_qf, dv_q0)
            dv_kf = max(dv_kf, dv_k0)

            def u_half(oh, sl, ldtag, first):
                if first:
                    b.wait("scalar", "st", st_osp[s - 1] if s > 0 else 0)
                return tm_half(oh, sl, ldtag, g6, 1024)

            pe_u0, ac_u0 = u_half(0, slu0, (s, "u", 0), True)
            wbuf_free[slu0] = ("pe", pe_u0)
            pe_u1, ac_u1 = u_half(1, slu1, (s, "u", 1), False)
            wbuf_free[slu1] = ("pe", pe_u1)
            b.wait("scalar", "ac", ac_u1)
            st_usp[s] = b.op("scalar", lambda E, s=s: E.dma_start(
                out=usp_h[s, :, :], in_=g6[:, :]), "st", 16)

            for cl in range(4):
                gc = 4 * s + cl
                pb = 5 + (pgrp[0] % 2)
                pgrp[0] += 1
                ps = pbank[pb]
                if s == 0 and cl == 0:
                    b.wait("tensor", "dv", dv_wg)
                b.wait("tensor", "dv", bank_dv[pb])
                b.wait("tensor", "ac", bank_ac[pb])
                cpe = 0
                g2b = g2[:, :].bitcast(BF16)
                for ic in range(8):
                    xhi = g2b[:, ic * 512 + cl * 128: ic * 512 + (cl + 1) * 128]
                    xlo = g2b[:, 4096 + ic * 512 + cl * 128:
                              4096 + ic * 512 + (cl + 1) * 128]
                    whi = wgb[:, ic * 8:(ic + 1) * 8]
                    wlo = wgb[:, 64 + ic * 8: 64 + (ic + 1) * 8]
                    st0 = (ic == 0)
                    b.op("tensor", lambda E, xhi=xhi, whi=whi, ps=ps, st0=st0:
                         E.matmul(ps[:, 0:8], xhi, whi, start=st0,
                                  stop=False), "pe")
                    b.op("tensor", lambda E, xhi=xhi, wlo=wlo, ps=ps: E.matmul(
                        ps[:, 0:8], xhi, wlo, start=False, stop=False), "pe")
                    cpe = b.op("tensor", lambda E, xlo=xlo, whi=whi, ps=ps, ic=ic:
                               E.matmul(ps[:, 0:8], xlo, whi,
                                        start=False, stop=(ic == 7)), "pe")
                b.wait("vector", "pe", cpe)
                dv_logit = b.op("vector", lambda E, gc=gc, ps=ps: E.tensor_copy(
                    out=logit[:, gc * 8:(gc + 1) * 8], in_=ps[:, 0:8]), "dv")
                bank_dv[pb] = dv_logit
            pe_g2_free = b.cnt["pe"]

            lnc2 = [ln_tile(0, g1[:, 0:1024], g1[:, 0:1024],
                            pre=(("pe", pe_xw_reader),))]

            if s > 0:
                b.wait("scalar", "st", st_osp[s - 1])
            ac_oevac = ac_oevac_prev
            pairs = [(cl, h) for cl in range(4) for h in range(8)]

            def emit_TS(j):
                cl, h = pairs[j]
                gc = 4 * s + cl
                sbank = p5 if h % 2 == 0 else p6
                pre = [("dv", dv_kf)] if j == 0 else []
                sl = p4n[0] % 2
                p4n[0] += 1
                tb = tslot[sl]
                b.waitg("tensor", p4g[sl])
                for g in pre:
                    b.waitg("tensor", g)
                ct = b.op("tensor", lambda E, h=h, cl=cl, tb=tb: E.transpose(
                    tb, g4[:, h * 512 + cl * 128: h * 512 + (cl + 1) * 128],
                    ident[:, :]), "pe")
                b.wait("scalar", "pe", max(ct, pe_smm.get((gc - 1, h), 0)))
                cke = b.op("scalar", lambda E, h=h, tb=tb: E.copy(
                    out=ktm[:, h * 128:(h + 1) * 128], in_=tb), "ac")
                p4g[sl] = ("ac", cke)
                ktm_ev[(gc, h)] = cke
                if j >= 2:
                    pcl, ph = pairs[j - 2]
                    b.wait("tensor", "dv", dv_msk.get((4 * s + pcl, ph), 0))
                else:
                    b.wait("tensor", "dv", dv_msk.get((4 * s - 1, 6 + j), 0))
                    pb = 5 + (h % 2)
                    b.wait("tensor", "dv", bank_dv[pb])
                    b.wait("tensor", "ac", bank_ac[pb])
                if j == 0:
                    b.wait("tensor", "dv", max(dv_qf, dv_kf))
                    b.wait("tensor", "ac", last_ac_v)
                cs = b.op("tensor", lambda E, h=h, cl=cl, sbank=sbank: E.matmul(
                    sbank[:, 0:128],
                    g4[:, h * 512 + cl * 128: h * 512 + (cl + 1) * 128],
                    g3[:, h * 512 + cl * 128: h * 512 + (cl + 1) * 128],
                    start=True, stop=True), "pe")
                b.wait("vector", "pe", max(cs, pe_oint.get((gc - 1, h), 0)))
                cm_ = b.op("vector", lambda E, h=h, sbank=sbank: E.tensor_tensor(
                    smb[:, h * 128:(h + 1) * 128], cmask[:, :], sbank[:, 0:128],
                    op=ALU.mult), "dv")
                dv_msk[(gc, h)] = cm_

            ktm_ev = {}
            emit_TS(0)
            emit_TS(1)
            for j, (cl, h) in enumerate(pairs):
                gc = 4 * s + cl
                if j + 2 < len(pairs):
                    emit_TS(j + 2)
                po = p0 if h < 4 else p1
                off = (h % 4) * 128
                b.wait("tensor", "dv", dv_msk[(gc, h)])
                if h == 0:
                    b.wait("tensor", "ac", ac_oevac)
                b.op("tensor", lambda E, h=h, cl=cl, po=po, off=off: E.matmul(
                    po[:, off:off + 128], smb[:, h * 128:(h + 1) * 128],
                    g5[:, cl * 1024 + h * 128: cl * 1024 + (h + 1) * 128],
                    start=True, stop=False), "pe")
                b.wait("tensor", "dv", st_cpy.get(gc - 1, 0))
                coi = b.op("tensor", lambda E, h=h, cl=cl, po=po, off=off:
                           E.matmul(
                               po[:, off:off + 128],
                               g3[:, h * 512 + cl * 128: h * 512 + (cl + 1) * 128],
                               mst[:, h * 128:(h + 1) * 128],
                               start=False, stop=True), "pe")
                pe_oint[(gc, h)] = coi
                stb = p2 if h < 4 else p3
                b.wait("tensor", "ac", ktm_ev[(gc, h)])
                if h == 0:
                    b.wait("tensor", "dv", st_cpy.get(gc - 1, 0))
                csm = b.op("tensor", lambda E, h=h, cl=cl, stb=stb, off=off:
                           E.matmul(
                    stb[:, off:off + 128], ktm[:, h * 128:(h + 1) * 128],
                    g5[:, cl * 1024 + h * 128: cl * 1024 + (h + 1) * 128],
                    start=False, stop=False, skip_group_check=True), "pe")
                pe_smm[(gc, h)] = csm
                ic = h
                sl = p4n[0] % 2
                p4n[0] += 1
                tb = tslot[sl]
                b.waitg("tensor", p4g[sl])
                if ic == 0:
                    b.wait("tensor", "dv", lnc2[cl])
                ctp_xh_last = b.op("tensor", lambda E, ic=ic, cl=cl, tb=tb:
                                   E.transpose(
                                       tb, g1[:, cl * 1024 + ic * 128:
                                              cl * 1024 + (ic + 1) * 128],
                                       ident[:, :]), "pe")
                if ic == 0 and cl == 0:
                    b.wait("scalar", "pe", pe_g2_free)
                    b.wait("vector", "pe", pe_g2_free)
                g2b_ = g2[:, :].bitcast(BF16)
                xh_hi = g2b_[:, ic * 512 + cl * 128: ic * 512 + (cl + 1) * 128]
                xh_lo = g2b_[:, 4096 + ic * 512 + cl * 128:
                             4096 + ic * 512 + (cl + 1) * 128]
                b.wait("scalar", "pe", ctp_xh_last)
                cxa = b.op("scalar", lambda E, xh_hi=xh_hi, tb=tb:
                           E.copy(out=xh_hi, in_=tb), "ac")
                b.wait("vector", "ac", cxa)
                ac_ev_xh = b.op("vector", lambda E, xh_hi=xh_hi, xh_lo=xh_lo,
                                tb=tb: E.tensor_tensor(
                                    xh_lo, tb, xh_hi, op=ALU.subtract), "dv")
                p4g[sl] = ("dv", ac_ev_xh)

                if h == 7:
                    if cl == 0:
                        b.wait("scalar", "st", st_usp[s])
                    b.wait("vector", "pe", max(csm, pe_oint[(gc, 7)]))
                    b.op("vector", lambda E: E.tensor_copy(
                        out=mst[:, 0:512], in_=p2[:, :]), "dv")
                    st_cpy[gc] = b.op("vector", lambda E: E.tensor_copy(
                        out=mst[:, 512:1024], in_=p3[:, :]), "dv")
                    b.wait("scalar", "pe", pe_oint[(gc, 7)])
                    b.op("scalar", lambda E, cl=cl: E.copy(
                        out=g6[:, cl * 1024: cl * 1024 + 512], in_=p0[:, :]), "ac")
                    ac_oevac = b.op("scalar", lambda E, cl=cl: E.copy(
                        out=g6[:, cl * 1024 + 512: (cl + 1) * 1024],
                        in_=p1[:, :]), "ac")
                    b.wait("scalar", "dv", dv_qf)
                    st_qsp[s] = b.op("scalar", lambda E, s=s, cl=cl: E.dma_start(
                        out=bass.AP(qsp_h.tensor, s * 128 * 4096 + cl * 128,
                                    [[4096, 128], [512, 8], [1, 128]]),
                        in_=bass.AP(g3, cl * 128,
                                    [[4096, 128], [512, 8], [1, 128]])),
                        "st", 16)
                    b.wait("scalar", "ac", ac_oevac)
                    st_osp[s] = b.op("scalar", lambda E, s=s, cl=cl: E.dma_start(
                        out=osp_h[s, :, cl * 1024:(cl + 1) * 1024],
                        in_=g6[:, cl * 1024:(cl + 1) * 1024]), "st", 16)
                    if s < NS - 1 and cl == 1:
                        slq0_next = load_whalf("q", 0, (s + 1, "q", 0))
                    if s < NS - 1 and cl == 2:
                        slq1_next = load_whalf("q", 1, (s + 1, "q", 1))
                    if s < NS - 1:
                        b.wait("gpsimd", "pe", ctp_xh_last)
                        ldn = b.op("gpsimd", lambda E, s=s, cl=cl: E.dma_start(
                            out=g1[:, cl * 1024:(cl + 1) * 1024],
                            in_=x_h[(s + 1) * 512 + cl * 128:
                                    (s + 1) * 512 + (cl + 1) * 128, :]), "lx", 16)
                    b.wait("scalar", "dv", ac_ev_xh)
                    st_xhsp[s] = b.op("scalar", lambda E, s=s, cl=cl: E.dma_start(
                        out=bass.AP(xhsp_h.tensor, s * 128 * 4096 + cl * 64,
                                    [[4096, 128], [2048, 2], [256, 8],
                                     [1, 64]]),
                        in_=bass.AP(g2, cl * 64,
                                    [[4096, 128], [2048, 2], [256, 8],
                                     [1, 64]])),
                        "sx", 16)
                    if cl < 3:
                        lnc2.append(ln_tile(cl + 1,
                                            g1[:, (cl + 1) * 1024:(cl + 2) * 1024],
                                            g1[:, (cl + 1) * 1024:(cl + 2) * 1024]))
            ac_oevac_prev = ac_oevac

            if s < NS - 1:
                lnc_cur = [ln_tile(4 + i, g1[:, i * 1024:(i + 1) * 1024],
                                   g1[:, i * 1024:(i + 1) * 1024],
                                   pre=(("pe", ctp_xh_last), ("lx", ldn)))
                           for i in range(4)]
            bank_dv[5] = max(bank_dv[5], dv_msk[(4 * s + 3, 6)])
            bank_dv[6] = max(bank_dv[6], dv_msk[(4 * s + 3, 7)])

            pe_xnf_reader = b.cnt["pe"]

        st_spill_total = b.cnt["st"]
        sx_spill_total = b.cnt["sx"]
        pe_b1_end = b.cnt["pe"]
        ac_b1_end = b.cnt["ac"]
        dv_b1_end = b.cnt["dv"]
        pe_oint_f = pe_oint[(4 * NS - 1, 7)]
        pe_smm_f = pe_smm[(4 * NS - 1, 7)]
        dv_msk_f = dv_msk[(4 * NS - 1, 7)]
        st_cpy_f = st_cpy[4 * NS - 1]

        b.wait("gpsimd", "dv", st_cpy_f)
        b.wait("gpsimd", "pe", pe_oint_f)
        c0 = b.op("gpsimd", lambda E: E.dma_start(
            out=agi_h[:, :], in_=mst[:, :]), "sp", 16)
        b.wait("gpsimd", "sp", c0)
        ccv = b.op("gpsimd", lambda E: E.collective_compute(
            "AllGather", ALU.bypass,
            replica_groups=[[0, 1, 2, 3], [4, 5, 6, 7]],
            ins=[agi_h[:, :]], outs=[ago_h[:, :]]), "cc")
        b.wait("gpsimd", "st", st_spill_total)
        b.wait("gpsimd", "sx", sx_spill_total)
        b.wait("gpsimd", "pe", pe_b1_end)
        b.wait("gpsimd", "dv", dv_b1_end)
        b.wait("gpsimd", "ac", ac_b1_end)
        b.op("gpsimd", lambda E: E.dma_start(
            out=g3[:, :], in_=osp_h[0, :, :]), "lr", 16)
        b.op("gpsimd", lambda E: E.dma_start(
            out=g5[:, :], in_=usp_h[0, :, :]), "lr", 16)
        ldo0 = b.op("gpsimd", lambda E: E.dma_start(
            out=g6[:, :], in_=qsp_h[0, :, :]), "lr", 16)
        ldxh0 = b.op("gpsimd", lambda E: E.dma_start(
            out=g4[:, :], in_=xhsp_h[0, :, :]), "lh", 16)
        ldx0 = []
        for i in range(4):
            c = b.op("gpsimd", lambda E, i=i: E.dma_start(
                out=g1[:, i * 1024:(i + 1) * 1024],
                in_=x_h[i * 128:(i + 1) * 128, :]), "lx", 16)
            ldx0.append(c)
        b.wait("vector", "sp", c0)
        b.wait("vector", "pe", pe_oint_f)
        dv_mz = b.op("vector", lambda E: E.memset(mst[:, :], 0.0), "dv")

        wosb = wos[:, :].bitcast(BF16)
        wo_pieces = []
        for hc in range(8):
            wo_pieces.append((
                [(wo_h[hc * 128:(hc + 1) * 128, :], 0, 1024)],
                wosb[:, hc * 1024:(hc + 1) * 1024],
                wosb[:, 8192 + hc * 1024: 8192 + (hc + 1) * 1024],
                scp[:, hc:hc + 1]))
        b.wait("scalar", "sm", sm_consts)
        b.wait("vector", "sm", sm_consts)
        b.wait("scalar", "pe", pe_b1_end)
        b.wait("vector", "pe", pe_b1_end)
        dv_wos = land_split(wo_pieces, None)[1]

        b.wait("gpsimd", "cc", ccv)
        dv_pref = max(dv_mz, dv_msk_f)
        b.wait("gpsimd", "pe", max(pe_oint_f, pe_smm_f))
        pbuf = [smb, ktm]
        dv_slot = [dv_pref, dv_pref]
        for r in range(4):
            buf = pbuf[r % 2]
            psem = "sp" if r % 2 == 0 else "sq"
            b.wait("gpsimd", "dv", dv_slot[r % 2])
            csp = b.op("gpsimd", lambda E, r=r, buf=buf: E.dma_start(
                out=buf[:, :], in_=ago_h[r * 128:(r + 1) * 128, :]), psem, 16)
            b.wait("vector", psem, csp)
            b.wait("vector", "dv", dv_pref)
            dv_pref = b.op("vector", lambda E, r=r, buf=buf: E.scalar_tensor_tensor(
                mst[:, :], buf[:, :], pmb[:, r:r + 1], mst[:, :],
                op0=ALU.mult, op1=ALU.add), "dv")
            dv_slot[r % 2] = dv_pref
        ld_w2 = {}

        def load_w12(e, tag, first):
            sl = wload_seq[0] % 2
            wload_seq[0] += 1
            dst = wslot[sl][:, :].bitcast(BF16)
            if not first:
                b.waitg("sync", wbuf_free[sl])
                b.wait("sync", "lq", b.cnt["lq"])
                wsem = "wa" if sl == 0 else "wb"
                cc = b.op("sync", lambda E, e=e, sl=sl: E.dma_start(
                    out=wslot[sl][:, :], in_=w12b_h[e]), wsem, 16)
                ld_w2[tag] = (wsem, cc)
                return sl
            b.waitg("scalar", wbuf_free[sl])
            b.waitg("vector", wbuf_free[sl])
            stq = b.cnt["lq"] if slot_store[sl] else 0
            b.wait("scalar", "lq", stq)
            b.wait("vector", "lq", stq)
            b.wait("tensor", "sm", sm_consts)
            b.wait("scalar", "sm", sm_consts)
            b.wait("vector", "sm", sm_consts)
            if True:
                b.wait("sync", "dv", max(wbc_free[0], wbc_free[1]))
                cc = b.op("sync", lambda E, e=e: E.dma_start(
                    out=bass.AP(wbc, 0, [[2048, 128], [256, 8], [1, 256]]),
                    in_=bass.AP(w1_h.tensor, e * 262144,
                                [[256, 128], [32768, 8], [1, 256]])), "wc", 16)
                cbv = 0
                b.wait("tensor", "wc", cc)
                if e == 0:
                    b.waitg("tensor", p4g[1])
                for jc in range(2):
                    for ic in range(8):
                        cbv = b.op("tensor", lambda E, jc=jc, ic=ic, e=e:
                                   E.matmul(
                                       p7[:, jc:jc + 1],
                                       wbc[:, ic * 256 + jc * 128:
                                           ic * 256 + (jc + 1) * 128],
                                       lnbp[:, ic * 8 + e: ic * 8 + e + 1],
                                       start=(ic == 0), stop=(ic == 7)), "pe")
                b.wait("vector", "pe", cbv)
                for jc in range(2):
                    dv_bg_e[e] = b.op("vector", lambda E, jc=jc, e=e:
                                      E.tensor_tensor(
                                          bgel[:, jc * 8 + e: jc * 8 + e + 1],
                                          b1p[:, jc * 8 + e: jc * 8 + e + 1],
                                          p7[:, jc:jc + 1], op=ALU.add), "dv")
                b.wait("scalar", "wc", cc)
                b.wait("scalar", "pe", cbv)
                last = 0
                for ic in range(8):
                    scol = lnsp[:, ic * 8 + e: ic * 8 + e + 1]
                    hi = dst[:, ic * 256:(ic + 1) * 256]
                    lo = dst[:, 2048 + ic * 256: 2048 + (ic + 1) * 256]
                    cac = b.op("scalar", lambda E, hi=hi, ic=ic, scol=scol:
                               E.activation(hi, wbc[:, ic * 256:(ic + 1) * 256],
                                            AFT.Copy, scale=scol), "ac")
                    b.wait("vector", "ac", cac)
                    last = b.op("vector", lambda E, hi=hi, lo=lo, ic=ic,
                                scol=scol: E.scalar_tensor_tensor(
                                    lo, wbc[:, ic * 256:(ic + 1) * 256], scol,
                                    hi, op0=ALU.mult, op1=ALU.subtract), "dv")
                wbc_free[0] = last
                wbc_free[1] = last
            pieces = []
            for jc in range(2):
                dmas = [(w2_h[e, jc * 128:(jc + 1) * 128, :], 0, 1024)]
                hi = dst[:, 4096 + jc * 1024: 4096 + (jc + 1) * 1024]
                lo = dst[:, 6144 + jc * 1024: 6144 + (jc + 1) * 1024]
                pieces.append((dmas, hi, lo, None))
            ld_w2[tag] = land_split(pieces, None)
            b.wait("scalar", "dv", ld_w2[tag][1])
            cst = b.op("scalar", lambda E, e=e, sl=sl: E.dma_start(
                out=w12b_h[e], in_=wslot[sl][:, :]), "lq", 16)
            wst[("w12", e)] = cst
            slot_store[sl] = cst
            return sl

        pe_prev_s_end = 0
        dv_g1_done = 0
        st_out = 0
        dv_bg_e = {}
        dv_fin_bank = {0: dv_b1_end, 1: dv_b1_end,
                       2: max(dv_b1_end, st_cpy_f), 3: max(dv_b1_end, st_cpy_f)}
        pe_hbf = [0]
        dv_cA_last = 0
        st_m1 = 0
        pe_at = pe_smm_f
        pe_m1_reader = pe_b1_end
        ac_sq = 0
        pe_g4_free = 0
        pe_g6_free = 0
        st_out_prev = 0

        for s in range(NS):
            st_out_prev = st_out
            if s == 0:
                ldo, ldxh, ldx = ldo0, ldxh0, ldx0
            else:
                b.wait("gpsimd", "ac", ac_sq)
                b.wait("gpsimd", "dv", dv_cA_last)
                b.op("gpsimd", lambda E, s=s: E.dma_start(
                    out=g3[:, :], in_=osp_h[s, :, :]), "lr", 16)
                b.wait("gpsimd", "dv", dv_cA_last)
                b.op("gpsimd", lambda E, s=s: E.dma_start(
                    out=g5[:, :], in_=usp_h[s, :, :]), "lr", 16)
                b.wait("gpsimd", "pe", pe_g6_free)
                ldo = b.op("gpsimd", lambda E, s=s: E.dma_start(
                    out=g6[:, :], in_=qsp_h[s, :, :]), "lr", 16)
                b.wait("gpsimd", "pe", pe_g4_free)
                ldxh = b.op("gpsimd", lambda E, s=s: E.dma_start(
                    out=g4[:, :], in_=xhsp_h[s, :, :]), "lh", 16)
                ldx = []
                b.wait("gpsimd", "dv", dv_g1_done)
                for i in range(4):
                    c = b.op("gpsimd", lambda E, s=s, i=i: E.dma_start(
                        out=g1[:, i * 1024:(i + 1) * 1024],
                        in_=x_h[s * 512 + i * 128: s * 512 + (i + 1) * 128, :]),
                        "lx", 16)
                    ldx.append(c)

            ctp_prev = 0
            dv_mfs = 0
            for cl in range(4):
                gc = 4 * s + cl
                b.wait("vector", "pe", ctp_prev)
                crm = b.op("vector", lambda E, gc=gc: E.reduce_max(
                    cols[:, 24:25], logit[:, gc * 8:(gc + 1) * 8],
                    mybir.AxisListType.X), "dv")
                b.wait("vector", "dv", crm)
                cdv = b.op("vector", lambda E, gc=gc: E.tensor_scalar(
                    mrow[:, :], logit[:, gc * 8:(gc + 1) * 8], cols[:, 24:25],
                    None, op0=ALU.is_equal), "dv")
                sl = p4n[0] % 2
                p4n[0] += 1
                tb = tslot[sl]
                b.waitg("tensor", p4g[sl])
                b.wait("tensor", "dv", cdv)
                ctp_prev = b.op("tensor", lambda E, cl=cl, tb=tb: E.transpose(
                    tb[0:8, :], mrow[:, :], ident[:, :]), "pe")
                b.wait("vector", "pe", ctp_prev)
                b.wait("vector", "sm", st_m1)
                dv_mfs = b.op("vector", lambda E, cl=cl, tb=tb: E.tensor_copy(
                    out=mfs[0:8, cl * 128:(cl + 1) * 128],
                    in_=tb[0:8, :]), "dv")
                p4g[sl] = ("dv", dv_mfs)
            ac_af = 0
            dv_corr = 0
            cA = 0
            atsl = [tslot[0], tslot[1],
                    pbank[5][:, 0:128], pbank[6][:, 0:128]]
            atg = [[p4g[0]], [p4g[1]],
                   [("dv", bank_dv[5]), ("ac", bank_ac[5])],
                   [("dv", bank_dv[6]), ("ac", bank_ac[6])]]
            atrot = [0]
            for cl in range(4):
                b.wait("tensor", "lr", ldo)
                b.wait("tensor", "dv", dv_corr)
                if s == 0 and cl == 0:
                    b.wait("tensor", "ac", ac_b1_end)
                    b.wait("tensor", "dv", dv_pref)
                cpe = 0
                for h in range(8):
                    po = p0 if h < 4 else p1
                    off = (h % 4) * 128
                    cpe = b.op("tensor", lambda E, h=h, cl=cl, po=po, off=off:
                               E.matmul(
                                   po[:, off:off + 128],
                                   g6[:, h * 512 + cl * 128: h * 512 + (cl + 1) * 128],
                                   mst[:, h * 128:(h + 1) * 128],
                                   start=True, stop=True), "pe")
                b.wait("vector", "pe", cpe)
                b.op("vector", lambda E, cl=cl: E.tensor_tensor(
                    g3[:, cl * 1024: cl * 1024 + 512],
                    g3[:, cl * 1024: cl * 1024 + 512], p0[:, :], op=ALU.add), "dv")
                dv_corr = b.op("vector", lambda E, cl=cl: E.tensor_tensor(
                    g3[:, cl * 1024 + 512: (cl + 1) * 1024],
                    g3[:, cl * 1024 + 512: (cl + 1) * 1024], p1[:, :],
                    op=ALU.add), "dv")

                b.wait("scalar", "dv", dv_corr)
                if s == 0 and cl == 0:
                    b.wait("scalar", "dv", dv_pref)
                b.wait("scalar", "ac", ac_sq)
                ac_sq = b.op("scalar", lambda E, cl=cl: E.activation(
                    smb[:, :], g3[:, cl * 1024:(cl + 1) * 1024], AFT.Square), "ac")
                b.wait("vector", "ac", ac_sq)
                dsum = b.op("vector", lambda E: E.reduce_sum(
                    cols[:, 25:26], smb[:, :], mybir.AxisListType.X), "dv")
                b.wait("scalar", "dv", dsum)
                crms = b.op("scalar", lambda E: E.activation(
                    cols[:, 26:27], cols[:, 25:26], AFT.Sqrt,
                    scale=1.0 / 1024.0), "ac")
                b.wait("vector", "ac", crms)
                cea = b.op("vector", lambda E: E.tensor_scalar(
                    cols[:, 27:28], cols[:, 26:27], EPS_RMS, None,
                    op0=ALU.add), "dv")
                b.wait("vector", "dv", cea)
                crc = b.op("vector", lambda E: E.reciprocal(
                    cols[:, 28:29], cols[:, 27:28]), "dv")
                b.wait("vector", "dv", crc)
                if cl == 0:
                    b.wait("vector", "lr", ldo)
                    if s == 0:
                        b.wait("vector", "dv", dv_pref)
                b.wait("vector", "pe", pe_at)
                cA = b.op("vector", lambda E, cl=cl: E.scalar_tensor_tensor(
                    ktm[:, :], g3[:, cl * 1024:(cl + 1) * 1024], cols[:, 28:29],
                    g5[:, cl * 1024:(cl + 1) * 1024],
                    op0=ALU.mult, op1=ALU.mult), "dv")
                g6b = g6[:, :].bitcast(BF16)
                for hc in range(8):
                    sl = atrot[0] % 4
                    atrot[0] += 1
                    tb = atsl[sl]
                    for g in atg[sl]:
                        b.waitg("tensor", g)
                    if hc == 0:
                        b.wait("tensor", "dv", cA)
                    ctp = b.op("tensor", lambda E, hc=hc, tb=tb: E.transpose(
                        tb, ktm[:, hc * 128:(hc + 1) * 128], ident[:, :]), "pe")
                    pe_at = ctp
                    ahi = g6b[:, 2 * (hc * 512 + cl * 128):
                              2 * (hc * 512 + cl * 128) + 128]
                    alo = g6b[:, 2 * (hc * 512 + cl * 128) + 128:
                              2 * (hc * 512 + cl * 128) + 256]
                    b.wait("scalar", "pe", ctp)
                    caa = b.op("scalar", lambda E, ahi=ahi, tb=tb: E.copy(
                        out=ahi, in_=tb), "ac")
                    b.wait("vector", "ac", caa)
                    ac_af = b.op("vector", lambda E, ahi=ahi, alo=alo, tb=tb:
                                 E.tensor_tensor(alo, tb, ahi,
                                                 op=ALU.subtract), "dv")
                    atg[sl] = [("dv", ac_af)]
            p4g[0] = atg[0][0]
            p4g[1] = atg[1][0]
            bank_dv[5] = max(bank_dv[5], atg[2][0][1])
            bank_dv[6] = max(bank_dv[6], atg[3][0][1])
            dv_cA_last = cA

            for th in range(2):
                sl_cur = load_w12(0, (s, th, 0), s == 0 and th == 0)
                b.wait("scalar", "dv", dv_mfs)
                b.wait("scalar", "pe", pe_m1_reader)
                st_m1 = b.op("scalar", lambda E, th=th: E.dma_start(
                    out=m1r[0:1, :], in_=mfs[0:8, th * 256:(th + 1) * 256]),
                    "sm", 16)
                b.wait("vector", "sm", st_m1)
                dv_m1 = b.op("vector", lambda E: E.tensor_scalar(
                    m1r[0:1, :], m1r[0:1, :], -1.0, None, op0=ALU.add), "dv")
                grp_last = {}
                for tl in range(2):
                    for dh in range(2):
                        pb = tl * 2 + dh
                        ps = pbank[pb]
                        cl = th * 2 + tl
                        b.wait("tensor", "dv", max(dv_fin_bank[pb], dv_corr))
                        b.wait("tensor", "dv", dv_mfs)
                        b.op("tensor", lambda E, ps=ps, cl=cl, dh=dh: E.matmul(
                            ps[:, :], mfs[0:8, cl * 128:(cl + 1) * 128],
                            b2s[0:8, dh * 512:(dh + 1) * 512],
                            start=True, stop=False), "pe")

                for e in range(NE):
                    sl_next = None
                    if e < NE - 1:
                        sl_next = load_w12(e + 1, (s, th, e + 1),
                                           s == 0 and th == 0)
                    wdst = wslot[sl_cur][:, :].bitcast(BF16)
                    g4b = g4[:, :].bitcast(BF16)
                    hbb = hbf[:, :].bitcast(BF16)

                    dv_hj = {}
                    for jc in range(2):
                        pb = 5 + jc
                        b.waitg("tensor", ld_w2[(s, th, e)])
                        b.wait("tensor", "dv", max(bank_dv[pb], dv_m1))
                        b.wait("tensor", "lh", ldxh)
                        b.wait("tensor", "ac", bank_ac[pb])
                        for ic in range(8):
                            whi = wdst[:, ic * 256 + jc * 128:
                                       ic * 256 + (jc + 1) * 128]
                            wlo = wdst[:, 2048 + ic * 256 + jc * 128:
                                       2048 + ic * 256 + (jc + 1) * 128]
                            xhi = g4b[:, ic * 512 + th * 256:
                                      ic * 512 + (th + 1) * 256]
                            xlo = g4b[:, 4096 + ic * 512 + th * 256:
                                      4096 + ic * 512 + (th + 1) * 256]
                            st0 = (ic == 0)
                            b.op("tensor", lambda E, whi=whi, xhi=xhi, jc=jc,
                                 st0=st0: E.matmul(
                                     pbank[5 + jc][:, 0:256], whi, xhi,
                                     start=st0, stop=False), "pe")
                            b.op("tensor", lambda E, whi=whi, xlo=xlo, jc=jc:
                                 E.matmul(pbank[5 + jc][:, 0:256], whi, xlo,
                                          start=False, stop=False), "pe")
                            b.op("tensor", lambda E, wlo=wlo, xhi=xhi, jc=jc:
                                 E.matmul(pbank[5 + jc][:, 0:256], wlo, xhi,
                                          start=False, stop=False), "pe")
                        cg = b.op("tensor", lambda E, jc=jc, th=th, e=e: E.matmul(
                            pbank[5 + jc][:, 0:256], bigr[0:1, 0:128],
                            m1r[0:1, e * 256:(e + 1) * 256],
                            start=False, stop=True), "pe")
                        b.wait("scalar", "pe", max(cg, pe_hbf[0]))
                        if s == 0:
                            b.wait("scalar", "dv", dv_bg_e[e])
                        cge = b.op("scalar", lambda E, jc=jc, e=e:
                                   E.activation(
                                       pbank[5 + jc][:, 256:512],
                                       pbank[5 + jc][:, 0:256], AFT.Gelu,
                                       bias=bgel[:, jc * 8 + e: jc * 8 + e + 1]),
                                   "ac")
                        hhi = hbb[:, jc * 256:(jc + 1) * 256]
                        hlo = hbb[:, 512 + jc * 256: 512 + (jc + 1) * 256]
                        b.wait("scalar", "ac", cge)
                        chc = b.op("scalar", lambda E, jc=jc, hhi=hhi: E.copy(
                            out=hhi, in_=pbank[5 + jc][:, 256:512]), "ac")
                        b.wait("vector", "ac", chc)
                        dv_hj[jc] = b.op("vector", lambda E, jc=jc, hhi=hhi,
                                         hlo=hlo: E.tensor_tensor(
                                             hlo, pbank[5 + jc][:, 256:512], hhi,
                                             op=ALU.subtract), "dv")
                        bank_ac[pb] = chc
                        bank_dv[pb] = dv_hj[jc]

                    cw2 = 0
                    for jc in range(2):
                        b.wait("tensor", "dv", dv_hj[jc])
                        for tl in range(2):
                            for dh in range(2):
                                pb = tl * 2 + dh
                                hhi = hbb[:, jc * 256 + tl * 128:
                                          jc * 256 + (tl + 1) * 128]
                                hlo = hbb[:, 512 + jc * 256 + tl * 128:
                                          512 + jc * 256 + (tl + 1) * 128]
                                whi = wdst[:, 4096 + jc * 1024 + dh * 512:
                                           4096 + jc * 1024 + (dh + 1) * 512]
                                wlo = wdst[:, 6144 + jc * 1024 + dh * 512:
                                           6144 + jc * 1024 + (dh + 1) * 512]
                                b.op("tensor", lambda E, hhi=hhi, whi=whi, pb=pb:
                                     E.matmul(pbank[pb][:, :], hhi, whi,
                                              start=False, stop=False), "pe")
                                b.op("tensor", lambda E, hhi=hhi, wlo=wlo, pb=pb:
                                     E.matmul(pbank[pb][:, :], hhi, wlo,
                                              start=False, stop=False), "pe")
                                cw2 = b.op("tensor", lambda E, hlo=hlo, whi=whi,
                                           pb=pb: E.matmul(
                                               pbank[pb][:, :], hlo, whi,
                                               start=False, stop=False), "pe")
                    pe_hbf[0] = cw2
                    wbuf_free[sl_cur] = ("pe", cw2)
                    if sl_next is not None:
                        sl_cur = sl_next
                pe_m1_reader = b.cnt["pe"]
                if th == 1:
                    pe_g4_free = b.cnt["pe"]
                b.wait("tensor", "dv", max(ac_af, dv_wos))
                g6b = g6[:, :].bitcast(BF16)
                for tl in range(2):
                    for dh in range(2):
                        pb = tl * 2 + dh
                        cl = th * 2 + tl
                        for hc in range(8):
                            ahi = g6b[:, 2 * (hc * 512 + cl * 128):
                                      2 * (hc * 512 + cl * 128) + 128]
                            alo = g6b[:, 2 * (hc * 512 + cl * 128) + 128:
                                      2 * (hc * 512 + cl * 128) + 256]
                            whi = wosb[:, hc * 1024 + dh * 512:
                                       hc * 1024 + (dh + 1) * 512]
                            wlo = wosb[:, 8192 + hc * 1024 + dh * 512:
                                       8192 + hc * 1024 + (dh + 1) * 512]
                            b.op("tensor", lambda E, pb=pb, ahi=ahi, whi=whi:
                                 E.matmul(pbank[pb][:, :], ahi, whi,
                                          start=False, stop=False), "pe")
                            b.op("tensor", lambda E, pb=pb, ahi=ahi, wlo=wlo:
                                 E.matmul(pbank[pb][:, :], ahi, wlo,
                                          start=False, stop=False), "pe")
                            cwo = b.op("tensor", lambda E, pb=pb, alo=alo,
                                       whi=whi, hc=hc: E.matmul(
                                           pbank[pb][:, :], alo, whi,
                                           start=False, stop=(hc == 7)), "pe")
                        grp_last[pb] = cwo
                if th == 1:
                    pe_g6_free = b.cnt["pe"]

                for tl in range(2):
                    cl = th * 2 + tl
                    dv_last = 0
                    for dh in range(2):
                        pb = tl * 2 + dh
                        b.wait("vector", "pe", grp_last[pb])
                        b.wait("vector", "lx", ldx[3])
                        if th == 0 and tl == 0 and dh == 0:
                            b.wait("vector", "st", st_out_prev)
                            if s == 0:
                                b.wait("vector", "ac", ac_b1_end)
                                b.wait("vector", "sx", sx_spill_total)
                        dvf = b.op("vector", lambda E, cl=cl, dh=dh, pb=pb:
                                   E.tensor_tensor(
                                       g2[:, cl * 1024 + dh * 512:
                                          cl * 1024 + (dh + 1) * 512],
                                       g1[:, cl * 1024 + dh * 512:
                                          cl * 1024 + (dh + 1) * 512],
                                       pbank[pb][:, :], op=ALU.add), "dv")
                        dv_fin_bank[pb] = dvf
                        dv_last = dvf
                    dv_g1_done = dv_last
                    b.wait("scalar", "dv", dv_last)
                    st_out = b.op("scalar", lambda E, s=s, cl=cl: E.dma_start(
                        out=out_h[s * 512 + cl * 128: s * 512 + (cl + 1) * 128, :],
                        in_=g2[:, cl * 1024:(cl + 1) * 1024]), "st", 16)

            pe_prev_s_end = b.cnt["pe"]

        b.wait("scalar", "st", b.cnt["st"])
        b.flush()

    return nc


_NC_CACHE = [None]


def kernel(**inputs):
    x = np.ascontiguousarray(np.asarray(inputs["L0_attn_x"]), dtype=np.float32)

    def f32c(a):
        return np.ascontiguousarray(np.asarray(a), dtype=np.float32)

    shared = {
        "wq": f32c(np.asarray(inputs["L0_attn_Wq"]).T),
        "wk": f32c(np.asarray(inputs["L0_attn_Wk"]).T),
        "wv": f32c(np.asarray(inputs["L0_attn_Wv"]).T),
        "wu": f32c(np.asarray(inputs["L0_attn_Wu"]).T),
        "wo": f32c(np.asarray(inputs["L0_attn_Wo"]).T),
        "wg": f32c(np.asarray(inputs["L0_ffn_Wg"]).T),
        "bqp": f32c(np.asarray(inputs["L0_attn_bq"]).reshape(8, 128).T),
        "bkp": f32c(np.asarray(inputs["L0_attn_bk"]).reshape(8, 128).T),
        "bvr": f32c(np.asarray(inputs["L0_attn_bv"]).reshape(1, 1024)),
        "bur": f32c(np.asarray(inputs["L0_attn_bu"]).reshape(1, 1024)),
        "bor": f32c(np.asarray(inputs["L0_attn_bo"]).reshape(1, 1024)),
        "scp": f32c(np.asarray(inputs["L0_attn_scale"]).reshape(8, 128).T),
        "lns": f32c(np.asarray(inputs["L0_ffn_ln_scale"]).T),
        "lnb": f32c(np.asarray(inputs["L0_ffn_ln_bias"]).T),
        "w1": f32c(inputs["L0_ffn_W1"]),
        "b1t": f32c(np.asarray(inputs["L0_ffn_b1"]).T),
        "w2": f32c(inputs["L0_ffn_W2"]),
        "b2": f32c(inputs["L0_ffn_b2"]),
        "ident": np.eye(128, dtype=np.float32),
        "cmask": np.triu(np.ones((128, 128), dtype=np.float32)),
    }

    in_maps = []
    for c in range(NCORES):
        bb, sl = c // 4, c % 4
        m = dict(shared)
        m["x"] = np.ascontiguousarray(x[bb, sl * T:(sl + 1) * T, :])
        m["pmask"] = np.array([[1.0 if r < sl else 0.0
                                for r in range(NCORES)]], dtype=np.float32)
        in_maps.append(m)

    if _NC_CACHE[0] is None:
        _NC_CACHE[0] = build_nc()
    nc = _NC_CACHE[0]

    res = run_bass_kernel_spmd(nc, in_maps, core_ids=list(range(NCORES)))
    out = np.empty((B, N, D), dtype=np.float32)
    for c in range(NCORES):
        bb, sl = c // 4, c % 4
        out[bb, sl * T:(sl + 1) * T, :] = res.results[c]["out"]
    return out


if __name__ == "__main__":
    build_nc()
    print("build ok")


# version 26
# speedup vs baseline: 1.0437x; 1.0208x over previous
import numpy as np
from contextlib import ExitStack

import concourse.bass as bass
import concourse.mybir as mybir
from concourse.bass_utils import run_bass_kernel_spmd

F32 = mybir.dt.float32
BF16 = mybir.dt.bfloat16
AFT = mybir.ActivationFunctionType
ALU = mybir.AluOpType

NCORES = 8
B, N, D = 2, 8192, 1024
T = 2048
S = 512
NS = T // S
C = 128
H = 8
NE = 8
HF = 256
EPS_LN = 1e-5
EPS_RMS = 1e-8
BIG = 1.0e9

ENGINES = ("sync", "tensor", "vector", "scalar", "gpsimd")


class Builder:
    def __init__(self, nc, sems):
        self.nc = nc
        self.sems = sems
        self.cnt = {k: 0 for k in sems}
        self.q = {e: [] for e in ENGINES}

    def wait(self, eng, sem, val):
        if not val or val <= 0:
            return
        h = self.sems[sem]
        self.q[eng].append(lambda E, h=h, v=val: E.wait_ge(h, v))

    def waitg(self, eng, guard):
        if guard:
            self.wait(eng, guard[0], guard[1])

    def op(self, eng, build, sem=None, by=1):
        tgt = None
        if sem is not None:
            self.cnt[sem] += by
            tgt = self.cnt[sem]
            h = self.sems[sem]

            def f(E, build=build, h=h, by=by):
                build(E).then_inc(h, by)
        else:
            def f(E, build=build):
                build(E)
        self.q[eng].append(f)
        return tgt

    def flush(self):
        nc = self.nc
        active = [e for e in ENGINES if self.q[e]]
        with nc.Block() as block:
            for e in active:
                lst = self.q[e]

                def body(E, lst=lst):
                    for f in lst:
                        f(E)

                getattr(block, e)(body)
        self.q = {e: [] for e in ENGINES}


def build_nc():
    nc = bass.Bass("TRN2", target_bir_lowering=False)

    dt = nc.dram_tensor
    x_h = dt("x", [T, D], F32, kind="ExternalInput").ap()
    wq_h = dt("wq", [D, D], F32, kind="ExternalInput").ap()
    wk_h = dt("wk", [D, D], F32, kind="ExternalInput").ap()
    wv_h = dt("wv", [D, D], F32, kind="ExternalInput").ap()
    wu_h = dt("wu", [D, D], F32, kind="ExternalInput").ap()
    wo_h = dt("wo", [D, D], F32, kind="ExternalInput").ap()
    wg_h = dt("wg", [D, NE], F32, kind="ExternalInput").ap()
    bqP_h = dt("bqp", [128, 8], F32, kind="ExternalInput").ap()
    bkP_h = dt("bkp", [128, 8], F32, kind="ExternalInput").ap()
    bv_h = dt("bvr", [1, D], F32, kind="ExternalInput").ap()
    bu_h = dt("bur", [1, D], F32, kind="ExternalInput").ap()
    bo_h = dt("bor", [1, D], F32, kind="ExternalInput").ap()
    scP_h = dt("scp", [128, 8], F32, kind="ExternalInput").ap()
    lns_h = dt("lns", [D, NE], F32, kind="ExternalInput").ap()
    lnb_h = dt("lnb", [D, NE], F32, kind="ExternalInput").ap()
    w1_h = dt("w1", [NE, D, HF], F32, kind="ExternalInput").ap()
    b1T_h = dt("b1t", [HF, NE], F32, kind="ExternalInput").ap()
    w2_h = dt("w2", [NE, HF, D], F32, kind="ExternalInput").ap()
    b2_h = dt("b2", [NE, D], F32, kind="ExternalInput").ap()
    id_h = dt("ident", [128, 128], F32, kind="ExternalInput").ap()
    cm_h = dt("cmask", [128, 128], F32, kind="ExternalInput").ap()
    pm_h = dt("pmask", [1, NCORES], F32, kind="ExternalInput").ap()
    out_h = dt("out", [T, D], F32, kind="ExternalOutput").ap()

    wqb_h = dt("wqbf", [2, 128, 4096], F32).ap()
    wkb_h = dt("wkbf", [2, 128, 4096], F32).ap()
    wvb_h = dt("wvbf", [2, 128, 4096], F32).ap()
    wub_h = dt("wubf", [2, 128, 4096], F32).ap()
    w12b_h = dt("w12bf", [NE, 128, 4096], F32).ap()
    qsp_h = dt("qspill", [NS, 128, H * S], F32).ap()
    osp_h = dt("ospill", [NS, 128, 4 * D], F32).ap()
    usp_h = dt("uspill", [NS, 128, 4 * D], F32).ap()
    xhsp_h = dt("xhspill", [NS, 128, 4096], F32).ap()
    agi_h = dt("agin", [128, H * 128], F32).ap()
    ago_h = dt("agout", [4 * 128, H * 128], F32).ap()

    es = ExitStack()
    with es:
        g1 = es.enter_context(nc.sbuf_tensor("g1", [128, 4096], F32))
        g2 = es.enter_context(nc.sbuf_tensor("g2", [128, 4096], F32))
        g3 = es.enter_context(nc.sbuf_tensor("g3", [128, 4096], F32))
        g4 = es.enter_context(nc.sbuf_tensor("g4", [128, 4096], F32))
        g5 = es.enter_context(nc.sbuf_tensor("g5", [128, 4096], F32))
        g6 = es.enter_context(nc.sbuf_tensor("g6", [128, 4096], F32))
        wba = es.enter_context(nc.sbuf_tensor("wba", [128, 4096], F32))
        wbb = es.enter_context(nc.sbuf_tensor("wbb", [128, 4096], F32))
        wos = es.enter_context(nc.sbuf_tensor("wos", [128, 8192], F32))
        ktm = es.enter_context(nc.sbuf_tensor("ktm", [128, 1024], F32))
        smb = es.enter_context(nc.sbuf_tensor("smb", [128, 1024], F32))
        mst = es.enter_context(nc.sbuf_tensor("mst", [128, 1024], F32))
        tmpa = es.enter_context(nc.sbuf_tensor("tmpa", [128, 512], F32))
        hbf = es.enter_context(nc.sbuf_tensor("hbf", [128, 512], F32))
        wbc = es.enter_context(nc.sbuf_tensor("wbc", [128, 2048], F32))
        wgb = es.enter_context(nc.sbuf_tensor("wgb", [128, 128], BF16))
        logit = es.enter_context(nc.sbuf_tensor("logit", [128, 128], F32))
        ident = es.enter_context(nc.sbuf_tensor("ident_s", [128, 128], F32))
        cmask = es.enter_context(nc.sbuf_tensor("cmask_s", [128, 128], F32))
        mfs = es.enter_context(nc.sbuf_tensor("mfs", [8, 512], F32))
        m1r = es.enter_context(nc.sbuf_tensor("m1r", [1, 2048], F32))
        b2s = es.enter_context(nc.sbuf_tensor("b2s", [8, 1024], F32))
        stat = es.enter_context(nc.sbuf_tensor("stat", [128, 96], F32))
        mvs = es.enter_context(nc.sbuf_tensor("mvs", [128, 16], F32))
        cols = es.enter_context(nc.sbuf_tensor("cols", [128, 32], F32))
        bqp = es.enter_context(nc.sbuf_tensor("bqp_s", [128, 8], F32))
        bkp = es.enter_context(nc.sbuf_tensor("bkp_s", [128, 8], F32))
        scp = es.enter_context(nc.sbuf_tensor("scp_s", [128, 8], F32))
        wgs = es.enter_context(nc.sbuf_tensor("wgs", [128, 64], F32))
        lnsp = es.enter_context(nc.sbuf_tensor("lnsp", [128, 64], F32))
        lnbp = es.enter_context(nc.sbuf_tensor("lnbp", [128, 64], F32))
        b1p = es.enter_context(nc.sbuf_tensor("b1p", [128, 16], F32))
        bgel = es.enter_context(nc.sbuf_tensor("bgel", [128, 16], F32))
        mrow = es.enter_context(nc.sbuf_tensor("mrow", [128, 8], F32))
        pmb = es.enter_context(nc.sbuf_tensor("pmb", [128, 8], F32))
        pmr = es.enter_context(nc.sbuf_tensor("pmr", [1, 8], F32))
        onesr = es.enter_context(nc.sbuf_tensor("onesr", [1, 128], F32))
        zrow = es.enter_context(nc.sbuf_tensor("zrow", [1, 128], F32))
        bigr = es.enter_context(nc.sbuf_tensor("bigr", [1, 128], F32))
        bor = es.enter_context(nc.sbuf_tensor("bor_s", [1, 1024], F32))
        p0 = es.enter_context(nc.psum_tensor("p0", [128, 512], F32))
        p1 = es.enter_context(nc.psum_tensor("p1", [128, 512], F32))
        p2 = es.enter_context(nc.psum_tensor("p2", [128, 512], F32))
        p3 = es.enter_context(nc.psum_tensor("p3", [128, 512], F32))
        p4 = es.enter_context(nc.psum_tensor("p4", [128, 512], F32))
        p5 = es.enter_context(nc.psum_tensor("p5", [128, 512], F32))
        p6 = es.enter_context(nc.psum_tensor("p6", [128, 512], F32))
        p7 = es.enter_context(nc.psum_tensor("p7", [128, 512], F32))
        s_lx = es.enter_context(nc.semaphore("s_lx"))
        s_wa = es.enter_context(nc.semaphore("s_wa"))
        s_wb = es.enter_context(nc.semaphore("s_wb"))
        s_lq = es.enter_context(nc.semaphore("s_lq"))
        s_sm = es.enter_context(nc.semaphore("s_sm"))
        s_sx = es.enter_context(nc.semaphore("s_sx"))
        s_lr = es.enter_context(nc.semaphore("s_lr"))
        s_lh = es.enter_context(nc.semaphore("s_lh"))
        s_sq = es.enter_context(nc.semaphore("s_sq"))
        s_wc = es.enter_context(nc.semaphore("s_wc"))
        s_wd = es.enter_context(nc.semaphore("s_wd"))
        s_st = es.enter_context(nc.semaphore("s_st"))
        s_pe = es.enter_context(nc.semaphore("s_pe"))
        s_dv = es.enter_context(nc.semaphore("s_dv"))
        s_ac = es.enter_context(nc.semaphore("s_ac"))
        s_sp = es.enter_context(nc.semaphore("s_sp"))
        s_cc = es.enter_context(nc.semaphore("s_cc"))
        sems = {"lx": s_lx, "wa": s_wa, "wb": s_wb, "lq": s_lq,
                "sm": s_sm, "sx": s_sx, "lr": s_lr, "lh": s_lh, "sq": s_sq, "wc": s_wc,
                "wd": s_wd, "st": s_st, "pe": s_pe,
                "dv": s_dv, "ac": s_ac, "sp": s_sp, "cc": s_cc}
        b = Builder(nc, sems)
        pbank = [p0, p1, p2, p3, p4, p5, p6, p7]
        tslot = [p4[:, 0:128], p7[:, 0:128]]

        p4g = [None, None]
        p4n = [0]

        for (dst, src) in [
            (ident[:, :], id_h[:, :]), (cmask[:, :], cm_h[:, :]),
            (bqp[:, :], bqP_h[:, :]), (bkp[:, :], bkP_h[:, :]),
            (scp[:, :], scP_h[:, :]), (b2s[:, :], b2_h[:, :]),
            (m1r[0:1, 0:1024], bv_h[:, :]), (m1r[0:1, 1024:2048], bu_h[:, :]),
            (bor[:, :], bo_h[:, :]), (pmr[:, :], pm_h[:, :]),
        ]:
            b.op("sync", lambda E, d=dst, s=src: E.dma_start(out=d, in_=s), "lq", 16)
        for ic in range(8):
            b.op("scalar", lambda E, ic=ic: E.dma_start(
                out=wgs[:, ic * 8:(ic + 1) * 8],
                in_=wg_h[ic * 128:(ic + 1) * 128, :]), "sm", 16)
            b.op("scalar", lambda E, ic=ic: E.dma_start(
                out=lnsp[:, ic * 8:(ic + 1) * 8],
                in_=lns_h[ic * 128:(ic + 1) * 128, :]), "sm", 16)
            b.op("scalar", lambda E, ic=ic: E.dma_start(
                out=lnbp[:, ic * 8:(ic + 1) * 8],
                in_=lnb_h[ic * 128:(ic + 1) * 128, :]), "sm", 16)
        for jc in range(2):
            b.op("scalar", lambda E, jc=jc: E.dma_start(
                out=b1p[:, jc * 8:(jc + 1) * 8],
                in_=b1T_h[jc * 128:(jc + 1) * 128, :]), "sm", 16)
        sm_consts = b.cnt["sm"]
        ld_consts = b.cnt["lq"]
        b.wait("sync", "lq", ld_consts)

        b.op("vector", lambda E: E.memset(mst[:, :], 0.0))
        b.op("vector", lambda E: E.memset(onesr[:, :], 1.0))
        b.op("vector", lambda E: E.memset(zrow[:, :], 0.0))
        dv_ones = b.op("vector", lambda E: E.memset(bigr[:, :], BIG), "dv")
        b.wait("tensor", "lq", ld_consts)
        b.wait("tensor", "dv", dv_ones)
        cbc = b.op("tensor", lambda E: E.matmul(
            p7[:, 0:8], onesr[0:1, 0:128], pmr[0:1, 0:8],
            start=True, stop=True), "pe")
        b.wait("vector", "pe", cbc)
        dv_pmb = b.op("vector", lambda E: E.tensor_copy(
            out=pmb[:, :], in_=p7[:, 0:8]), "dv")
        dv_b0 = dv_pmb
        for oh in range(2):
            b.wait("tensor", "dv", dv_b0)
            cb2 = b.op("tensor", lambda E, oh=oh: E.matmul(
                p7[0:8, :], onesr[0:1, 0:8], bor[0:1, oh * 512:(oh + 1) * 512],
                start=True, stop=True), "pe")
            b.wait("vector", "pe", cb2)
            dv_b0 = b.op("vector", lambda E, oh=oh: E.tensor_tensor(
                b2s[0:8, oh * 512:(oh + 1) * 512],
                b2s[0:8, oh * 512:(oh + 1) * 512],
                p7[0:8, :], op=ALU.add), "dv")
        b.flush()

        def ln_tile(i, src, dst, pre=()):
            st6 = stat[:, i * 12:(i + 1) * 12]
            for g in pre:
                b.waitg("vector", g)
            b.op("vector", lambda E, i=i, st6=st6, src=src: E.bn_stats(
                st6[:, 0:6], src[:, 0:512]))
            cst = b.op("vector", lambda E, i=i, st6=st6, src=src: E.bn_stats(
                st6[:, 6:12], src[:, 512:1024]), "dv")
            b.wait("vector", "dv", cst)
            cag = b.op("vector", lambda E, i=i, st6=st6: E.bn_aggr(
                mvs[:, 2 * i:2 * i + 2], st6[:, 0:12]), "dv")
            b.wait("vector", "dv", cag)
            c = b.op("vector", lambda E, i=i: E.tensor_scalar(
                cols[:, i:i + 1], mvs[:, 2 * i + 1:2 * i + 2], EPS_LN, None,
                op0=ALU.add), "dv")
            b.wait("scalar", "dv", c)
            c2 = b.op("scalar", lambda E, i=i: E.activation(
                cols[:, 8 + i:9 + i], cols[:, i:i + 1], AFT.Sqrt), "ac")
            b.wait("vector", "ac", c2)
            crc = b.op("vector", lambda E, i=i: E.reciprocal(
                cols[:, 16 + i:17 + i], cols[:, 8 + i:9 + i]), "dv")
            b.wait("vector", "dv", crc)
            c = b.op("vector", lambda E, i=i, src=src, dst=dst: E.tensor_scalar(
                dst[:, :], src[:, :], mvs[:, 2 * i:2 * i + 1],
                cols[:, 16 + i:17 + i], op0=ALU.subtract, op1=ALU.mult), "dv")
            return c

        def ln_pass(src, dst, pre_dv_guard=None):
            outs = []
            for i in range(4):
                pre = (pre_dv_guard,) if (i == 0 and pre_dv_guard) else ()
                outs.append(ln_tile(i, src(i), dst(i), pre))
            return outs

        def transpose_pass(src, dst, lnc, writer_pre, xslots=()):
            last = 0
            first = True
            dstb = dst[:, :].bitcast(BF16)
            slots = [tslot[0], tslot[1]] + [x[0] for x in xslots]
            gl = [p4g[0], p4g[1]] + [x[1] for x in xslots]
            nsl = len(slots)
            rot = 0
            for i in range(4):
                for ic in range(8):
                    pre = [("dv", lnc[i])] if ic == 0 else []
                    sl = rot % nsl
                    rot += 1
                    tb = slots[sl]
                    b.waitg("tensor", gl[sl])
                    for g in pre:
                        b.waitg("tensor", g)
                    cpe = b.op("tensor", lambda E, i=i, ic=ic, tb=tb: E.transpose(
                        tb, src(i)[:, ic * 128:(ic + 1) * 128], ident[:, :]), "pe")
                    if first:
                        for g in writer_pre:
                            b.waitg("scalar", g)
                            b.waitg("vector", g)
                        first = False
                    hi = dstb[:, ic * 512 + i * 128: ic * 512 + (i + 1) * 128]
                    lo = dstb[:, 4096 + ic * 512 + i * 128:
                              4096 + ic * 512 + (i + 1) * 128]
                    b.wait("scalar", "pe", cpe)
                    cac = b.op("scalar", lambda E, hi=hi, tb=tb: E.copy(
                        out=hi, in_=tb), "ac")
                    b.wait("vector", "ac", cac)
                    last = b.op("vector", lambda E, hi=hi, lo=lo, tb=tb:
                                E.tensor_tensor(lo, tb, hi, op=ALU.subtract), "dv")
                    gl[sl] = ("dv", last)
            p4g[0] = gl[0]
            p4g[1] = gl[1]
            return last

        WHALF = {"q": wq_h, "k": wk_h, "v": wv_h, "u": wu_h}
        wslot = [wba, wbb]

        st_qsp = [0] * NS
        st_osp = [0] * NS
        st_usp = [0] * NS
        st_xhsp = [0] * NS
        pe_xnf_reader = 0
        pe_xw_reader = 0
        wload_seq = [0]
        wbuf_free = [None, None]
        ld_w = {}

        wbc_free = [0, 0]
        wbc_par = [0]

        def land_split(pieces, dsts, scale_cols=None):
            last = 0
            for (dmas, hi_ap, lo_ap, scol) in pieces:
                hp = wbc_par[0] % 2
                wbc_par[0] += 1
                base = hp * 1024
                wsem = "wc" if hp == 0 else "wd"
                b.wait("sync", "dv", wbc_free[hp])
                cc = 0
                for ent in dmas:
                    if ent[0] == "ap3":
                        cc = b.op("sync", lambda E, soff=ent[1], sten=ent[2],
                                  base=base: E.dma_start(
                                      out=bass.AP(wbc, base,
                                                  [[2048, 128], [512, 2],
                                                   [1, 512]]),
                                      in_=bass.AP(sten, soff,
                                                  [[1024, 128], [131072, 2],
                                                   [1, 512]])), wsem, 16)
                    else:
                        (src_ap, c0, nc_) = ent
                        cc = b.op("sync", lambda E, src_ap=src_ap, c0=c0,
                                  nc_=nc_, base=base: E.dma_start(
                                      out=wbc[:, base + c0:base + c0 + nc_],
                                      in_=src_ap), wsem, 16)
                n = sum(1024 if x[0] == "ap3" else x[2] for x in dmas)
                b.wait("scalar", wsem, cc)
                if scol is not None:
                    cac = b.op("scalar", lambda E, hi_ap=hi_ap, n=n, scol=scol,
                               base=base:
                               E.activation(hi_ap, wbc[:, base:base + n],
                                            AFT.Copy, scale=scol), "ac")
                    b.wait("vector", "ac", cac)
                    last = b.op("vector", lambda E, hi_ap=hi_ap, lo_ap=lo_ap,
                                n=n, scol=scol, base=base:
                                E.scalar_tensor_tensor(
                                    lo_ap, wbc[:, base:base + n], scol, hi_ap,
                                    op0=ALU.mult, op1=ALU.subtract), "dv")
                else:
                    cac = b.op("scalar", lambda E, hi_ap=hi_ap, n=n, base=base:
                               E.copy(out=hi_ap, in_=wbc[:, base:base + n]),
                               "ac")
                    b.wait("vector", "ac", cac)
                    last = b.op("vector", lambda E, hi_ap=hi_ap, lo_ap=lo_ap,
                                n=n, base=base:
                                E.tensor_tensor(lo_ap, wbc[:, base:base + n],
                                                hi_ap, op=ALU.subtract), "dv")
                wbc_free[hp] = last
            return ("dv", last)

        wst = {}
        slot_store = [0, 0]
        WBF = {"q": wqb_h, "k": wkb_h, "v": wvb_h, "u": wub_h}

        def load_whalf(which, half, tag):
            sl = wload_seq[0] % 2
            wload_seq[0] += 1
            dst = wslot[sl][:, :].bitcast(BF16)
            if tag[0] == 0 and which == "q":
                stage = g5 if half == 0 else g6
                wsem = "wa" if half == 0 else "wb"
                cc = 0
                for p in range(4):
                    cc = b.op("sync", lambda E, p=p, half=half, stage=stage:
                              E.dma_start(
                                  out=bass.AP(stage, p * 1024,
                                              [[4096, 128], [512, 2],
                                               [1, 512]]),
                                  in_=bass.AP(WHALF["q"].tensor,
                                              (2 * p) * 128 * 1024 + half * 512,
                                              [[1024, 128], [131072, 2],
                                               [1, 512]])), wsem, 16)
                last = 0
                for p in range(4):
                    hi_ap = dst[:, p * 1024:(p + 1) * 1024]
                    lo_ap = dst[:, 4096 + p * 1024: 4096 + (p + 1) * 1024]
                    b.wait("scalar", wsem, cc)
                    cac = b.op("scalar", lambda E, hi_ap=hi_ap, p=p,
                               stage=stage: E.copy(
                                   out=hi_ap,
                                   in_=stage[:, p * 1024:(p + 1) * 1024]),
                               "ac")
                    b.wait("vector", "ac", cac)
                    last = b.op("vector", lambda E, hi_ap=hi_ap, lo_ap=lo_ap,
                                p=p, stage=stage: E.tensor_tensor(
                                    lo_ap, stage[:, p * 1024:(p + 1) * 1024],
                                    hi_ap, op=ALU.subtract), "dv")
                ld_w[tag] = ("dv", last)
                b.wait("scalar", "dv", last)
                cst = b.op("scalar", lambda E, half=half, sl=sl:
                           E.dma_start(out=WBF["q"][half],
                                       in_=wslot[sl][:, :]), "lq", 16)
                wst[("q", half)] = cst
                slot_store[sl] = cst
                return sl
            if tag[0] == 0:
                b.waitg("scalar", wbuf_free[sl])
                b.waitg("vector", wbuf_free[sl])
                stq = b.cnt["lq"] if slot_store[sl] else 0
                b.wait("scalar", "lq", stq)
                b.wait("vector", "lq", stq)
                pieces = []
                for p in range(4):
                    dmas = [("ap3", (2 * p) * 128 * 1024 + half * 512,
                             WHALF[which].tensor)]
                    hi_ap = dst[:, p * 1024:(p + 1) * 1024]
                    lo_ap = dst[:, 4096 + p * 1024: 4096 + (p + 1) * 1024]
                    pieces.append((dmas, hi_ap, lo_ap, None))
                ld_w[tag] = land_split(pieces, None)
                b.wait("scalar", "dv", ld_w[tag][1])
                cst = b.op("scalar", lambda E, which=which, half=half, sl=sl:
                           E.dma_start(out=WBF[which][half],
                                       in_=wslot[sl][:, :]), "lq", 16)
                wst[(which, half)] = cst
                slot_store[sl] = cst
            else:
                b.waitg("sync", wbuf_free[sl])
                b.wait("sync", "lq", b.cnt["lq"])
                wsem = "wa" if sl == 0 else "wb"
                cc = b.op("sync", lambda E, which=which, half=half, sl=sl:
                          E.dma_start(out=wslot[sl][:, :],
                                      in_=WBF[which][half]), wsem, 16)
                ld_w[tag] = (wsem, cc)
            return sl

        pgrp = [0]
        bank_dv = {5: 0, 6: 0}
        bank_ac = {5: 0, 6: 0}
        exp_guard = [0]
        elu_prev = [0]

        def fm_half(dst, biasP, oh, sl, ldtag):
            last_dv = 0
            last_pe = 0
            if True:
                for ocl in range(4):
                    oc = oh * 4 + ocl
                    pb = 5 + (pgrp[0] % 2)
                    pgrp[0] += 1
                    ps = pbank[pb]
                    b.waitg("tensor", ld_w[ldtag])
                    b.wait("tensor", "dv", bank_dv[pb])
                    b.wait("tensor", "ac", bank_ac[pb])
                    cpe = 0
                    wb_ = wslot[sl][:, :].bitcast(BF16)
                    g2b = g2[:, :].bitcast(BF16)
                    for ic in range(8):
                        whi = wb_[:, ic * 512 + ocl * 128: ic * 512 + (ocl + 1) * 128]
                        wlo = wb_[:, 4096 + ic * 512 + ocl * 128:
                                  4096 + ic * 512 + (ocl + 1) * 128]
                        xhi = g2b[:, ic * 512:(ic + 1) * 512]
                        xlo = g2b[:, 4096 + ic * 512: 4096 + (ic + 1) * 512]
                        st0 = (ic == 0)
                        b.op("tensor", lambda E, whi=whi, xhi=xhi, ps=ps, st0=st0:
                             E.matmul(ps[:, :], whi, xhi,
                                      start=st0, stop=False), "pe")
                        b.op("tensor", lambda E, whi=whi, xlo=xlo, ps=ps:
                             E.matmul(ps[:, :], whi, xlo,
                                      start=False, stop=False), "pe")
                        cpe = b.op("tensor", lambda E, wlo=wlo, xhi=xhi, ps=ps,
                                   ic=ic: E.matmul(ps[:, :], wlo, xhi,
                                                   start=False, stop=(ic == 7)),
                                   "pe")
                    last_pe = cpe
                    ta = tmpa[:, 0:512]
                    dsl = dst[:, oc * 512:(oc + 1) * 512]
                    b.wait("vector", "pe", cpe)
                    b.wait("vector", "ac", exp_guard[0])
                    b.wait("vector", "dv", elu_prev[0])
                    b.op("vector", lambda E, ps=ps, oc=oc, ta=ta, biasP=biasP:
                         E.tensor_scalar(ta, ps[:, :], biasP[:, oc:oc + 1], 0.0,
                                         op0=ALU.add, op1=ALU.min), "dv")
                    cdv = b.op("vector", lambda E, ps=ps, oc=oc, dsl=dsl, biasP=biasP:
                               E.tensor_scalar(dsl, ps[:, :], biasP[:, oc:oc + 1], 0.0,
                                               op0=ALU.add, op1=ALU.max), "dv")
                    bank_dv[pb] = cdv
                    b.wait("scalar", "dv", cdv)
                    cac = b.op("scalar", lambda E, ta=ta:
                               E.activation(ta, ta, AFT.Exp), "ac")
                    exp_guard[0] = cac
                    b.wait("vector", "ac", cac)
                    last_dv = b.op("vector", lambda E, ta=ta, dsl=dsl:
                                   E.scalar_tensor_tensor(
                                       dsl, ta, -1.0, dsl,
                                       op0=ALU.add, op1=ALU.add), "dv")
                    elu_prev[0] = last_dv
            return last_dv, last_pe

        dv_msk = {}
        st_cpy = {}
        pe_oint = {}
        pe_smm = {}
        dv_logit = 0
        ac_oevac_prev = 0
        for stb in (p2, p3):
            b.op("tensor", lambda E, stb=stb: E.matmul(
                stb[:, :], zrow[0:1, 0:128], bor[0:1, 0:512],
                start=True, stop=True), "pe")

        c = 0
        for i in range(4):
            c = b.op("gpsimd", lambda E, i=i: E.dma_start(
                out=g1[:, i * 1024:(i + 1) * 1024],
                in_=x_h[i * 128:(i + 1) * 128, :]), "lx", 16)
        lnc_cur = [ln_tile(4 + i, g1[:, i * 1024:(i + 1) * 1024],
                           g1[:, i * 1024:(i + 1) * 1024],
                           pre=(("lx", c),)) for i in range(4)]

        slq0_next = load_whalf("q", 0, (0, "q", 0))
        slq1_next = load_whalf("q", 1, (0, "q", 1))
        b.wait("scalar", "sm", sm_consts)
        cwg = b.op("scalar", lambda E: E.copy(out=wgb[:, 0:64], in_=wgs[:, :]), "ac")
        b.wait("vector", "ac", cwg)
        dv_wg = b.op("vector", lambda E: E.tensor_tensor(
            wgb[:, 64:128], wgs[:, :], wgb[:, 0:64], op=ALU.subtract), "dv")
        for s in range(NS):
            slq0 = slq0_next
            slq1 = slq1_next

            og = ("ac", ac_oevac_prev) if s > 0 else None
            ac_xnf = transpose_pass(lambda i: g1[:, i * 1024:(i + 1) * 1024],
                                    g2, lnc_cur, [("pe", pe_xnf_reader),
                                                  ("sx", st_xhsp[s - 1] if s > 0 else 0)],
                                    xslots=[(p0[:, 0:128], og),
                                            (p1[:, 0:128], og)])
            pe_xw_reader = b.cnt["pe"]

            if s > 0:
                b.wait("vector", "st", st_osp[s - 1])
            b.wait("tensor", "dv", ac_xnf)

            def tm_half(oh, sl, ldtag, dst, brow_off):
                last_ac = 0
                cpe = 0
                wb_ = wslot[sl][:, :].bitcast(BF16)
                g2b = g2[:, :].bitcast(BF16)
                for sc in range(4):
                    pb = 5 + (pgrp[0] % 2)
                    pgrp[0] += 1
                    ps = pbank[pb]
                    b.waitg("tensor", ld_w[ldtag])
                    b.wait("tensor", "dv", bank_dv[pb])
                    b.wait("tensor", "ac", bank_ac[pb])
                    b.op("tensor", lambda E, ps=ps, oh=oh, brow_off=brow_off:
                         E.matmul(ps[:, :], onesr[0:1, 0:128],
                                  m1r[0:1, brow_off + oh * 512:
                                      brow_off + (oh + 1) * 512],
                                  start=True, stop=False), "pe")
                    for ic in range(8):
                        xhi = g2b[:, ic * 512 + sc * 128: ic * 512 + (sc + 1) * 128]
                        xlo = g2b[:, 4096 + ic * 512 + sc * 128:
                                  4096 + ic * 512 + (sc + 1) * 128]
                        whi = wb_[:, ic * 512:(ic + 1) * 512]
                        wlo = wb_[:, 4096 + ic * 512: 4096 + (ic + 1) * 512]
                        b.op("tensor", lambda E, xhi=xhi, whi=whi, ps=ps:
                             E.matmul(ps[:, :], xhi, whi,
                                      start=False, stop=False), "pe")
                        b.op("tensor", lambda E, xhi=xhi, wlo=wlo, ps=ps:
                             E.matmul(ps[:, :], xhi, wlo,
                                      start=False, stop=False), "pe")
                        cpe = b.op("tensor", lambda E, xlo=xlo, whi=whi, ps=ps,
                                   ic=ic: E.matmul(ps[:, :], xlo, whi,
                                                   start=False, stop=(ic == 7)),
                                   "pe")
                    dsl = dst[:, sc * 1024 + oh * 512: sc * 1024 + (oh + 1) * 512]
                    b.wait("scalar", "pe", cpe)
                    last_ac = b.op("scalar", lambda E, dsl=dsl, ps=ps:
                                   E.activation(dsl, ps[:, :], AFT.Silu), "ac")
                    bank_ac[pb] = last_ac
                    bank_dv[pb] = 0
                return cpe, last_ac

            def v_half(oh, sl, ldtag):
                return tm_half(oh, sl, ldtag, g5, 0)

            dv_q0, pe_q0 = fm_half(g3, bqp, 0, slq0, (s, "q", 0))
            wbuf_free[slq0] = ("pe", pe_q0)
            slk0 = load_whalf("k", 0, (s, "k", 0))
            dv_qf, pe_q1 = fm_half(g3, bqp, 1, slq1, (s, "q", 1))
            wbuf_free[slq1] = ("pe", pe_q1)
            slk1 = load_whalf("k", 1, (s, "k", 1))
            dv_k0, pe_k0 = fm_half(g4, bkp, 0, slk0, (s, "k", 0))
            wbuf_free[slk0] = ("pe", pe_k0)
            slv0 = load_whalf("v", 0, (s, "v", 0))
            dv_kf, pe_k1 = fm_half(g4, bkp, 1, slk1, (s, "k", 1))
            wbuf_free[slk1] = ("pe", pe_k1)
            slv1 = load_whalf("v", 1, (s, "v", 1))
            pe_v0, ac_v0 = v_half(0, slv0, (s, "v", 0))
            wbuf_free[slv0] = ("pe", pe_v0)
            slu0 = load_whalf("u", 0, (s, "u", 0))
            pe_v1, last_ac_v = v_half(1, slv1, (s, "v", 1))
            wbuf_free[slv1] = ("pe", pe_v1)
            slu1 = load_whalf("u", 1, (s, "u", 1))
            dv_qf = max(dv_qf, dv_q0)
            dv_kf = max(dv_kf, dv_k0)

            def u_half(oh, sl, ldtag, first):
                if first:
                    b.wait("scalar", "st", st_osp[s - 1] if s > 0 else 0)
                return tm_half(oh, sl, ldtag, g6, 1024)

            pe_u0, ac_u0 = u_half(0, slu0, (s, "u", 0), True)
            wbuf_free[slu0] = ("pe", pe_u0)
            pe_u1, ac_u1 = u_half(1, slu1, (s, "u", 1), False)
            wbuf_free[slu1] = ("pe", pe_u1)
            b.wait("scalar", "ac", ac_u1)
            st_usp[s] = b.op("scalar", lambda E, s=s: E.dma_start(
                out=usp_h[s, :, :], in_=g6[:, :]), "st", 16)

            for cl in range(4):
                gc = 4 * s + cl
                pb = 5 + (pgrp[0] % 2)
                pgrp[0] += 1
                ps = pbank[pb]
                if s == 0 and cl == 0:
                    b.wait("tensor", "dv", dv_wg)
                b.wait("tensor", "dv", bank_dv[pb])
                b.wait("tensor", "ac", bank_ac[pb])
                cpe = 0
                g2b = g2[:, :].bitcast(BF16)
                for ic in range(8):
                    xhi = g2b[:, ic * 512 + cl * 128: ic * 512 + (cl + 1) * 128]
                    xlo = g2b[:, 4096 + ic * 512 + cl * 128:
                              4096 + ic * 512 + (cl + 1) * 128]
                    whi = wgb[:, ic * 8:(ic + 1) * 8]
                    wlo = wgb[:, 64 + ic * 8: 64 + (ic + 1) * 8]
                    st0 = (ic == 0)
                    b.op("tensor", lambda E, xhi=xhi, whi=whi, ps=ps, st0=st0:
                         E.matmul(ps[:, 0:8], xhi, whi, start=st0,
                                  stop=False), "pe")
                    b.op("tensor", lambda E, xhi=xhi, wlo=wlo, ps=ps: E.matmul(
                        ps[:, 0:8], xhi, wlo, start=False, stop=False), "pe")
                    cpe = b.op("tensor", lambda E, xlo=xlo, whi=whi, ps=ps, ic=ic:
                               E.matmul(ps[:, 0:8], xlo, whi,
                                        start=False, stop=(ic == 7)), "pe")
                b.wait("vector", "pe", cpe)
                dv_logit = b.op("vector", lambda E, gc=gc, ps=ps: E.tensor_copy(
                    out=logit[:, gc * 8:(gc + 1) * 8], in_=ps[:, 0:8]), "dv")
                bank_dv[pb] = dv_logit
            pe_g2_free = b.cnt["pe"]

            lnc2 = [ln_tile(0, g1[:, 0:1024], g1[:, 0:1024],
                            pre=(("pe", pe_xw_reader),
                                 ("dv", lnc_cur[0])))]

            if s > 0:
                b.wait("scalar", "st", st_osp[s - 1])
            ac_oevac = ac_oevac_prev
            pairs = [(cl, h) for cl in range(4) for h in range(8)]

            def emit_TS(j):
                cl, h = pairs[j]
                gc = 4 * s + cl
                sbank = p5 if h % 2 == 0 else p6
                pre = [("dv", dv_kf)] if j == 0 else []
                sl = p4n[0] % 2
                p4n[0] += 1
                tb = tslot[sl]
                b.waitg("tensor", p4g[sl])
                for g in pre:
                    b.waitg("tensor", g)
                ct = b.op("tensor", lambda E, h=h, cl=cl, tb=tb: E.transpose(
                    tb, g4[:, h * 512 + cl * 128: h * 512 + (cl + 1) * 128],
                    ident[:, :]), "pe")
                b.wait("scalar", "pe", max(ct, pe_smm.get((gc - 1, h), 0)))
                cke = b.op("scalar", lambda E, h=h, tb=tb: E.copy(
                    out=ktm[:, h * 128:(h + 1) * 128], in_=tb), "ac")
                p4g[sl] = ("ac", cke)
                ktm_ev[(gc, h)] = cke
                if j >= 2:
                    pcl, ph = pairs[j - 2]
                    b.wait("tensor", "dv", dv_msk.get((4 * s + pcl, ph), 0))
                else:
                    b.wait("tensor", "dv", dv_msk.get((4 * s - 1, 6 + j), 0))
                    pb = 5 + (h % 2)
                    b.wait("tensor", "dv", bank_dv[pb])
                    b.wait("tensor", "ac", bank_ac[pb])
                if j == 0:
                    b.wait("tensor", "dv", max(dv_qf, dv_kf))
                    b.wait("tensor", "ac", last_ac_v)
                cs = b.op("tensor", lambda E, h=h, cl=cl, sbank=sbank: E.matmul(
                    sbank[:, 0:128],
                    g4[:, h * 512 + cl * 128: h * 512 + (cl + 1) * 128],
                    g3[:, h * 512 + cl * 128: h * 512 + (cl + 1) * 128],
                    start=True, stop=True), "pe")
                b.wait("vector", "pe", max(cs, pe_oint.get((gc - 1, h), 0)))
                cm_ = b.op("vector", lambda E, h=h, sbank=sbank: E.tensor_tensor(
                    smb[:, h * 128:(h + 1) * 128], cmask[:, :], sbank[:, 0:128],
                    op=ALU.mult), "dv")
                dv_msk[(gc, h)] = cm_

            ktm_ev = {}
            lnc_next = []
            gate_hist = []
            emit_TS(0)
            emit_TS(1)
            for j, (cl, h) in enumerate(pairs):
                gc = 4 * s + cl
                if j + 2 < len(pairs):
                    emit_TS(j + 2)
                po = p0 if h < 4 else p1
                off = (h % 4) * 128
                b.wait("tensor", "dv", dv_msk[(gc, h)])
                if h == 0:
                    b.wait("tensor", "ac", ac_oevac)
                b.op("tensor", lambda E, h=h, cl=cl, po=po, off=off: E.matmul(
                    po[:, off:off + 128], smb[:, h * 128:(h + 1) * 128],
                    g5[:, cl * 1024 + h * 128: cl * 1024 + (h + 1) * 128],
                    start=True, stop=False), "pe")
                b.wait("tensor", "dv", st_cpy.get(gc - 1, 0))
                coi = b.op("tensor", lambda E, h=h, cl=cl, po=po, off=off:
                           E.matmul(
                               po[:, off:off + 128],
                               g3[:, h * 512 + cl * 128: h * 512 + (cl + 1) * 128],
                               mst[:, h * 128:(h + 1) * 128],
                               start=False, stop=True), "pe")
                pe_oint[(gc, h)] = coi
                stb = p2 if h < 4 else p3
                b.wait("tensor", "ac", ktm_ev[(gc, h)])
                if h == 0:
                    b.wait("tensor", "dv", st_cpy.get(gc - 1, 0))
                csm = b.op("tensor", lambda E, h=h, cl=cl, stb=stb, off=off:
                           E.matmul(
                    stb[:, off:off + 128], ktm[:, h * 128:(h + 1) * 128],
                    g5[:, cl * 1024 + h * 128: cl * 1024 + (h + 1) * 128],
                    start=False, stop=False, skip_group_check=True), "pe")
                pe_smm[(gc, h)] = csm
                ic = h
                sl = p4n[0] % 2
                p4n[0] += 1
                tb = tslot[sl]
                b.waitg("tensor", p4g[sl])
                if ic == 0:
                    b.wait("tensor", "dv", lnc2[cl])
                ctp_xh_last = b.op("tensor", lambda E, ic=ic, cl=cl, tb=tb:
                                   E.transpose(
                                       tb, g1[:, cl * 1024 + ic * 128:
                                              cl * 1024 + (ic + 1) * 128],
                                       ident[:, :]), "pe")
                if ic == 0 and cl == 0:
                    b.wait("scalar", "pe", pe_g2_free)
                    b.wait("vector", "pe", pe_g2_free)
                g2b_ = g2[:, :].bitcast(BF16)
                xh_hi = g2b_[:, ic * 512 + cl * 128: ic * 512 + (cl + 1) * 128]
                xh_lo = g2b_[:, 4096 + ic * 512 + cl * 128:
                             4096 + ic * 512 + (cl + 1) * 128]
                b.wait("scalar", "pe", ctp_xh_last)
                cxa = b.op("scalar", lambda E, xh_hi=xh_hi, tb=tb:
                           E.copy(out=xh_hi, in_=tb), "ac")
                b.wait("vector", "ac", cxa)
                ac_ev_xh = b.op("vector", lambda E, xh_hi=xh_hi, xh_lo=xh_lo,
                                tb=tb: E.tensor_tensor(
                                    xh_lo, tb, xh_hi, op=ALU.subtract), "dv")
                p4g[sl] = ("dv", ac_ev_xh)

                if h == 7:
                    if cl == 0:
                        b.wait("scalar", "st", st_usp[s])
                    b.wait("vector", "pe", max(csm, pe_oint[(gc, 7)]))
                    b.op("vector", lambda E: E.tensor_copy(
                        out=mst[:, 0:512], in_=p2[:, :]), "dv")
                    st_cpy[gc] = b.op("vector", lambda E: E.tensor_copy(
                        out=mst[:, 512:1024], in_=p3[:, :]), "dv")
                    b.wait("scalar", "pe", pe_oint[(gc, 7)])
                    b.op("scalar", lambda E, cl=cl: E.copy(
                        out=g6[:, cl * 1024: cl * 1024 + 512], in_=p0[:, :]), "ac")
                    ac_oevac = b.op("scalar", lambda E, cl=cl: E.copy(
                        out=g6[:, cl * 1024 + 512: (cl + 1) * 1024],
                        in_=p1[:, :]), "ac")
                    b.wait("scalar", "dv", dv_qf)
                    st_qsp[s] = b.op("scalar", lambda E, s=s, cl=cl: E.dma_start(
                        out=bass.AP(qsp_h.tensor, s * 128 * 4096 + cl * 128,
                                    [[4096, 128], [512, 8], [1, 128]]),
                        in_=bass.AP(g3, cl * 128,
                                    [[4096, 128], [512, 8], [1, 128]])),
                        "st", 16)
                    b.wait("scalar", "ac", ac_oevac)
                    st_osp[s] = b.op("scalar", lambda E, s=s, cl=cl: E.dma_start(
                        out=osp_h[s, :, cl * 1024:(cl + 1) * 1024],
                        in_=g6[:, cl * 1024:(cl + 1) * 1024]), "st", 16)
                    if s < NS - 1 and cl == 1:
                        slq0_next = load_whalf("q", 0, (s + 1, "q", 0))
                    if s < NS - 1 and cl == 2:
                        slq1_next = load_whalf("q", 1, (s + 1, "q", 1))
                    if s < NS - 1:
                        b.wait("gpsimd", "pe", ctp_xh_last)
                        ldn = b.op("gpsimd", lambda E, s=s, cl=cl: E.dma_start(
                            out=g1[:, cl * 1024:(cl + 1) * 1024],
                            in_=x_h[(s + 1) * 512 + cl * 128:
                                    (s + 1) * 512 + (cl + 1) * 128, :]), "lx", 16)
                        b.wait("gpsimd", "lx", ldn)
                        cgate = b.op("gpsimd", lambda E: E.memset(
                            cols[0:1, 29:30], 0.0), "cc")
                        gate_hist.append(cgate)
                        if cl >= 1:
                            tj = cl - 1
                            lnc_next.append(ln_tile(
                                4 + tj, g1[:, tj * 1024:(tj + 1) * 1024],
                                g1[:, tj * 1024:(tj + 1) * 1024],
                                pre=(("cc", gate_hist[tj]),)))
                    b.wait("scalar", "dv", ac_ev_xh)
                    st_xhsp[s] = b.op("scalar", lambda E, s=s, cl=cl: E.dma_start(
                        out=bass.AP(xhsp_h.tensor, s * 128 * 4096 + cl * 64,
                                    [[4096, 128], [2048, 2], [256, 8],
                                     [1, 64]]),
                        in_=bass.AP(g2, cl * 64,
                                    [[4096, 128], [2048, 2], [256, 8],
                                     [1, 64]])),
                        "sx", 16)
                    if cl < 3:
                        lnc2.append(ln_tile(
                            cl + 1, g1[:, (cl + 1) * 1024:(cl + 2) * 1024],
                            g1[:, (cl + 1) * 1024:(cl + 2) * 1024],
                            pre=(("dv", lnc_cur[cl + 1]),)))
            ac_oevac_prev = ac_oevac

            if s < NS - 1:
                lnc_next.append(ln_tile(
                    4 + 3, g1[:, 3 * 1024:4 * 1024],
                    g1[:, 3 * 1024:4 * 1024],
                    pre=(("cc", gate_hist[3]),)))
                lnc_cur = lnc_next
            bank_dv[5] = max(bank_dv[5], dv_msk[(4 * s + 3, 6)])
            bank_dv[6] = max(bank_dv[6], dv_msk[(4 * s + 3, 7)])

            pe_xnf_reader = b.cnt["pe"]

        st_spill_total = b.cnt["st"]
        sx_spill_total = b.cnt["sx"]
        pe_b1_end = b.cnt["pe"]
        ac_b1_end = b.cnt["ac"]
        dv_b1_end = b.cnt["dv"]
        pe_oint_f = pe_oint[(4 * NS - 1, 7)]
        pe_smm_f = pe_smm[(4 * NS - 1, 7)]
        dv_msk_f = dv_msk[(4 * NS - 1, 7)]
        st_cpy_f = st_cpy[4 * NS - 1]

        b.wait("gpsimd", "dv", st_cpy_f)
        b.wait("gpsimd", "pe", pe_oint_f)
        c0 = b.op("gpsimd", lambda E: E.dma_start(
            out=agi_h[:, :], in_=mst[:, :]), "sp", 16)
        b.wait("gpsimd", "sp", c0)
        ccv = b.op("gpsimd", lambda E: E.collective_compute(
            "AllGather", ALU.bypass,
            replica_groups=[[0, 1, 2, 3], [4, 5, 6, 7]],
            ins=[agi_h[:, :]], outs=[ago_h[:, :]]), "cc")
        b.wait("gpsimd", "st", st_spill_total)
        b.wait("gpsimd", "sx", sx_spill_total)
        b.wait("gpsimd", "pe", pe_b1_end)
        b.wait("gpsimd", "dv", dv_b1_end)
        b.wait("gpsimd", "ac", ac_b1_end)
        b.op("gpsimd", lambda E: E.dma_start(
            out=g3[:, :], in_=osp_h[0, :, :]), "lr", 16)
        b.op("gpsimd", lambda E: E.dma_start(
            out=g5[:, :], in_=usp_h[0, :, :]), "lr", 16)
        ldo0 = b.op("gpsimd", lambda E: E.dma_start(
            out=g6[:, :], in_=qsp_h[0, :, :]), "lr", 16)
        ldxh0 = b.op("gpsimd", lambda E: E.dma_start(
            out=g4[:, :], in_=xhsp_h[0, :, :]), "lh", 16)
        ldx0 = []
        for i in range(4):
            c = b.op("gpsimd", lambda E, i=i: E.dma_start(
                out=g1[:, i * 1024:(i + 1) * 1024],
                in_=x_h[i * 128:(i + 1) * 128, :]), "lx", 16)
            ldx0.append(c)
        b.wait("vector", "sp", c0)
        b.wait("vector", "pe", pe_oint_f)
        dv_mz = b.op("vector", lambda E: E.memset(mst[:, :], 0.0), "dv")

        wosb = wos[:, :].bitcast(BF16)
        wo_pieces = []
        for hc in range(8):
            wo_pieces.append((
                [(wo_h[hc * 128:(hc + 1) * 128, :], 0, 1024)],
                wosb[:, hc * 1024:(hc + 1) * 1024],
                wosb[:, 8192 + hc * 1024: 8192 + (hc + 1) * 1024],
                scp[:, hc:hc + 1]))
        b.wait("scalar", "sm", sm_consts)
        b.wait("vector", "sm", sm_consts)
        b.wait("scalar", "pe", pe_b1_end)
        b.wait("vector", "pe", pe_b1_end)
        dv_wos = land_split(wo_pieces, None)[1]

        b.wait("gpsimd", "cc", ccv)
        dv_pref = max(dv_mz, dv_msk_f)
        b.wait("gpsimd", "pe", max(pe_oint_f, pe_smm_f))
        pbuf = [smb, ktm]
        dv_slot = [dv_pref, dv_pref]
        for r in range(4):
            buf = pbuf[r % 2]
            psem = "sp" if r % 2 == 0 else "sq"
            b.wait("gpsimd", "dv", dv_slot[r % 2])
            csp = b.op("gpsimd", lambda E, r=r, buf=buf: E.dma_start(
                out=buf[:, :], in_=ago_h[r * 128:(r + 1) * 128, :]), psem, 16)
            b.wait("vector", psem, csp)
            b.wait("vector", "dv", dv_pref)
            dv_pref = b.op("vector", lambda E, r=r, buf=buf: E.scalar_tensor_tensor(
                mst[:, :], buf[:, :], pmb[:, r:r + 1], mst[:, :],
                op0=ALU.mult, op1=ALU.add), "dv")
            dv_slot[r % 2] = dv_pref
        ld_w2 = {}

        def load_w12(e, tag, first):
            sl = wload_seq[0] % 2
            wload_seq[0] += 1
            dst = wslot[sl][:, :].bitcast(BF16)
            if not first:
                b.waitg("sync", wbuf_free[sl])
                b.wait("sync", "lq", b.cnt["lq"])
                wsem = "wa" if sl == 0 else "wb"
                cc = b.op("sync", lambda E, e=e, sl=sl: E.dma_start(
                    out=wslot[sl][:, :], in_=w12b_h[e]), wsem, 16)
                ld_w2[tag] = (wsem, cc)
                return sl
            b.waitg("scalar", wbuf_free[sl])
            b.waitg("vector", wbuf_free[sl])
            stq = b.cnt["lq"] if slot_store[sl] else 0
            b.wait("scalar", "lq", stq)
            b.wait("vector", "lq", stq)
            b.wait("tensor", "sm", sm_consts)
            b.wait("scalar", "sm", sm_consts)
            b.wait("vector", "sm", sm_consts)
            if True:
                cbv = 0
                last = 0
                for hh in range(2):
                    hp = wbc_par[0] % 2
                    wbc_par[0] += 1
                    base = hp * 1024
                    wsem = "wc" if hp == 0 else "wd"
                    b.wait("sync", "dv", wbc_free[hp])
                    cch = b.op("sync", lambda E, e=e, hh=hh, base=base:
                               E.dma_start(
                                   out=bass.AP(wbc, base,
                                               [[2048, 128], [256, 4],
                                                [1, 256]]),
                                   in_=bass.AP(w1_h.tensor,
                                               e * 262144 + hh * 131072,
                                               [[256, 128], [32768, 4],
                                                [1, 256]])), wsem, 16)
                    b.wait("tensor", wsem, cch)
                    if e == 0 and hh == 0:
                        b.waitg("tensor", p4g[1])
                    for jc in range(2):
                        for i4 in range(4):
                            ic = hh * 4 + i4
                            cbv = b.op("tensor", lambda E, jc=jc, i4=i4, ic=ic,
                                       e=e, base=base, hh=hh: E.matmul(
                                           p7[:, jc:jc + 1],
                                           wbc[:, base + i4 * 256 + jc * 128:
                                               base + i4 * 256 +
                                               (jc + 1) * 128],
                                           lnbp[:, ic * 8 + e: ic * 8 + e + 1],
                                           start=(hh == 0 and jc == 0
                                                  and i4 == 0),
                                           stop=(hh == 1 and jc == 1
                                                 and i4 == 3)), "pe")
                    b.wait("scalar", wsem, cch)
                    b.wait("scalar", "pe", cbv)
                    for i4 in range(4):
                        ic = hh * 4 + i4
                        scol = lnsp[:, ic * 8 + e: ic * 8 + e + 1]
                        hi = dst[:, ic * 256:(ic + 1) * 256]
                        lo = dst[:, 2048 + ic * 256: 2048 + (ic + 1) * 256]
                        cac = b.op("scalar", lambda E, hi=hi, i4=i4, base=base,
                                   scol=scol: E.activation(
                                       hi, wbc[:, base + i4 * 256:
                                               base + (i4 + 1) * 256],
                                       AFT.Copy, scale=scol), "ac")
                        b.wait("vector", "ac", cac)
                        last = b.op("vector", lambda E, hi=hi, lo=lo, i4=i4,
                                    base=base, scol=scol:
                                    E.scalar_tensor_tensor(
                                        lo, wbc[:, base + i4 * 256:
                                                base + (i4 + 1) * 256],
                                        scol, hi,
                                        op0=ALU.mult, op1=ALU.subtract), "dv")
                    wbc_free[hp] = last
                b.wait("vector", "pe", cbv)
                for jc in range(2):
                    dv_bg_e[e] = b.op("vector", lambda E, jc=jc, e=e:
                                      E.tensor_tensor(
                                          bgel[:, jc * 8 + e: jc * 8 + e + 1],
                                          b1p[:, jc * 8 + e: jc * 8 + e + 1],
                                          p7[:, jc:jc + 1], op=ALU.add), "dv")
            pieces = []
            for jc in range(2):
                dmas = [(w2_h[e, jc * 128:(jc + 1) * 128, :], 0, 1024)]
                hi = dst[:, 4096 + jc * 1024: 4096 + (jc + 1) * 1024]
                lo = dst[:, 6144 + jc * 1024: 6144 + (jc + 1) * 1024]
                pieces.append((dmas, hi, lo, None))
            ld_w2[tag] = land_split(pieces, None)
            b.wait("scalar", "dv", ld_w2[tag][1])
            cst = b.op("scalar", lambda E, e=e, sl=sl: E.dma_start(
                out=w12b_h[e], in_=wslot[sl][:, :]), "lq", 16)
            wst[("w12", e)] = cst
            slot_store[sl] = cst
            return sl

        pe_prev_s_end = 0
        dv_g1_done = 0
        st_out = 0
        dv_bg_e = {}
        dv_fin_bank = {0: dv_b1_end, 1: dv_b1_end,
                       2: max(dv_b1_end, st_cpy_f), 3: max(dv_b1_end, st_cpy_f)}
        pe_hbf = [0]
        dv_cA_last = 0
        st_m1 = 0
        pe_at = pe_smm_f
        pe_m1_reader = pe_b1_end
        ac_sq = 0
        pe_g4_free = 0
        pe_g6_free = 0
        st_out_prev = 0

        for s in range(NS):
            st_out_prev = st_out
            if s == 0:
                ldo, ldxh, ldx = ldo0, ldxh0, ldx0
            else:
                b.wait("gpsimd", "ac", ac_sq)
                b.wait("gpsimd", "dv", dv_cA_last)
                b.op("gpsimd", lambda E, s=s: E.dma_start(
                    out=g3[:, :], in_=osp_h[s, :, :]), "lr", 16)
                b.wait("gpsimd", "dv", dv_cA_last)
                b.op("gpsimd", lambda E, s=s: E.dma_start(
                    out=g5[:, :], in_=usp_h[s, :, :]), "lr", 16)
                b.wait("gpsimd", "pe", pe_g6_free)
                ldo = b.op("gpsimd", lambda E, s=s: E.dma_start(
                    out=g6[:, :], in_=qsp_h[s, :, :]), "lr", 16)
                b.wait("gpsimd", "pe", pe_g4_free)
                ldxh = b.op("gpsimd", lambda E, s=s: E.dma_start(
                    out=g4[:, :], in_=xhsp_h[s, :, :]), "lh", 16)
                ldx = []
                b.wait("gpsimd", "dv", dv_g1_done)
                for i in range(4):
                    c = b.op("gpsimd", lambda E, s=s, i=i: E.dma_start(
                        out=g1[:, i * 1024:(i + 1) * 1024],
                        in_=x_h[s * 512 + i * 128: s * 512 + (i + 1) * 128, :]),
                        "lx", 16)
                    ldx.append(c)

            ctp_prev = 0
            dv_mfs = 0
            for cl in range(4):
                gc = 4 * s + cl
                b.wait("vector", "pe", ctp_prev)
                crm = b.op("vector", lambda E, gc=gc: E.reduce_max(
                    cols[:, 24:25], logit[:, gc * 8:(gc + 1) * 8],
                    mybir.AxisListType.X), "dv")
                b.wait("vector", "dv", crm)
                cdv = b.op("vector", lambda E, gc=gc: E.tensor_scalar(
                    mrow[:, :], logit[:, gc * 8:(gc + 1) * 8], cols[:, 24:25],
                    None, op0=ALU.is_equal), "dv")
                sl = p4n[0] % 2
                p4n[0] += 1
                tb = tslot[sl]
                b.waitg("tensor", p4g[sl])
                b.wait("tensor", "dv", cdv)
                ctp_prev = b.op("tensor", lambda E, cl=cl, tb=tb: E.transpose(
                    tb[0:8, :], mrow[:, :], ident[:, :]), "pe")
                b.wait("vector", "pe", ctp_prev)
                b.wait("vector", "sm", st_m1)
                dv_mfs = b.op("vector", lambda E, cl=cl, tb=tb: E.tensor_copy(
                    out=mfs[0:8, cl * 128:(cl + 1) * 128],
                    in_=tb[0:8, :]), "dv")
                p4g[sl] = ("dv", dv_mfs)
            ac_af = 0
            dv_corr = 0
            cA = 0
            atsl = [tslot[0], tslot[1],
                    pbank[5][:, 0:128], pbank[6][:, 0:128]]
            atg = [[p4g[0]], [p4g[1]],
                   [("dv", bank_dv[5]), ("ac", bank_ac[5])],
                   [("dv", bank_dv[6]), ("ac", bank_ac[6])]]
            atrot = [0]
            for cl in range(4):
                b.wait("tensor", "lr", ldo)
                b.wait("tensor", "dv", dv_corr)
                if s == 0 and cl == 0:
                    b.wait("tensor", "ac", ac_b1_end)
                    b.wait("tensor", "dv", dv_pref)
                cpe = 0
                for h in range(8):
                    po = p0 if h < 4 else p1
                    off = (h % 4) * 128
                    cpe = b.op("tensor", lambda E, h=h, cl=cl, po=po, off=off:
                               E.matmul(
                                   po[:, off:off + 128],
                                   g6[:, h * 512 + cl * 128: h * 512 + (cl + 1) * 128],
                                   mst[:, h * 128:(h + 1) * 128],
                                   start=True, stop=True), "pe")
                b.wait("vector", "pe", cpe)
                b.op("vector", lambda E, cl=cl: E.tensor_tensor(
                    g3[:, cl * 1024: cl * 1024 + 512],
                    g3[:, cl * 1024: cl * 1024 + 512], p0[:, :], op=ALU.add), "dv")
                dv_corr = b.op("vector", lambda E, cl=cl: E.tensor_tensor(
                    g3[:, cl * 1024 + 512: (cl + 1) * 1024],
                    g3[:, cl * 1024 + 512: (cl + 1) * 1024], p1[:, :],
                    op=ALU.add), "dv")

                b.wait("scalar", "dv", dv_corr)
                if s == 0 and cl == 0:
                    b.wait("scalar", "dv", dv_pref)
                b.wait("scalar", "ac", ac_sq)
                ac_sq = b.op("scalar", lambda E, cl=cl: E.activation(
                    smb[:, :], g3[:, cl * 1024:(cl + 1) * 1024], AFT.Square), "ac")
                b.wait("vector", "ac", ac_sq)
                dsum = b.op("vector", lambda E: E.reduce_sum(
                    cols[:, 25:26], smb[:, :], mybir.AxisListType.X), "dv")
                b.wait("scalar", "dv", dsum)
                crms = b.op("scalar", lambda E: E.activation(
                    cols[:, 26:27], cols[:, 25:26], AFT.Sqrt,
                    scale=1.0 / 1024.0), "ac")
                b.wait("vector", "ac", crms)
                cea = b.op("vector", lambda E: E.tensor_scalar(
                    cols[:, 27:28], cols[:, 26:27], EPS_RMS, None,
                    op0=ALU.add), "dv")
                b.wait("vector", "dv", cea)
                crc = b.op("vector", lambda E: E.reciprocal(
                    cols[:, 28:29], cols[:, 27:28]), "dv")
                b.wait("vector", "dv", crc)
                if cl == 0:
                    b.wait("vector", "lr", ldo)
                    if s == 0:
                        b.wait("vector", "dv", dv_pref)
                b.wait("vector", "pe", pe_at)
                cA = b.op("vector", lambda E, cl=cl: E.scalar_tensor_tensor(
                    ktm[:, :], g3[:, cl * 1024:(cl + 1) * 1024], cols[:, 28:29],
                    g5[:, cl * 1024:(cl + 1) * 1024],
                    op0=ALU.mult, op1=ALU.mult), "dv")
                g6b = g6[:, :].bitcast(BF16)
                for hc in range(8):
                    sl = atrot[0] % 4
                    atrot[0] += 1
                    tb = atsl[sl]
                    for g in atg[sl]:
                        b.waitg("tensor", g)
                    if hc == 0:
                        b.wait("tensor", "dv", cA)
                    ctp = b.op("tensor", lambda E, hc=hc, tb=tb: E.transpose(
                        tb, ktm[:, hc * 128:(hc + 1) * 128], ident[:, :]), "pe")
                    pe_at = ctp
                    ahi = g6b[:, 2 * (hc * 512 + cl * 128):
                              2 * (hc * 512 + cl * 128) + 128]
                    alo = g6b[:, 2 * (hc * 512 + cl * 128) + 128:
                              2 * (hc * 512 + cl * 128) + 256]
                    b.wait("scalar", "pe", ctp)
                    caa = b.op("scalar", lambda E, ahi=ahi, tb=tb: E.copy(
                        out=ahi, in_=tb), "ac")
                    b.wait("vector", "ac", caa)
                    ac_af = b.op("vector", lambda E, ahi=ahi, alo=alo, tb=tb:
                                 E.tensor_tensor(alo, tb, ahi,
                                                 op=ALU.subtract), "dv")
                    atg[sl] = [("dv", ac_af)]
            p4g[0] = atg[0][0]
            p4g[1] = atg[1][0]
            bank_dv[5] = max(bank_dv[5], atg[2][0][1])
            bank_dv[6] = max(bank_dv[6], atg[3][0][1])
            dv_cA_last = cA

            for th in range(2):
                sl_cur = load_w12(0, (s, th, 0), s == 0 and th == 0)
                b.wait("scalar", "dv", dv_mfs)
                b.wait("scalar", "pe", pe_m1_reader)
                st_m1 = b.op("scalar", lambda E, th=th: E.dma_start(
                    out=m1r[0:1, :], in_=mfs[0:8, th * 256:(th + 1) * 256]),
                    "sm", 16)
                b.wait("vector", "sm", st_m1)
                dv_m1 = b.op("vector", lambda E: E.tensor_scalar(
                    m1r[0:1, :], m1r[0:1, :], -1.0, None, op0=ALU.add), "dv")
                grp_last = {}
                for tl in range(2):
                    for dh in range(2):
                        pb = tl * 2 + dh
                        ps = pbank[pb]
                        cl = th * 2 + tl
                        b.wait("tensor", "dv", max(dv_fin_bank[pb], dv_corr))
                        b.wait("tensor", "dv", dv_mfs)
                        b.op("tensor", lambda E, ps=ps, cl=cl, dh=dh: E.matmul(
                            ps[:, :], mfs[0:8, cl * 128:(cl + 1) * 128],
                            b2s[0:8, dh * 512:(dh + 1) * 512],
                            start=True, stop=False), "pe")

                for e in range(NE):
                    sl_next = None
                    if e < NE - 1:
                        sl_next = load_w12(e + 1, (s, th, e + 1),
                                           s == 0 and th == 0)
                    wdst = wslot[sl_cur][:, :].bitcast(BF16)
                    g4b = g4[:, :].bitcast(BF16)
                    hbb = hbf[:, :].bitcast(BF16)

                    dv_hj = {}
                    for jc in range(2):
                        pb = 5 + jc
                        b.waitg("tensor", ld_w2[(s, th, e)])
                        b.wait("tensor", "dv", max(bank_dv[pb], dv_m1))
                        b.wait("tensor", "lh", ldxh)
                        b.wait("tensor", "ac", bank_ac[pb])
                        for ic in range(8):
                            whi = wdst[:, ic * 256 + jc * 128:
                                       ic * 256 + (jc + 1) * 128]
                            wlo = wdst[:, 2048 + ic * 256 + jc * 128:
                                       2048 + ic * 256 + (jc + 1) * 128]
                            xhi = g4b[:, ic * 512 + th * 256:
                                      ic * 512 + (th + 1) * 256]
                            xlo = g4b[:, 4096 + ic * 512 + th * 256:
                                      4096 + ic * 512 + (th + 1) * 256]
                            st0 = (ic == 0)
                            b.op("tensor", lambda E, whi=whi, xhi=xhi, jc=jc,
                                 st0=st0: E.matmul(
                                     pbank[5 + jc][:, 0:256], whi, xhi,
                                     start=st0, stop=False), "pe")
                            b.op("tensor", lambda E, whi=whi, xlo=xlo, jc=jc:
                                 E.matmul(pbank[5 + jc][:, 0:256], whi, xlo,
                                          start=False, stop=False), "pe")
                            b.op("tensor", lambda E, wlo=wlo, xhi=xhi, jc=jc:
                                 E.matmul(pbank[5 + jc][:, 0:256], wlo, xhi,
                                          start=False, stop=False), "pe")
                        cg = b.op("tensor", lambda E, jc=jc, th=th, e=e: E.matmul(
                            pbank[5 + jc][:, 0:256], bigr[0:1, 0:128],
                            m1r[0:1, e * 256:(e + 1) * 256],
                            start=False, stop=True), "pe")
                        b.wait("scalar", "pe", max(cg, pe_hbf[0]))
                        if s == 0:
                            b.wait("scalar", "dv", dv_bg_e[e])
                        cge = b.op("scalar", lambda E, jc=jc, e=e:
                                   E.activation(
                                       pbank[5 + jc][:, 256:512],
                                       pbank[5 + jc][:, 0:256], AFT.Gelu,
                                       bias=bgel[:, jc * 8 + e: jc * 8 + e + 1]),
                                   "ac")
                        hhi = hbb[:, jc * 256:(jc + 1) * 256]
                        hlo = hbb[:, 512 + jc * 256: 512 + (jc + 1) * 256]
                        b.wait("scalar", "ac", cge)
                        chc = b.op("scalar", lambda E, jc=jc, hhi=hhi: E.copy(
                            out=hhi, in_=pbank[5 + jc][:, 256:512]), "ac")
                        b.wait("vector", "ac", chc)
                        dv_hj[jc] = b.op("vector", lambda E, jc=jc, hhi=hhi,
                                         hlo=hlo: E.tensor_tensor(
                                             hlo, pbank[5 + jc][:, 256:512], hhi,
                                             op=ALU.subtract), "dv")
                        bank_ac[pb] = chc
                        bank_dv[pb] = dv_hj[jc]

                    cw2 = 0
                    for jc in range(2):
                        b.wait("tensor", "dv", dv_hj[jc])
                        for tl in range(2):
                            for dh in range(2):
                                pb = tl * 2 + dh
                                hhi = hbb[:, jc * 256 + tl * 128:
                                          jc * 256 + (tl + 1) * 128]
                                hlo = hbb[:, 512 + jc * 256 + tl * 128:
                                          512 + jc * 256 + (tl + 1) * 128]
                                whi = wdst[:, 4096 + jc * 1024 + dh * 512:
                                           4096 + jc * 1024 + (dh + 1) * 512]
                                wlo = wdst[:, 6144 + jc * 1024 + dh * 512:
                                           6144 + jc * 1024 + (dh + 1) * 512]
                                b.op("tensor", lambda E, hhi=hhi, whi=whi, pb=pb:
                                     E.matmul(pbank[pb][:, :], hhi, whi,
                                              start=False, stop=False), "pe")
                                b.op("tensor", lambda E, hhi=hhi, wlo=wlo, pb=pb:
                                     E.matmul(pbank[pb][:, :], hhi, wlo,
                                              start=False, stop=False), "pe")
                                cw2 = b.op("tensor", lambda E, hlo=hlo, whi=whi,
                                           pb=pb: E.matmul(
                                               pbank[pb][:, :], hlo, whi,
                                               start=False, stop=False), "pe")
                    pe_hbf[0] = cw2
                    wbuf_free[sl_cur] = ("pe", cw2)
                    if sl_next is not None:
                        sl_cur = sl_next
                pe_m1_reader = b.cnt["pe"]
                if th == 1:
                    pe_g4_free = b.cnt["pe"]
                b.wait("tensor", "dv", max(ac_af, dv_wos))
                g6b = g6[:, :].bitcast(BF16)
                for tl in range(2):
                    for dh in range(2):
                        pb = tl * 2 + dh
                        cl = th * 2 + tl
                        for hc in range(8):
                            ahi = g6b[:, 2 * (hc * 512 + cl * 128):
                                      2 * (hc * 512 + cl * 128) + 128]
                            alo = g6b[:, 2 * (hc * 512 + cl * 128) + 128:
                                      2 * (hc * 512 + cl * 128) + 256]
                            whi = wosb[:, hc * 1024 + dh * 512:
                                       hc * 1024 + (dh + 1) * 512]
                            wlo = wosb[:, 8192 + hc * 1024 + dh * 512:
                                       8192 + hc * 1024 + (dh + 1) * 512]
                            b.op("tensor", lambda E, pb=pb, ahi=ahi, whi=whi:
                                 E.matmul(pbank[pb][:, :], ahi, whi,
                                          start=False, stop=False), "pe")
                            b.op("tensor", lambda E, pb=pb, ahi=ahi, wlo=wlo:
                                 E.matmul(pbank[pb][:, :], ahi, wlo,
                                          start=False, stop=False), "pe")
                            cwo = b.op("tensor", lambda E, pb=pb, alo=alo,
                                       whi=whi, hc=hc: E.matmul(
                                           pbank[pb][:, :], alo, whi,
                                           start=False, stop=(hc == 7)), "pe")
                        grp_last[pb] = cwo
                if th == 1:
                    pe_g6_free = b.cnt["pe"]

                for tl in range(2):
                    cl = th * 2 + tl
                    dv_last = 0
                    for dh in range(2):
                        pb = tl * 2 + dh
                        b.wait("vector", "pe", grp_last[pb])
                        b.wait("vector", "lx", ldx[3])
                        if th == 0 and tl == 0 and dh == 0:
                            b.wait("vector", "st", st_out_prev)
                            if s == 0:
                                b.wait("vector", "ac", ac_b1_end)
                                b.wait("vector", "sx", sx_spill_total)
                        dvf = b.op("vector", lambda E, cl=cl, dh=dh, pb=pb:
                                   E.tensor_tensor(
                                       g2[:, cl * 1024 + dh * 512:
                                          cl * 1024 + (dh + 1) * 512],
                                       g1[:, cl * 1024 + dh * 512:
                                          cl * 1024 + (dh + 1) * 512],
                                       pbank[pb][:, :], op=ALU.add), "dv")
                        dv_fin_bank[pb] = dvf
                        dv_last = dvf
                    dv_g1_done = dv_last
                    b.wait("scalar", "dv", dv_last)
                    st_out = b.op("scalar", lambda E, s=s, cl=cl: E.dma_start(
                        out=out_h[s * 512 + cl * 128: s * 512 + (cl + 1) * 128, :],
                        in_=g2[:, cl * 1024:(cl + 1) * 1024]), "st", 16)

            pe_prev_s_end = b.cnt["pe"]

        b.wait("scalar", "st", b.cnt["st"])
        b.flush()

    return nc


_NC_CACHE = [None]


def kernel(**inputs):
    x = np.ascontiguousarray(np.asarray(inputs["L0_attn_x"]), dtype=np.float32)

    def f32c(a):
        return np.ascontiguousarray(np.asarray(a), dtype=np.float32)

    shared = {
        "wq": f32c(np.asarray(inputs["L0_attn_Wq"]).T),
        "wk": f32c(np.asarray(inputs["L0_attn_Wk"]).T),
        "wv": f32c(np.asarray(inputs["L0_attn_Wv"]).T),
        "wu": f32c(np.asarray(inputs["L0_attn_Wu"]).T),
        "wo": f32c(np.asarray(inputs["L0_attn_Wo"]).T),
        "wg": f32c(np.asarray(inputs["L0_ffn_Wg"]).T),
        "bqp": f32c(np.asarray(inputs["L0_attn_bq"]).reshape(8, 128).T),
        "bkp": f32c(np.asarray(inputs["L0_attn_bk"]).reshape(8, 128).T),
        "bvr": f32c(np.asarray(inputs["L0_attn_bv"]).reshape(1, 1024)),
        "bur": f32c(np.asarray(inputs["L0_attn_bu"]).reshape(1, 1024)),
        "bor": f32c(np.asarray(inputs["L0_attn_bo"]).reshape(1, 1024)),
        "scp": f32c(np.asarray(inputs["L0_attn_scale"]).reshape(8, 128).T),
        "lns": f32c(np.asarray(inputs["L0_ffn_ln_scale"]).T),
        "lnb": f32c(np.asarray(inputs["L0_ffn_ln_bias"]).T),
        "w1": f32c(inputs["L0_ffn_W1"]),
        "b1t": f32c(np.asarray(inputs["L0_ffn_b1"]).T),
        "w2": f32c(inputs["L0_ffn_W2"]),
        "b2": f32c(inputs["L0_ffn_b2"]),
        "ident": np.eye(128, dtype=np.float32),
        "cmask": np.triu(np.ones((128, 128), dtype=np.float32)),
    }

    in_maps = []
    for c in range(NCORES):
        bb, sl = c // 4, c % 4
        m = dict(shared)
        m["x"] = np.ascontiguousarray(x[bb, sl * T:(sl + 1) * T, :])
        m["pmask"] = np.array([[1.0 if r < sl else 0.0
                                for r in range(NCORES)]], dtype=np.float32)
        in_maps.append(m)

    if _NC_CACHE[0] is None:
        _NC_CACHE[0] = build_nc()
    nc = _NC_CACHE[0]

    res = run_bass_kernel_spmd(nc, in_maps, core_ids=list(range(NCORES)))
    out = np.empty((B, N, D), dtype=np.float32)
    for c in range(NCORES):
        bb, sl = c // 4, c % 4
        out[bb, sl * T:(sl + 1) * T, :] = res.results[c]["out"]
    return out


if __name__ == "__main__":
    build_nc()
    print("build ok")


# version 27
# speedup vs baseline: 1.0629x; 1.0184x over previous
import numpy as np
from contextlib import ExitStack

import concourse.bass as bass
import concourse.mybir as mybir
from concourse.bass_utils import run_bass_kernel_spmd

F32 = mybir.dt.float32
BF16 = mybir.dt.bfloat16
AFT = mybir.ActivationFunctionType
ALU = mybir.AluOpType

NCORES = 8
B, N, D = 2, 8192, 1024
T = 2048
S = 512
NS = T // S
C = 128
H = 8
NE = 8
HF = 256
EPS_LN = 1e-5
EPS_RMS = 1e-8
BIG = 1.0e9

ENGINES = ("sync", "tensor", "vector", "scalar", "gpsimd")


class Builder:
    def __init__(self, nc, sems):
        self.nc = nc
        self.sems = sems
        self.cnt = {k: 0 for k in sems}
        self.q = {e: [] for e in ENGINES}

    def wait(self, eng, sem, val):
        if not val or val <= 0:
            return
        h = self.sems[sem]
        self.q[eng].append(lambda E, h=h, v=val: E.wait_ge(h, v))

    def waitg(self, eng, guard):
        if guard:
            self.wait(eng, guard[0], guard[1])

    def op(self, eng, build, sem=None, by=1):
        tgt = None
        if sem is not None:
            self.cnt[sem] += by
            tgt = self.cnt[sem]
            h = self.sems[sem]

            def f(E, build=build, h=h, by=by):
                build(E).then_inc(h, by)
        else:
            def f(E, build=build):
                build(E)
        self.q[eng].append(f)
        return tgt

    def flush(self):
        nc = self.nc
        active = [e for e in ENGINES if self.q[e]]
        with nc.Block() as block:
            for e in active:
                lst = self.q[e]

                def body(E, lst=lst):
                    for f in lst:
                        f(E)

                getattr(block, e)(body)
        self.q = {e: [] for e in ENGINES}


def build_nc():
    nc = bass.Bass("TRN2", target_bir_lowering=False)

    dt = nc.dram_tensor
    x_h = dt("x", [T, D], F32, kind="ExternalInput").ap()
    wq_h = dt("wq", [D, D], F32, kind="ExternalInput").ap()
    wk_h = dt("wk", [D, D], F32, kind="ExternalInput").ap()
    wv_h = dt("wv", [D, D], F32, kind="ExternalInput").ap()
    wu_h = dt("wu", [D, D], F32, kind="ExternalInput").ap()
    wo_h = dt("wo", [D, D], F32, kind="ExternalInput").ap()
    wg_h = dt("wg", [D, NE], F32, kind="ExternalInput").ap()
    bqP_h = dt("bqp", [128, 8], F32, kind="ExternalInput").ap()
    bkP_h = dt("bkp", [128, 8], F32, kind="ExternalInput").ap()
    bv_h = dt("bvr", [1, D], F32, kind="ExternalInput").ap()
    bu_h = dt("bur", [1, D], F32, kind="ExternalInput").ap()
    bo_h = dt("bor", [1, D], F32, kind="ExternalInput").ap()
    scP_h = dt("scp", [128, 8], F32, kind="ExternalInput").ap()
    lns_h = dt("lns", [D, NE], F32, kind="ExternalInput").ap()
    lnb_h = dt("lnb", [D, NE], F32, kind="ExternalInput").ap()
    w1_h = dt("w1", [NE, D, HF], F32, kind="ExternalInput").ap()
    b1T_h = dt("b1t", [HF, NE], F32, kind="ExternalInput").ap()
    w2_h = dt("w2", [NE, HF, D], F32, kind="ExternalInput").ap()
    b2_h = dt("b2", [NE, D], F32, kind="ExternalInput").ap()
    id_h = dt("ident", [128, 128], F32, kind="ExternalInput").ap()
    cm_h = dt("cmask", [128, 128], F32, kind="ExternalInput").ap()
    pm_h = dt("pmask", [1, NCORES], F32, kind="ExternalInput").ap()
    out_h = dt("out", [T, D], F32, kind="ExternalOutput").ap()

    wqb_h = dt("wqbf", [2, 128, 4096], F32).ap()
    wkb_h = dt("wkbf", [2, 128, 4096], F32).ap()
    wvb_h = dt("wvbf", [2, 128, 4096], F32).ap()
    wub_h = dt("wubf", [2, 128, 4096], F32).ap()
    w12b_h = dt("w12bf", [NE, 128, 4096], F32).ap()
    qsp_h = dt("qspill", [NS, 128, H * S], F32).ap()
    osp_h = dt("ospill", [NS, 128, 4 * D], F32).ap()
    usp_h = dt("uspill", [NS, 128, 4 * D], F32).ap()
    xhsp_h = dt("xhspill", [NS, 128, 4096], F32).ap()
    agi_h = dt("agin", [128, H * 128], F32).ap()
    ago_h = dt("agout", [4 * 128, H * 128], F32).ap()

    es = ExitStack()
    with es:
        g1 = es.enter_context(nc.sbuf_tensor("g1", [128, 4096], F32))
        g2 = es.enter_context(nc.sbuf_tensor("g2", [128, 4096], F32))
        g3 = es.enter_context(nc.sbuf_tensor("g3", [128, 4096], F32))
        g4 = es.enter_context(nc.sbuf_tensor("g4", [128, 4096], F32))
        g5 = es.enter_context(nc.sbuf_tensor("g5", [128, 4096], F32))
        g6 = es.enter_context(nc.sbuf_tensor("g6", [128, 4096], F32))
        wba = es.enter_context(nc.sbuf_tensor("wba", [128, 4096], F32))
        wbb = es.enter_context(nc.sbuf_tensor("wbb", [128, 4096], F32))
        wos = es.enter_context(nc.sbuf_tensor("wos", [128, 8192], F32))
        ktm = es.enter_context(nc.sbuf_tensor("ktm", [128, 1024], F32))
        smb = es.enter_context(nc.sbuf_tensor("smb", [128, 1024], F32))
        mst = es.enter_context(nc.sbuf_tensor("mst", [128, 1024], F32))
        tmpa = es.enter_context(nc.sbuf_tensor("tmpa", [128, 512], F32))
        hbf = es.enter_context(nc.sbuf_tensor("hbf", [128, 512], F32))
        wbc = es.enter_context(nc.sbuf_tensor("wbc", [128, 2048], F32))
        wgb = es.enter_context(nc.sbuf_tensor("wgb", [128, 128], BF16))
        logit = es.enter_context(nc.sbuf_tensor("logit", [128, 128], F32))
        ident = es.enter_context(nc.sbuf_tensor("ident_s", [128, 128], F32))
        cmask = es.enter_context(nc.sbuf_tensor("cmask_s", [128, 128], F32))
        mfs = es.enter_context(nc.sbuf_tensor("mfs", [8, 512], F32))
        m1r = es.enter_context(nc.sbuf_tensor("m1r", [1, 2048], F32))
        b2s = es.enter_context(nc.sbuf_tensor("b2s", [8, 1024], F32))
        stat = es.enter_context(nc.sbuf_tensor("stat", [128, 96], F32))
        mvs = es.enter_context(nc.sbuf_tensor("mvs", [128, 16], F32))
        cols = es.enter_context(nc.sbuf_tensor("cols", [128, 32], F32))
        bqp = es.enter_context(nc.sbuf_tensor("bqp_s", [128, 8], F32))
        bkp = es.enter_context(nc.sbuf_tensor("bkp_s", [128, 8], F32))
        scp = es.enter_context(nc.sbuf_tensor("scp_s", [128, 8], F32))
        wgs = es.enter_context(nc.sbuf_tensor("wgs", [128, 64], F32))
        lnsp = es.enter_context(nc.sbuf_tensor("lnsp", [128, 64], F32))
        lnbp = es.enter_context(nc.sbuf_tensor("lnbp", [128, 64], F32))
        b1p = es.enter_context(nc.sbuf_tensor("b1p", [128, 16], F32))
        bgel = es.enter_context(nc.sbuf_tensor("bgel", [128, 16], F32))
        mrow = es.enter_context(nc.sbuf_tensor("mrow", [128, 8], F32))
        pmb = es.enter_context(nc.sbuf_tensor("pmb", [128, 8], F32))
        pmr = es.enter_context(nc.sbuf_tensor("pmr", [1, 8], F32))
        onesr = es.enter_context(nc.sbuf_tensor("onesr", [1, 128], F32))
        zrow = es.enter_context(nc.sbuf_tensor("zrow", [1, 128], F32))
        bigr = es.enter_context(nc.sbuf_tensor("bigr", [1, 128], F32))
        bor = es.enter_context(nc.sbuf_tensor("bor_s", [1, 1024], F32))
        p0 = es.enter_context(nc.psum_tensor("p0", [128, 512], F32))
        p1 = es.enter_context(nc.psum_tensor("p1", [128, 512], F32))
        p2 = es.enter_context(nc.psum_tensor("p2", [128, 512], F32))
        p3 = es.enter_context(nc.psum_tensor("p3", [128, 512], F32))
        p4 = es.enter_context(nc.psum_tensor("p4", [128, 512], F32))
        p5 = es.enter_context(nc.psum_tensor("p5", [128, 512], F32))
        p6 = es.enter_context(nc.psum_tensor("p6", [128, 512], F32))
        p7 = es.enter_context(nc.psum_tensor("p7", [128, 512], F32))
        s_lx = es.enter_context(nc.semaphore("s_lx"))
        s_wa = es.enter_context(nc.semaphore("s_wa"))
        s_wb = es.enter_context(nc.semaphore("s_wb"))
        s_lq = es.enter_context(nc.semaphore("s_lq"))
        s_sm = es.enter_context(nc.semaphore("s_sm"))
        s_sx = es.enter_context(nc.semaphore("s_sx"))
        s_lr = es.enter_context(nc.semaphore("s_lr"))
        s_lh = es.enter_context(nc.semaphore("s_lh"))
        s_sq = es.enter_context(nc.semaphore("s_sq"))
        s_wc = es.enter_context(nc.semaphore("s_wc"))
        s_wd = es.enter_context(nc.semaphore("s_wd"))
        s_st = es.enter_context(nc.semaphore("s_st"))
        s_pe = es.enter_context(nc.semaphore("s_pe"))
        s_dv = es.enter_context(nc.semaphore("s_dv"))
        s_ac = es.enter_context(nc.semaphore("s_ac"))
        s_sp = es.enter_context(nc.semaphore("s_sp"))
        s_cc = es.enter_context(nc.semaphore("s_cc"))
        sems = {"lx": s_lx, "wa": s_wa, "wb": s_wb, "lq": s_lq,
                "sm": s_sm, "sx": s_sx, "lr": s_lr, "lh": s_lh, "sq": s_sq, "wc": s_wc,
                "wd": s_wd, "st": s_st, "pe": s_pe,
                "dv": s_dv, "ac": s_ac, "sp": s_sp, "cc": s_cc}
        b = Builder(nc, sems)
        pbank = [p0, p1, p2, p3, p4, p5, p6, p7]
        tslot = [p4[:, 0:128], p7[:, 0:128]]

        p4g = [None, None]
        p4n = [0]

        for (dst, src) in [
            (ident[:, :], id_h[:, :]), (cmask[:, :], cm_h[:, :]),
            (bqp[:, :], bqP_h[:, :]), (bkp[:, :], bkP_h[:, :]),
            (scp[:, :], scP_h[:, :]), (b2s[:, :], b2_h[:, :]),
            (m1r[0:1, 0:1024], bv_h[:, :]), (m1r[0:1, 1024:2048], bu_h[:, :]),
            (bor[:, :], bo_h[:, :]), (pmr[:, :], pm_h[:, :]),
        ]:
            b.op("sync", lambda E, d=dst, s=src: E.dma_start(out=d, in_=s), "lq", 16)
        for ic in range(8):
            b.op("scalar", lambda E, ic=ic: E.dma_start(
                out=wgs[:, ic * 8:(ic + 1) * 8],
                in_=wg_h[ic * 128:(ic + 1) * 128, :]), "sm", 16)
            b.op("scalar", lambda E, ic=ic: E.dma_start(
                out=lnsp[:, ic * 8:(ic + 1) * 8],
                in_=lns_h[ic * 128:(ic + 1) * 128, :]), "sm", 16)
            b.op("scalar", lambda E, ic=ic: E.dma_start(
                out=lnbp[:, ic * 8:(ic + 1) * 8],
                in_=lnb_h[ic * 128:(ic + 1) * 128, :]), "sm", 16)
        for jc in range(2):
            b.op("scalar", lambda E, jc=jc: E.dma_start(
                out=b1p[:, jc * 8:(jc + 1) * 8],
                in_=b1T_h[jc * 128:(jc + 1) * 128, :]), "sm", 16)
        sm_consts = b.cnt["sm"]
        ld_consts = b.cnt["lq"]
        b.wait("sync", "lq", ld_consts)

        b.op("vector", lambda E: E.memset(mst[:, :], 0.0))
        b.op("vector", lambda E: E.memset(onesr[:, :], 1.0))
        b.op("vector", lambda E: E.memset(zrow[:, :], 0.0))
        dv_ones = b.op("vector", lambda E: E.memset(bigr[:, :], BIG), "dv")
        b.wait("tensor", "lq", ld_consts)
        b.wait("tensor", "dv", dv_ones)
        cbc = b.op("tensor", lambda E: E.matmul(
            p7[:, 0:8], onesr[0:1, 0:128], pmr[0:1, 0:8],
            start=True, stop=True), "pe")
        b.wait("vector", "pe", cbc)
        dv_pmb = b.op("vector", lambda E: E.tensor_copy(
            out=pmb[:, :], in_=p7[:, 0:8]), "dv")
        dv_b0 = dv_pmb
        for oh in range(2):
            b.wait("tensor", "dv", dv_b0)
            cb2 = b.op("tensor", lambda E, oh=oh: E.matmul(
                p7[0:8, :], onesr[0:1, 0:8], bor[0:1, oh * 512:(oh + 1) * 512],
                start=True, stop=True), "pe")
            b.wait("vector", "pe", cb2)
            dv_b0 = b.op("vector", lambda E, oh=oh: E.tensor_tensor(
                b2s[0:8, oh * 512:(oh + 1) * 512],
                b2s[0:8, oh * 512:(oh + 1) * 512],
                p7[0:8, :], op=ALU.add), "dv")
        b.flush()

        def ln_tile(i, src, dst, pre=()):
            st6 = stat[:, i * 12:(i + 1) * 12]
            for g in pre:
                b.waitg("vector", g)
            b.op("vector", lambda E, i=i, st6=st6, src=src: E.bn_stats(
                st6[:, 0:6], src[:, 0:512]))
            cst = b.op("vector", lambda E, i=i, st6=st6, src=src: E.bn_stats(
                st6[:, 6:12], src[:, 512:1024]), "dv")
            b.wait("vector", "dv", cst)
            cag = b.op("vector", lambda E, i=i, st6=st6: E.bn_aggr(
                mvs[:, 2 * i:2 * i + 2], st6[:, 0:12]), "dv")
            b.wait("vector", "dv", cag)
            c = b.op("vector", lambda E, i=i: E.tensor_scalar(
                cols[:, i:i + 1], mvs[:, 2 * i + 1:2 * i + 2], EPS_LN, None,
                op0=ALU.add), "dv")
            b.wait("scalar", "dv", c)
            c2 = b.op("scalar", lambda E, i=i: E.activation(
                cols[:, 8 + i:9 + i], cols[:, i:i + 1], AFT.Sqrt), "ac")
            b.wait("vector", "ac", c2)
            crc = b.op("vector", lambda E, i=i: E.reciprocal(
                cols[:, 16 + i:17 + i], cols[:, 8 + i:9 + i]), "dv")
            b.wait("vector", "dv", crc)
            c = b.op("vector", lambda E, i=i, src=src, dst=dst: E.tensor_scalar(
                dst[:, :], src[:, :], mvs[:, 2 * i:2 * i + 1],
                cols[:, 16 + i:17 + i], op0=ALU.subtract, op1=ALU.mult), "dv")
            return c

        def ln_pass(src, dst, pre_dv_guard=None):
            outs = []
            for i in range(4):
                pre = (pre_dv_guard,) if (i == 0 and pre_dv_guard) else ()
                outs.append(ln_tile(i, src(i), dst(i), pre))
            return outs

        def transpose_pass(src, dst, lnc, writer_pre, xslots=()):
            last = 0
            first = True
            dstb = dst[:, :].bitcast(BF16)
            slots = [tslot[0], tslot[1]] + [x[0] for x in xslots]
            gl = [p4g[0], p4g[1]] + [x[1] for x in xslots]
            nsl = len(slots)
            rot = 0
            for i in range(4):
                for ic in range(8):
                    pre = [("dv", lnc[i])] if ic == 0 else []
                    sl = rot % nsl
                    rot += 1
                    tb = slots[sl]
                    b.waitg("tensor", gl[sl])
                    for g in pre:
                        b.waitg("tensor", g)
                    cpe = b.op("tensor", lambda E, i=i, ic=ic, tb=tb: E.transpose(
                        tb, src(i)[:, ic * 128:(ic + 1) * 128], ident[:, :]), "pe")
                    if first:
                        for g in writer_pre:
                            b.waitg("scalar", g)
                            b.waitg("vector", g)
                        first = False
                    hi = dstb[:, ic * 512 + i * 128: ic * 512 + (i + 1) * 128]
                    lo = dstb[:, 4096 + ic * 512 + i * 128:
                              4096 + ic * 512 + (i + 1) * 128]
                    b.wait("scalar", "pe", cpe)
                    cac = b.op("scalar", lambda E, hi=hi, tb=tb: E.copy(
                        out=hi, in_=tb), "ac")
                    b.wait("vector", "ac", cac)
                    last = b.op("vector", lambda E, hi=hi, lo=lo, tb=tb:
                                E.tensor_tensor(lo, tb, hi, op=ALU.subtract), "dv")
                    gl[sl] = ("dv", last)
            p4g[0] = gl[0]
            p4g[1] = gl[1]
            return last

        WHALF = {"q": wq_h, "k": wk_h, "v": wv_h, "u": wu_h}
        wslot = [wba, wbb]

        st_qsp = [0] * NS
        st_osp = [0] * NS
        st_usp = [0] * NS
        st_xhsp = [0] * NS
        pe_xnf_reader = 0
        pe_xw_reader = 0
        wload_seq = [0]
        wbuf_free = [None, None]
        ld_w = {}

        wbc_free = [0, 0]
        wbc_par = [0]

        def land_split(pieces, dsts, scale_cols=None):
            last = 0
            for (dmas, hi_ap, lo_ap, scol) in pieces:
                hp = wbc_par[0] % 2
                wbc_par[0] += 1
                base = hp * 1024
                wsem = "wc" if hp == 0 else "wd"
                b.wait("sync", "dv", wbc_free[hp])
                cc = 0
                for ent in dmas:
                    if ent[0] == "ap3":
                        cc = b.op("sync", lambda E, soff=ent[1], sten=ent[2],
                                  base=base: E.dma_start(
                                      out=bass.AP(wbc, base,
                                                  [[2048, 128], [512, 2],
                                                   [1, 512]]),
                                      in_=bass.AP(sten, soff,
                                                  [[1024, 128], [131072, 2],
                                                   [1, 512]])), wsem, 16)
                    else:
                        (src_ap, c0, nc_) = ent
                        cc = b.op("sync", lambda E, src_ap=src_ap, c0=c0,
                                  nc_=nc_, base=base: E.dma_start(
                                      out=wbc[:, base + c0:base + c0 + nc_],
                                      in_=src_ap), wsem, 16)
                n = sum(1024 if x[0] == "ap3" else x[2] for x in dmas)
                b.wait("scalar", wsem, cc)
                if scol is not None:
                    cac = b.op("scalar", lambda E, hi_ap=hi_ap, n=n, scol=scol,
                               base=base:
                               E.activation(hi_ap, wbc[:, base:base + n],
                                            AFT.Copy, scale=scol), "ac")
                    b.wait("vector", "ac", cac)
                    last = b.op("vector", lambda E, hi_ap=hi_ap, lo_ap=lo_ap,
                                n=n, scol=scol, base=base:
                                E.scalar_tensor_tensor(
                                    lo_ap, wbc[:, base:base + n], scol, hi_ap,
                                    op0=ALU.mult, op1=ALU.subtract), "dv")
                else:
                    cac = b.op("scalar", lambda E, hi_ap=hi_ap, n=n, base=base:
                               E.copy(out=hi_ap, in_=wbc[:, base:base + n]),
                               "ac")
                    b.wait("vector", "ac", cac)
                    last = b.op("vector", lambda E, hi_ap=hi_ap, lo_ap=lo_ap,
                                n=n, base=base:
                                E.tensor_tensor(lo_ap, wbc[:, base:base + n],
                                                hi_ap, op=ALU.subtract), "dv")
                wbc_free[hp] = last
            return ("dv", last)

        wst = {}
        slot_store = [0, 0]
        WBF = {"q": wqb_h, "k": wkb_h, "v": wvb_h, "u": wub_h}

        def load_whalf(which, half, tag):
            sl = wload_seq[0] % 2
            wload_seq[0] += 1
            dst = wslot[sl][:, :].bitcast(BF16)
            if tag[0] == 0 and which == "q":
                stage = g5 if half == 0 else g6
                wsem = "wa" if half == 0 else "wb"
                cc = 0
                for p in range(4):
                    cc = b.op("sync", lambda E, p=p, half=half, stage=stage:
                              E.dma_start(
                                  out=bass.AP(stage, p * 1024,
                                              [[4096, 128], [512, 2],
                                               [1, 512]]),
                                  in_=bass.AP(WHALF["q"].tensor,
                                              (2 * p) * 128 * 1024 + half * 512,
                                              [[1024, 128], [131072, 2],
                                               [1, 512]])), wsem, 16)
                last = 0
                for p in range(4):
                    hi_ap = dst[:, p * 1024:(p + 1) * 1024]
                    lo_ap = dst[:, 4096 + p * 1024: 4096 + (p + 1) * 1024]
                    b.wait("scalar", wsem, cc)
                    cac = b.op("scalar", lambda E, hi_ap=hi_ap, p=p,
                               stage=stage: E.copy(
                                   out=hi_ap,
                                   in_=stage[:, p * 1024:(p + 1) * 1024]),
                               "ac")
                    b.wait("vector", "ac", cac)
                    last = b.op("vector", lambda E, hi_ap=hi_ap, lo_ap=lo_ap,
                                p=p, stage=stage: E.tensor_tensor(
                                    lo_ap, stage[:, p * 1024:(p + 1) * 1024],
                                    hi_ap, op=ALU.subtract), "dv")
                ld_w[tag] = ("dv", last)
                b.wait("scalar", "dv", last)
                cst = b.op("scalar", lambda E, half=half, sl=sl:
                           E.dma_start(out=WBF["q"][half],
                                       in_=wslot[sl][:, :]), "lq", 16)
                wst[("q", half)] = cst
                slot_store[sl] = cst
                return sl
            if tag[0] == 0:
                b.waitg("scalar", wbuf_free[sl])
                b.waitg("vector", wbuf_free[sl])
                stq = b.cnt["lq"] if slot_store[sl] else 0
                b.wait("scalar", "lq", stq)
                b.wait("vector", "lq", stq)
                pieces = []
                for p in range(4):
                    dmas = [("ap3", (2 * p) * 128 * 1024 + half * 512,
                             WHALF[which].tensor)]
                    hi_ap = dst[:, p * 1024:(p + 1) * 1024]
                    lo_ap = dst[:, 4096 + p * 1024: 4096 + (p + 1) * 1024]
                    pieces.append((dmas, hi_ap, lo_ap, None))
                ld_w[tag] = land_split(pieces, None)
                b.wait("scalar", "dv", ld_w[tag][1])
                cst = b.op("scalar", lambda E, which=which, half=half, sl=sl:
                           E.dma_start(out=WBF[which][half],
                                       in_=wslot[sl][:, :]), "lq", 16)
                wst[(which, half)] = cst
                slot_store[sl] = cst
            else:
                b.waitg("sync", wbuf_free[sl])
                b.wait("sync", "lq", b.cnt["lq"])
                wsem = "wa" if sl == 0 else "wb"
                cc = b.op("sync", lambda E, which=which, half=half, sl=sl:
                          E.dma_start(out=wslot[sl][:, :],
                                      in_=WBF[which][half]), wsem, 16)
                ld_w[tag] = (wsem, cc)
            return sl

        pgrp = [0]
        bank_dv = {5: 0, 6: 0}
        bank_ac = {5: 0, 6: 0}
        exp_guard = [0]
        elu_prev = [0]

        def fm_half(dst, biasP, oh, sl, ldtag):
            last_dv = 0
            last_pe = 0
            if True:
                for ocl in range(4):
                    oc = oh * 4 + ocl
                    pb = 5 + (pgrp[0] % 2)
                    pgrp[0] += 1
                    ps = pbank[pb]
                    b.waitg("tensor", ld_w[ldtag])
                    b.wait("tensor", "dv", bank_dv[pb])
                    b.wait("tensor", "ac", bank_ac[pb])
                    cpe = 0
                    wb_ = wslot[sl][:, :].bitcast(BF16)
                    g2b = g2[:, :].bitcast(BF16)
                    for ic in range(8):
                        whi = wb_[:, ic * 512 + ocl * 128: ic * 512 + (ocl + 1) * 128]
                        wlo = wb_[:, 4096 + ic * 512 + ocl * 128:
                                  4096 + ic * 512 + (ocl + 1) * 128]
                        xhi = g2b[:, ic * 512:(ic + 1) * 512]
                        xlo = g2b[:, 4096 + ic * 512: 4096 + (ic + 1) * 512]
                        st0 = (ic == 0)
                        b.op("tensor", lambda E, whi=whi, xhi=xhi, ps=ps, st0=st0:
                             E.matmul(ps[:, :], whi, xhi,
                                      start=st0, stop=False), "pe")
                        b.op("tensor", lambda E, whi=whi, xlo=xlo, ps=ps:
                             E.matmul(ps[:, :], whi, xlo,
                                      start=False, stop=False), "pe")
                        cpe = b.op("tensor", lambda E, wlo=wlo, xhi=xhi, ps=ps,
                                   ic=ic: E.matmul(ps[:, :], wlo, xhi,
                                                   start=False, stop=(ic == 7)),
                                   "pe")
                    last_pe = cpe
                    ta = tmpa[:, 0:512]
                    dsl = dst[:, oc * 512:(oc + 1) * 512]
                    b.wait("vector", "pe", cpe)
                    b.wait("vector", "ac", exp_guard[0])
                    b.wait("vector", "dv", elu_prev[0])
                    b.op("vector", lambda E, ps=ps, oc=oc, ta=ta, biasP=biasP:
                         E.tensor_scalar(ta, ps[:, :], biasP[:, oc:oc + 1], 0.0,
                                         op0=ALU.add, op1=ALU.min), "dv")
                    cdv = b.op("vector", lambda E, ps=ps, oc=oc, dsl=dsl, biasP=biasP:
                               E.tensor_scalar(dsl, ps[:, :], biasP[:, oc:oc + 1], 0.0,
                                               op0=ALU.add, op1=ALU.max), "dv")
                    bank_dv[pb] = cdv
                    b.wait("scalar", "dv", cdv)
                    cac = b.op("scalar", lambda E, ta=ta:
                               E.activation(ta, ta, AFT.Exp), "ac")
                    exp_guard[0] = cac
                    b.wait("vector", "ac", cac)
                    last_dv = b.op("vector", lambda E, ta=ta, dsl=dsl:
                                   E.scalar_tensor_tensor(
                                       dsl, ta, -1.0, dsl,
                                       op0=ALU.add, op1=ALU.add), "dv")
                    elu_prev[0] = last_dv
            return last_dv, last_pe

        dv_msk = {}
        st_cpy = {}
        pe_oint = {}
        pe_smm = {}
        dv_logit = 0
        ac_oevac_prev = 0
        for stb in (p2, p3):
            b.op("tensor", lambda E, stb=stb: E.matmul(
                stb[:, :], zrow[0:1, 0:128], bor[0:1, 0:512],
                start=True, stop=True), "pe")

        c = 0
        for i in range(4):
            c = b.op("gpsimd", lambda E, i=i: E.dma_start(
                out=g1[:, i * 1024:(i + 1) * 1024],
                in_=x_h[i * 128:(i + 1) * 128, :]), "lx", 16)
        lnc_cur = [ln_tile(4 + i, g1[:, i * 1024:(i + 1) * 1024],
                           g1[:, i * 1024:(i + 1) * 1024],
                           pre=(("lx", c),)) for i in range(4)]

        slq0_next = load_whalf("q", 0, (0, "q", 0))
        slq1_next = load_whalf("q", 1, (0, "q", 1))
        b.wait("scalar", "sm", sm_consts)
        cwg = b.op("scalar", lambda E: E.copy(out=wgb[:, 0:64], in_=wgs[:, :]), "ac")
        b.wait("vector", "ac", cwg)
        dv_wg = b.op("vector", lambda E: E.tensor_tensor(
            wgb[:, 64:128], wgs[:, :], wgb[:, 0:64], op=ALU.subtract), "dv")
        for s in range(NS):
            slq0 = slq0_next
            slq1 = slq1_next

            og = ("ac", ac_oevac_prev) if s > 0 else None
            ac_xnf = transpose_pass(lambda i: g1[:, i * 1024:(i + 1) * 1024],
                                    g2, lnc_cur, [("pe", pe_xnf_reader),
                                                  ("sx", st_xhsp[s - 1] if s > 0 else 0)],
                                    xslots=[(p0[:, 0:128], og),
                                            (p1[:, 0:128], og)])
            pe_xw_reader = b.cnt["pe"]

            if s > 0:
                b.wait("vector", "st", st_osp[s - 1])
            b.wait("tensor", "dv", ac_xnf)

            def tm_half(oh, sl, ldtag, dst, brow_off):
                last_ac = 0
                cpe = 0
                wb_ = wslot[sl][:, :].bitcast(BF16)
                g2b = g2[:, :].bitcast(BF16)
                for sc in range(4):
                    pb = 5 + (pgrp[0] % 2)
                    pgrp[0] += 1
                    ps = pbank[pb]
                    b.waitg("tensor", ld_w[ldtag])
                    b.wait("tensor", "dv", bank_dv[pb])
                    b.wait("tensor", "ac", bank_ac[pb])
                    b.op("tensor", lambda E, ps=ps, oh=oh, brow_off=brow_off:
                         E.matmul(ps[:, :], onesr[0:1, 0:128],
                                  m1r[0:1, brow_off + oh * 512:
                                      brow_off + (oh + 1) * 512],
                                  start=True, stop=False), "pe")
                    for ic in range(8):
                        xhi = g2b[:, ic * 512 + sc * 128: ic * 512 + (sc + 1) * 128]
                        xlo = g2b[:, 4096 + ic * 512 + sc * 128:
                                  4096 + ic * 512 + (sc + 1) * 128]
                        whi = wb_[:, ic * 512:(ic + 1) * 512]
                        wlo = wb_[:, 4096 + ic * 512: 4096 + (ic + 1) * 512]
                        b.op("tensor", lambda E, xhi=xhi, whi=whi, ps=ps:
                             E.matmul(ps[:, :], xhi, whi,
                                      start=False, stop=False), "pe")
                        b.op("tensor", lambda E, xhi=xhi, wlo=wlo, ps=ps:
                             E.matmul(ps[:, :], xhi, wlo,
                                      start=False, stop=False), "pe")
                        cpe = b.op("tensor", lambda E, xlo=xlo, whi=whi, ps=ps,
                                   ic=ic: E.matmul(ps[:, :], xlo, whi,
                                                   start=False, stop=(ic == 7)),
                                   "pe")
                    dsl = dst[:, sc * 1024 + oh * 512: sc * 1024 + (oh + 1) * 512]
                    b.wait("scalar", "pe", cpe)
                    last_ac = b.op("scalar", lambda E, dsl=dsl, ps=ps:
                                   E.activation(dsl, ps[:, :], AFT.Silu), "ac")
                    bank_ac[pb] = last_ac
                    bank_dv[pb] = 0
                return cpe, last_ac

            def v_half(oh, sl, ldtag):
                return tm_half(oh, sl, ldtag, g5, 0)

            dv_q0, pe_q0 = fm_half(g3, bqp, 0, slq0, (s, "q", 0))
            wbuf_free[slq0] = ("pe", pe_q0)
            slk0 = load_whalf("k", 0, (s, "k", 0))
            dv_qf, pe_q1 = fm_half(g3, bqp, 1, slq1, (s, "q", 1))
            wbuf_free[slq1] = ("pe", pe_q1)
            slk1 = load_whalf("k", 1, (s, "k", 1))
            dv_k0, pe_k0 = fm_half(g4, bkp, 0, slk0, (s, "k", 0))
            wbuf_free[slk0] = ("pe", pe_k0)
            slv0 = load_whalf("v", 0, (s, "v", 0))
            dv_kf, pe_k1 = fm_half(g4, bkp, 1, slk1, (s, "k", 1))
            wbuf_free[slk1] = ("pe", pe_k1)
            slv1 = load_whalf("v", 1, (s, "v", 1))
            pe_v0, ac_v0 = v_half(0, slv0, (s, "v", 0))
            wbuf_free[slv0] = ("pe", pe_v0)
            slu0 = load_whalf("u", 0, (s, "u", 0))
            pe_v1, last_ac_v = v_half(1, slv1, (s, "v", 1))
            wbuf_free[slv1] = ("pe", pe_v1)
            slu1 = load_whalf("u", 1, (s, "u", 1))
            dv_qf = max(dv_qf, dv_q0)
            dv_kf = max(dv_kf, dv_k0)

            def u_half(oh, sl, ldtag, first):
                if first:
                    b.wait("scalar", "st", st_osp[s - 1] if s > 0 else 0)
                return tm_half(oh, sl, ldtag, g6, 1024)

            pe_u0, ac_u0 = u_half(0, slu0, (s, "u", 0), True)
            wbuf_free[slu0] = ("pe", pe_u0)
            pe_u1, ac_u1 = u_half(1, slu1, (s, "u", 1), False)
            wbuf_free[slu1] = ("pe", pe_u1)
            b.wait("scalar", "ac", ac_u1)
            st_usp[s] = b.op("scalar", lambda E, s=s: E.dma_start(
                out=usp_h[s, :, :], in_=g6[:, :]), "st", 16)

            for cl in range(4):
                gc = 4 * s + cl
                pb = 5 + (pgrp[0] % 2)
                pgrp[0] += 1
                ps = pbank[pb]
                if s == 0 and cl == 0:
                    b.wait("tensor", "dv", dv_wg)
                b.wait("tensor", "dv", bank_dv[pb])
                b.wait("tensor", "ac", bank_ac[pb])
                cpe = 0
                g2b = g2[:, :].bitcast(BF16)
                for ic in range(8):
                    xhi = g2b[:, ic * 512 + cl * 128: ic * 512 + (cl + 1) * 128]
                    xlo = g2b[:, 4096 + ic * 512 + cl * 128:
                              4096 + ic * 512 + (cl + 1) * 128]
                    whi = wgb[:, ic * 8:(ic + 1) * 8]
                    wlo = wgb[:, 64 + ic * 8: 64 + (ic + 1) * 8]
                    st0 = (ic == 0)
                    b.op("tensor", lambda E, xhi=xhi, whi=whi, ps=ps, st0=st0:
                         E.matmul(ps[:, 0:8], xhi, whi, start=st0,
                                  stop=False), "pe")
                    b.op("tensor", lambda E, xhi=xhi, wlo=wlo, ps=ps: E.matmul(
                        ps[:, 0:8], xhi, wlo, start=False, stop=False), "pe")
                    cpe = b.op("tensor", lambda E, xlo=xlo, whi=whi, ps=ps, ic=ic:
                               E.matmul(ps[:, 0:8], xlo, whi,
                                        start=False, stop=(ic == 7)), "pe")
                b.wait("vector", "pe", cpe)
                dv_logit = b.op("vector", lambda E, gc=gc, ps=ps: E.tensor_copy(
                    out=logit[:, gc * 8:(gc + 1) * 8], in_=ps[:, 0:8]), "dv")
                bank_dv[pb] = dv_logit
            pe_g2_free = b.cnt["pe"]

            lnc2 = [ln_tile(0, g1[:, 0:1024], g1[:, 0:1024],
                            pre=(("pe", pe_xw_reader),
                                 ("dv", lnc_cur[0])))]

            if s > 0:
                b.wait("scalar", "st", st_osp[s - 1])
            ac_oevac = ac_oevac_prev
            pairs = [(cl, h) for cl in range(4) for h in range(8)]

            def emit_TS(j):
                cl, h = pairs[j]
                gc = 4 * s + cl
                sbank = p5 if h % 2 == 0 else p6
                pre = [("dv", dv_kf)] if j == 0 else []
                sl = p4n[0] % 2
                p4n[0] += 1
                tb = tslot[sl]
                b.waitg("tensor", p4g[sl])
                for g in pre:
                    b.waitg("tensor", g)
                ct = b.op("tensor", lambda E, h=h, cl=cl, tb=tb: E.transpose(
                    tb, g4[:, h * 512 + cl * 128: h * 512 + (cl + 1) * 128],
                    ident[:, :]), "pe")
                b.wait("scalar", "pe", max(ct, pe_smm.get((gc - 1, h), 0)))
                cke = b.op("scalar", lambda E, h=h, tb=tb: E.copy(
                    out=ktm[:, h * 128:(h + 1) * 128], in_=tb), "ac")
                p4g[sl] = ("ac", cke)
                ktm_ev[(gc, h)] = cke
                if j >= 2:
                    pcl, ph = pairs[j - 2]
                    b.wait("tensor", "dv", dv_msk.get((4 * s + pcl, ph), 0))
                else:
                    b.wait("tensor", "dv", dv_msk.get((4 * s - 1, 6 + j), 0))
                    pb = 5 + (h % 2)
                    b.wait("tensor", "dv", bank_dv[pb])
                    b.wait("tensor", "ac", bank_ac[pb])
                if j == 0:
                    b.wait("tensor", "dv", max(dv_qf, dv_kf))
                    b.wait("tensor", "ac", last_ac_v)
                cs = b.op("tensor", lambda E, h=h, cl=cl, sbank=sbank: E.matmul(
                    sbank[:, 0:128],
                    g4[:, h * 512 + cl * 128: h * 512 + (cl + 1) * 128],
                    g3[:, h * 512 + cl * 128: h * 512 + (cl + 1) * 128],
                    start=True, stop=True), "pe")
                b.wait("vector", "pe", max(cs, pe_oint.get((gc - 1, h), 0)))
                cm_ = b.op("vector", lambda E, h=h, sbank=sbank: E.tensor_tensor(
                    smb[:, h * 128:(h + 1) * 128], cmask[:, :], sbank[:, 0:128],
                    op=ALU.mult), "dv")
                dv_msk[(gc, h)] = cm_

            ktm_ev = {}
            lnc_next = []
            gate_hist = []
            emit_TS(0)
            emit_TS(1)
            for j, (cl, h) in enumerate(pairs):
                gc = 4 * s + cl
                if j + 2 < len(pairs):
                    emit_TS(j + 2)
                po = p0 if h < 4 else p1
                off = (h % 4) * 128
                b.wait("tensor", "dv", dv_msk[(gc, h)])
                if h == 0:
                    b.wait("tensor", "ac", ac_oevac)
                b.op("tensor", lambda E, h=h, cl=cl, po=po, off=off: E.matmul(
                    po[:, off:off + 128], smb[:, h * 128:(h + 1) * 128],
                    g5[:, cl * 1024 + h * 128: cl * 1024 + (h + 1) * 128],
                    start=True, stop=False), "pe")
                b.wait("tensor", "dv", st_cpy.get(gc - 1, 0))
                coi = b.op("tensor", lambda E, h=h, cl=cl, po=po, off=off:
                           E.matmul(
                               po[:, off:off + 128],
                               g3[:, h * 512 + cl * 128: h * 512 + (cl + 1) * 128],
                               mst[:, h * 128:(h + 1) * 128],
                               start=False, stop=True), "pe")
                pe_oint[(gc, h)] = coi
                stb = p2 if h < 4 else p3
                b.wait("tensor", "ac", ktm_ev[(gc, h)])
                if h == 0:
                    b.wait("tensor", "dv", st_cpy.get(gc - 1, 0))
                csm = b.op("tensor", lambda E, h=h, cl=cl, stb=stb, off=off:
                           E.matmul(
                    stb[:, off:off + 128], ktm[:, h * 128:(h + 1) * 128],
                    g5[:, cl * 1024 + h * 128: cl * 1024 + (h + 1) * 128],
                    start=False, stop=False, skip_group_check=True), "pe")
                pe_smm[(gc, h)] = csm
                ic = h
                sl = p4n[0] % 2
                p4n[0] += 1
                tb = tslot[sl]
                b.waitg("tensor", p4g[sl])
                if ic == 0:
                    b.wait("tensor", "dv", lnc2[cl])
                ctp_xh_last = b.op("tensor", lambda E, ic=ic, cl=cl, tb=tb:
                                   E.transpose(
                                       tb, g1[:, cl * 1024 + ic * 128:
                                              cl * 1024 + (ic + 1) * 128],
                                       ident[:, :]), "pe")
                if ic == 0 and cl == 0:
                    b.wait("scalar", "pe", pe_g2_free)
                    b.wait("vector", "pe", pe_g2_free)
                g2b_ = g2[:, :].bitcast(BF16)
                xh_hi = g2b_[:, ic * 512 + cl * 128: ic * 512 + (cl + 1) * 128]
                xh_lo = g2b_[:, 4096 + ic * 512 + cl * 128:
                             4096 + ic * 512 + (cl + 1) * 128]
                b.wait("scalar", "pe", ctp_xh_last)
                cxa = b.op("scalar", lambda E, xh_hi=xh_hi, tb=tb:
                           E.copy(out=xh_hi, in_=tb), "ac")
                b.wait("vector", "ac", cxa)
                ac_ev_xh = b.op("vector", lambda E, xh_hi=xh_hi, xh_lo=xh_lo,
                                tb=tb: E.tensor_tensor(
                                    xh_lo, tb, xh_hi, op=ALU.subtract), "dv")
                p4g[sl] = ("dv", ac_ev_xh)

                if h == 7:
                    if cl == 0:
                        b.wait("scalar", "st", st_usp[s])
                    b.wait("vector", "pe", max(csm, pe_oint[(gc, 7)]))
                    b.op("vector", lambda E: E.tensor_copy(
                        out=mst[:, 0:512], in_=p2[:, :]), "dv")
                    st_cpy[gc] = b.op("vector", lambda E: E.tensor_copy(
                        out=mst[:, 512:1024], in_=p3[:, :]), "dv")
                    b.wait("scalar", "pe", pe_oint[(gc, 7)])
                    b.op("scalar", lambda E, cl=cl: E.copy(
                        out=g6[:, cl * 1024: cl * 1024 + 512], in_=p0[:, :]), "ac")
                    ac_oevac = b.op("scalar", lambda E, cl=cl: E.copy(
                        out=g6[:, cl * 1024 + 512: (cl + 1) * 1024],
                        in_=p1[:, :]), "ac")
                    b.wait("scalar", "dv", dv_qf)
                    st_qsp[s] = b.op("scalar", lambda E, s=s, cl=cl: E.dma_start(
                        out=bass.AP(qsp_h.tensor, s * 128 * 4096 + cl * 128,
                                    [[4096, 128], [512, 8], [1, 128]]),
                        in_=bass.AP(g3, cl * 128,
                                    [[4096, 128], [512, 8], [1, 128]])),
                        "st", 16)
                    b.wait("scalar", "ac", ac_oevac)
                    st_osp[s] = b.op("scalar", lambda E, s=s, cl=cl: E.dma_start(
                        out=osp_h[s, :, cl * 1024:(cl + 1) * 1024],
                        in_=g6[:, cl * 1024:(cl + 1) * 1024]), "st", 16)
                    if s < NS - 1 and cl == 1:
                        slq0_next = load_whalf("q", 0, (s + 1, "q", 0))
                    if s < NS - 1 and cl == 2:
                        slq1_next = load_whalf("q", 1, (s + 1, "q", 1))
                    if s < NS - 1:
                        b.wait("gpsimd", "pe", ctp_xh_last)
                        ldn = b.op("gpsimd", lambda E, s=s, cl=cl: E.dma_start(
                            out=g1[:, cl * 1024:(cl + 1) * 1024],
                            in_=x_h[(s + 1) * 512 + cl * 128:
                                    (s + 1) * 512 + (cl + 1) * 128, :]), "lx", 16)
                        b.wait("gpsimd", "lx", ldn)
                        cgate = b.op("gpsimd", lambda E: E.memset(
                            cols[0:1, 29:30], 0.0), "cc")
                        gate_hist.append(cgate)
                        if cl >= 1:
                            tj = cl - 1
                            lnc_next.append(ln_tile(
                                4 + tj, g1[:, tj * 1024:(tj + 1) * 1024],
                                g1[:, tj * 1024:(tj + 1) * 1024],
                                pre=(("cc", gate_hist[tj]),)))
                    b.wait("scalar", "dv", ac_ev_xh)
                    st_xhsp[s] = b.op("scalar", lambda E, s=s, cl=cl: E.dma_start(
                        out=bass.AP(xhsp_h.tensor, s * 128 * 4096 + cl * 64,
                                    [[4096, 128], [2048, 2], [256, 8],
                                     [1, 64]]),
                        in_=bass.AP(g2, cl * 64,
                                    [[4096, 128], [2048, 2], [256, 8],
                                     [1, 64]])),
                        "sx", 16)
                    if cl < 3:
                        lnc2.append(ln_tile(
                            cl + 1, g1[:, (cl + 1) * 1024:(cl + 2) * 1024],
                            g1[:, (cl + 1) * 1024:(cl + 2) * 1024],
                            pre=(("dv", lnc_cur[cl + 1]),)))
            ac_oevac_prev = ac_oevac

            if s < NS - 1:
                lnc_next.append(ln_tile(
                    4 + 3, g1[:, 3 * 1024:4 * 1024],
                    g1[:, 3 * 1024:4 * 1024],
                    pre=(("cc", gate_hist[3]),)))
                lnc_cur = lnc_next
            bank_dv[5] = max(bank_dv[5], dv_msk[(4 * s + 3, 6)])
            bank_dv[6] = max(bank_dv[6], dv_msk[(4 * s + 3, 7)])

            pe_xnf_reader = b.cnt["pe"]

        st_spill_total = b.cnt["st"]
        sx_spill_total = b.cnt["sx"]
        pe_b1_end = b.cnt["pe"]
        ac_b1_end = b.cnt["ac"]
        dv_b1_end = b.cnt["dv"]
        pe_oint_f = pe_oint[(4 * NS - 1, 7)]
        pe_smm_f = pe_smm[(4 * NS - 1, 7)]
        dv_msk_f = dv_msk[(4 * NS - 1, 7)]
        st_cpy_f = st_cpy[4 * NS - 1]

        b.wait("gpsimd", "dv", st_cpy_f)
        b.wait("gpsimd", "pe", pe_oint_f)
        c0 = b.op("gpsimd", lambda E: E.dma_start(
            out=agi_h[:, :], in_=mst[:, :]), "sp", 16)
        b.wait("gpsimd", "sp", c0)
        ccv = b.op("gpsimd", lambda E: E.collective_compute(
            "AllGather", ALU.bypass,
            replica_groups=[[0, 1, 2, 3], [4, 5, 6, 7]],
            ins=[agi_h[:, :]], outs=[ago_h[:, :]]), "cc")
        b.wait("gpsimd", "st", st_spill_total)
        b.wait("gpsimd", "sx", sx_spill_total)
        b.wait("gpsimd", "pe", pe_b1_end)
        b.wait("gpsimd", "dv", dv_b1_end)
        b.wait("gpsimd", "ac", ac_b1_end)
        b.op("gpsimd", lambda E: E.dma_start(
            out=g3[:, :], in_=osp_h[0, :, :]), "lr", 16)
        b.op("gpsimd", lambda E: E.dma_start(
            out=g5[:, :], in_=usp_h[0, :, :]), "lr", 16)
        ldo0 = b.op("gpsimd", lambda E: E.dma_start(
            out=g6[:, :], in_=qsp_h[0, :, :]), "lr", 16)
        ldxh0 = b.op("gpsimd", lambda E: E.dma_start(
            out=g4[:, :], in_=xhsp_h[0, :, :]), "lh", 16)
        ldx0 = []
        for i in range(4):
            c = b.op("gpsimd", lambda E, i=i: E.dma_start(
                out=g1[:, i * 1024:(i + 1) * 1024],
                in_=x_h[i * 128:(i + 1) * 128, :]), "lx", 16)
            ldx0.append(c)
        b.wait("vector", "sp", c0)
        b.wait("vector", "pe", pe_oint_f)
        dv_mz = b.op("vector", lambda E: E.memset(mst[:, :], 0.0), "dv")

        wosb = wos[:, :].bitcast(BF16)
        wo_pieces = []
        for hc in range(8):
            wo_pieces.append((
                [(wo_h[hc * 128:(hc + 1) * 128, :], 0, 1024)],
                wosb[:, hc * 1024:(hc + 1) * 1024],
                wosb[:, 8192 + hc * 1024: 8192 + (hc + 1) * 1024],
                scp[:, hc:hc + 1]))
        b.wait("scalar", "sm", sm_consts)
        b.wait("vector", "sm", sm_consts)
        b.wait("scalar", "pe", pe_b1_end)
        b.wait("vector", "pe", pe_b1_end)
        dv_wos = land_split(wo_pieces, None)[1]

        b.wait("gpsimd", "cc", ccv)
        dv_pref = max(dv_mz, dv_msk_f)
        b.wait("gpsimd", "pe", max(pe_oint_f, pe_smm_f))
        pbuf = [smb, ktm]
        dv_slot = [dv_pref, dv_pref]
        for r in range(4):
            buf = pbuf[r % 2]
            psem = "sp" if r % 2 == 0 else "sq"
            b.wait("gpsimd", "dv", dv_slot[r % 2])
            csp = b.op("gpsimd", lambda E, r=r, buf=buf: E.dma_start(
                out=buf[:, :], in_=ago_h[r * 128:(r + 1) * 128, :]), psem, 16)
            b.wait("vector", psem, csp)
            b.wait("vector", "dv", dv_pref)
            dv_pref = b.op("vector", lambda E, r=r, buf=buf: E.scalar_tensor_tensor(
                mst[:, :], buf[:, :], pmb[:, r:r + 1], mst[:, :],
                op0=ALU.mult, op1=ALU.add), "dv")
            dv_slot[r % 2] = dv_pref
        ld_w2 = {}

        def load_w12(e, tag, first):
            sl = wload_seq[0] % 2
            wload_seq[0] += 1
            dst = wslot[sl][:, :].bitcast(BF16)
            if not first:
                b.waitg("sync", wbuf_free[sl])
                b.wait("sync", "lq", b.cnt["lq"])
                wsem = "wa" if sl == 0 else "wb"
                cc = b.op("sync", lambda E, e=e, sl=sl: E.dma_start(
                    out=wslot[sl][:, :], in_=w12b_h[e]), wsem, 16)
                ld_w2[tag] = (wsem, cc)
                return sl
            b.waitg("scalar", wbuf_free[sl])
            b.waitg("vector", wbuf_free[sl])
            stq = b.cnt["lq"] if slot_store[sl] else 0
            b.wait("scalar", "lq", stq)
            b.wait("vector", "lq", stq)
            b.wait("tensor", "sm", sm_consts)
            b.wait("scalar", "sm", sm_consts)
            b.wait("vector", "sm", sm_consts)
            if True:
                cbv = 0
                last = 0
                for hh in range(2):
                    hp = wbc_par[0] % 2
                    wbc_par[0] += 1
                    base = hp * 1024
                    wsem = "wc" if hp == 0 else "wd"
                    b.wait("sync", "dv", wbc_free[hp])
                    cch = b.op("sync", lambda E, e=e, hh=hh, base=base:
                               E.dma_start(
                                   out=bass.AP(wbc, base,
                                               [[2048, 128], [256, 4],
                                                [1, 256]]),
                                   in_=bass.AP(w1_h.tensor,
                                               e * 262144 + hh * 131072,
                                               [[256, 128], [32768, 4],
                                                [1, 256]])), wsem, 16)
                    b.wait("tensor", wsem, cch)
                    if e == 0 and hh == 0:
                        b.waitg("tensor", p4g[1])
                    for jc in range(2):
                        for i4 in range(4):
                            ic = hh * 4 + i4
                            cbv = b.op("tensor", lambda E, jc=jc, i4=i4, ic=ic,
                                       e=e, base=base, hh=hh: E.matmul(
                                           p7[:, jc:jc + 1],
                                           wbc[:, base + i4 * 256 + jc * 128:
                                               base + i4 * 256 +
                                               (jc + 1) * 128],
                                           lnbp[:, ic * 8 + e: ic * 8 + e + 1],
                                           start=(hh == 0 and jc == 0
                                                  and i4 == 0),
                                           stop=(hh == 1 and jc == 1
                                                 and i4 == 3)), "pe")
                    b.wait("scalar", wsem, cch)
                    b.wait("scalar", "pe", cbv)
                    for i4 in range(4):
                        ic = hh * 4 + i4
                        scol = lnsp[:, ic * 8 + e: ic * 8 + e + 1]
                        hi = dst[:, ic * 256:(ic + 1) * 256]
                        lo = dst[:, 2048 + ic * 256: 2048 + (ic + 1) * 256]
                        cac = b.op("scalar", lambda E, hi=hi, i4=i4, base=base,
                                   scol=scol: E.activation(
                                       hi, wbc[:, base + i4 * 256:
                                               base + (i4 + 1) * 256],
                                       AFT.Copy, scale=scol), "ac")
                        b.wait("vector", "ac", cac)
                        last = b.op("vector", lambda E, hi=hi, lo=lo, i4=i4,
                                    base=base, scol=scol:
                                    E.scalar_tensor_tensor(
                                        lo, wbc[:, base + i4 * 256:
                                                base + (i4 + 1) * 256],
                                        scol, hi,
                                        op0=ALU.mult, op1=ALU.subtract), "dv")
                    wbc_free[hp] = last
                b.wait("vector", "pe", cbv)
                for jc in range(2):
                    dv_bg_e[e] = b.op("vector", lambda E, jc=jc, e=e:
                                      E.tensor_tensor(
                                          bgel[:, jc * 8 + e: jc * 8 + e + 1],
                                          b1p[:, jc * 8 + e: jc * 8 + e + 1],
                                          p7[:, jc:jc + 1], op=ALU.add), "dv")
            pieces = []
            for jc in range(2):
                dmas = [(w2_h[e, jc * 128:(jc + 1) * 128, :], 0, 1024)]
                hi = dst[:, 4096 + jc * 1024: 4096 + (jc + 1) * 1024]
                lo = dst[:, 6144 + jc * 1024: 6144 + (jc + 1) * 1024]
                pieces.append((dmas, hi, lo, None))
            ld_w2[tag] = land_split(pieces, None)
            b.wait("scalar", "dv", ld_w2[tag][1])
            cst = b.op("scalar", lambda E, e=e, sl=sl: E.dma_start(
                out=w12b_h[e], in_=wslot[sl][:, :]), "lq", 16)
            wst[("w12", e)] = cst
            slot_store[sl] = cst
            return sl

        pe_prev_s_end = 0
        dv_g1_done = 0
        st_out = 0
        dv_bg_e = {}
        dv_fin_bank = {0: dv_b1_end, 1: dv_b1_end,
                       2: max(dv_b1_end, st_cpy_f), 3: max(dv_b1_end, st_cpy_f)}
        pe_hbf = [0]
        dv_cA_last = 0
        st_m1 = 0
        pe_atd = {"k": pe_smm_f, 0: 0, 1: 0}
        wbc_ac_g = [0]
        pe_m1_reader = pe_b1_end
        ac_sq = 0
        pe_g4_free = 0
        pe_g6_free = 0
        st_out_prev = 0

        for s in range(NS):
            st_out_prev = st_out
            if s == 0:
                ldo, ldxh, ldx = ldo0, ldxh0, ldx0
            else:
                b.wait("gpsimd", "ac", ac_sq)
                b.wait("gpsimd", "dv", dv_cA_last)
                b.op("gpsimd", lambda E, s=s: E.dma_start(
                    out=g3[:, :], in_=osp_h[s, :, :]), "lr", 16)
                b.wait("gpsimd", "dv", dv_cA_last)
                b.op("gpsimd", lambda E, s=s: E.dma_start(
                    out=g5[:, :], in_=usp_h[s, :, :]), "lr", 16)
                b.wait("gpsimd", "pe", pe_g6_free)
                ldo = b.op("gpsimd", lambda E, s=s: E.dma_start(
                    out=g6[:, :], in_=qsp_h[s, :, :]), "lr", 16)
                b.wait("gpsimd", "pe", pe_g4_free)
                ldxh = b.op("gpsimd", lambda E, s=s: E.dma_start(
                    out=g4[:, :], in_=xhsp_h[s, :, :]), "lh", 16)
                ldx = []
                b.wait("gpsimd", "dv", dv_g1_done)
                for i in range(4):
                    c = b.op("gpsimd", lambda E, s=s, i=i: E.dma_start(
                        out=g1[:, i * 1024:(i + 1) * 1024],
                        in_=x_h[s * 512 + i * 128: s * 512 + (i + 1) * 128, :]),
                        "lx", 16)
                    ldx.append(c)

            ctp_prev = 0
            dv_mfs = 0
            for cl in range(4):
                gc = 4 * s + cl
                b.wait("vector", "pe", ctp_prev)
                crm = b.op("vector", lambda E, gc=gc: E.reduce_max(
                    cols[:, 24:25], logit[:, gc * 8:(gc + 1) * 8],
                    mybir.AxisListType.X), "dv")
                b.wait("vector", "dv", crm)
                cdv = b.op("vector", lambda E, gc=gc: E.tensor_scalar(
                    mrow[:, :], logit[:, gc * 8:(gc + 1) * 8], cols[:, 24:25],
                    None, op0=ALU.is_equal), "dv")
                sl = p4n[0] % 2
                p4n[0] += 1
                tb = tslot[sl]
                b.waitg("tensor", p4g[sl])
                b.wait("tensor", "dv", cdv)
                ctp_prev = b.op("tensor", lambda E, cl=cl, tb=tb: E.transpose(
                    tb[0:8, :], mrow[:, :], ident[:, :]), "pe")
                b.wait("vector", "pe", ctp_prev)
                b.wait("vector", "sm", st_m1)
                dv_mfs = b.op("vector", lambda E, cl=cl, tb=tb: E.tensor_copy(
                    out=mfs[0:8, cl * 128:(cl + 1) * 128],
                    in_=tb[0:8, :]), "dv")
                p4g[sl] = ("dv", dv_mfs)
            ac_af = 0
            dv_corr = 0
            cA = 0
            atsl = [tslot[0], tslot[1],
                    pbank[5][:, 0:128], pbank[6][:, 0:128]]
            atg = [[p4g[0]], [p4g[1]],
                   [("dv", bank_dv[5]), ("ac", bank_ac[5])],
                   [("dv", bank_dv[6]), ("ac", bank_ac[6])]]
            atrot = [0]
            g6b = g6[:, :].bitcast(BF16)
            at_jobs = []

            def emit_AT(cl, par, cAc):
                nonlocal ac_af
                for hc in range(8):
                    sl = atrot[0] % 4
                    atrot[0] += 1
                    tb = atsl[sl]
                    for g in atg[sl]:
                        b.waitg("tensor", g)
                    if hc == 0:
                        b.wait("tensor", "dv", cAc)
                    if par == "k":
                        srcb = ktm[:, hc * 128:(hc + 1) * 128]
                    else:
                        srcb = wbc[:, par * 1024 + hc * 128:
                                   par * 1024 + (hc + 1) * 128]
                    ctp = b.op("tensor", lambda E, tb=tb, srcb=srcb:
                               E.transpose(tb, srcb, ident[:, :]), "pe")
                    pe_atd[par] = ctp
                    ahi = g6b[:, 2 * (hc * 512 + cl * 128):
                              2 * (hc * 512 + cl * 128) + 128]
                    alo = g6b[:, 2 * (hc * 512 + cl * 128) + 128:
                              2 * (hc * 512 + cl * 128) + 256]
                    b.wait("scalar", "pe", ctp)
                    caa = b.op("scalar", lambda E, ahi=ahi, tb=tb: E.copy(
                        out=ahi, in_=tb), "ac")
                    b.wait("vector", "ac", caa)
                    ac_af = b.op("vector", lambda E, ahi=ahi, alo=alo, tb=tb:
                                 E.tensor_tensor(alo, tb, ahi,
                                                 op=ALU.subtract), "dv")
                    atg[sl] = [("dv", ac_af)]

            for cl in range(4):
                b.wait("tensor", "lr", ldo)
                b.wait("tensor", "dv", dv_corr)
                if s == 0 and cl == 0:
                    b.wait("tensor", "ac", ac_b1_end)
                    b.wait("tensor", "dv", dv_pref)
                cpe = 0
                for h in range(8):
                    po = p0 if h < 4 else p1
                    off = (h % 4) * 128
                    cpe = b.op("tensor", lambda E, h=h, cl=cl, po=po, off=off:
                               E.matmul(
                                   po[:, off:off + 128],
                                   g6[:, h * 512 + cl * 128: h * 512 + (cl + 1) * 128],
                                   mst[:, h * 128:(h + 1) * 128],
                                   start=True, stop=True), "pe")
                b.wait("vector", "pe", cpe)
                b.op("vector", lambda E, cl=cl: E.tensor_tensor(
                    g3[:, cl * 1024: cl * 1024 + 512],
                    g3[:, cl * 1024: cl * 1024 + 512], p0[:, :], op=ALU.add), "dv")
                dv_corr = b.op("vector", lambda E, cl=cl: E.tensor_tensor(
                    g3[:, cl * 1024 + 512: (cl + 1) * 1024],
                    g3[:, cl * 1024 + 512: (cl + 1) * 1024], p1[:, :],
                    op=ALU.add), "dv")

                b.wait("scalar", "dv", dv_corr)
                if s == 0 and cl == 0:
                    b.wait("scalar", "dv", dv_pref)
                b.wait("scalar", "ac", ac_sq)
                ac_sq = b.op("scalar", lambda E, cl=cl: E.activation(
                    smb[:, :], g3[:, cl * 1024:(cl + 1) * 1024], AFT.Square), "ac")
                b.wait("vector", "ac", ac_sq)
                dsum = b.op("vector", lambda E: E.reduce_sum(
                    cols[:, 25:26], smb[:, :], mybir.AxisListType.X), "dv")
                b.wait("scalar", "dv", dsum)
                crms = b.op("scalar", lambda E: E.activation(
                    cols[:, 26:27], cols[:, 25:26], AFT.Sqrt,
                    scale=1.0 / 1024.0), "ac")
                b.wait("vector", "ac", crms)
                cea = b.op("vector", lambda E: E.tensor_scalar(
                    cols[:, 27:28], cols[:, 26:27], EPS_RMS, None,
                    op0=ALU.add), "dv")
                b.wait("vector", "dv", cea)
                crc = b.op("vector", lambda E: E.reciprocal(
                    cols[:, 28:29], cols[:, 27:28]), "dv")
                b.wait("vector", "dv", crc)
                if cl == 0:
                    b.wait("vector", "lr", ldo)
                    if s == 0:
                        b.wait("vector", "dv", dv_pref)
                if s == 0:
                    par = "k"
                    adst = ktm[:, :]
                else:
                    par = cl % 2
                    adst = wbc[:, par * 1024:(par + 1) * 1024]
                b.wait("vector", "pe", pe_atd[par])
                if s >= 1:
                    b.waitg("vector", ("ac", wbc_ac_g[0]))
                cA = b.op("vector", lambda E, cl=cl, adst=adst:
                          E.scalar_tensor_tensor(
                              adst, g3[:, cl * 1024:(cl + 1) * 1024],
                              cols[:, 28:29],
                              g5[:, cl * 1024:(cl + 1) * 1024],
                              op0=ALU.mult, op1=ALU.mult), "dv")
                at_jobs.append((cl, par, cA))
                if s == 0:
                    emit_AT(*at_jobs[cl])
                elif cl >= 1:
                    emit_AT(*at_jobs[cl - 1])
            if s >= 1:
                emit_AT(*at_jobs[3])
            p4g[0] = atg[0][0]
            p4g[1] = atg[1][0]
            bank_dv[5] = max(bank_dv[5], atg[2][0][1])
            bank_dv[6] = max(bank_dv[6], atg[3][0][1])
            dv_cA_last = cA

            for th in range(2):
                sl_cur = load_w12(0, (s, th, 0), s == 0 and th == 0)
                b.wait("scalar", "dv", dv_mfs)
                b.wait("scalar", "pe", pe_m1_reader)
                st_m1 = b.op("scalar", lambda E, th=th: E.dma_start(
                    out=m1r[0:1, :], in_=mfs[0:8, th * 256:(th + 1) * 256]),
                    "sm", 16)
                b.wait("vector", "sm", st_m1)
                dv_m1 = b.op("vector", lambda E: E.tensor_scalar(
                    m1r[0:1, :], m1r[0:1, :], -1.0, None, op0=ALU.add), "dv")
                grp_last = {}
                for tl in range(2):
                    for dh in range(2):
                        pb = tl * 2 + dh
                        ps = pbank[pb]
                        cl = th * 2 + tl
                        b.wait("tensor", "dv", max(dv_fin_bank[pb], dv_corr))
                        b.wait("tensor", "dv", dv_mfs)
                        b.op("tensor", lambda E, ps=ps, cl=cl, dh=dh: E.matmul(
                            ps[:, :], mfs[0:8, cl * 128:(cl + 1) * 128],
                            b2s[0:8, dh * 512:(dh + 1) * 512],
                            start=True, stop=False), "pe")

                for e in range(NE):
                    sl_next = None
                    if e < NE - 1:
                        sl_next = load_w12(e + 1, (s, th, e + 1),
                                           s == 0 and th == 0)
                    wdst = wslot[sl_cur][:, :].bitcast(BF16)
                    g4b = g4[:, :].bitcast(BF16)
                    hbb = hbf[:, :].bitcast(BF16)

                    dv_hj = {}
                    for jc in range(2):
                        pb = 5 + jc
                        b.waitg("tensor", ld_w2[(s, th, e)])
                        b.wait("tensor", "dv", max(bank_dv[pb], dv_m1))
                        b.wait("tensor", "lh", ldxh)
                        b.wait("tensor", "ac", bank_ac[pb])
                        for ic in range(8):
                            whi = wdst[:, ic * 256 + jc * 128:
                                       ic * 256 + (jc + 1) * 128]
                            wlo = wdst[:, 2048 + ic * 256 + jc * 128:
                                       2048 + ic * 256 + (jc + 1) * 128]
                            xhi = g4b[:, ic * 512 + th * 256:
                                      ic * 512 + (th + 1) * 256]
                            xlo = g4b[:, 4096 + ic * 512 + th * 256:
                                      4096 + ic * 512 + (th + 1) * 256]
                            st0 = (ic == 0)
                            b.op("tensor", lambda E, whi=whi, xhi=xhi, jc=jc,
                                 st0=st0: E.matmul(
                                     pbank[5 + jc][:, 0:256], whi, xhi,
                                     start=st0, stop=False), "pe")
                            b.op("tensor", lambda E, whi=whi, xlo=xlo, jc=jc:
                                 E.matmul(pbank[5 + jc][:, 0:256], whi, xlo,
                                          start=False, stop=False), "pe")
                            b.op("tensor", lambda E, wlo=wlo, xhi=xhi, jc=jc:
                                 E.matmul(pbank[5 + jc][:, 0:256], wlo, xhi,
                                          start=False, stop=False), "pe")
                        cg = b.op("tensor", lambda E, jc=jc, th=th, e=e: E.matmul(
                            pbank[5 + jc][:, 0:256], bigr[0:1, 0:128],
                            m1r[0:1, e * 256:(e + 1) * 256],
                            start=False, stop=True), "pe")
                        b.wait("scalar", "pe", max(cg, pe_hbf[0]))
                        if s == 0:
                            b.wait("scalar", "dv", dv_bg_e[e])
                        cge = b.op("scalar", lambda E, jc=jc, e=e:
                                   E.activation(
                                       pbank[5 + jc][:, 256:512],
                                       pbank[5 + jc][:, 0:256], AFT.Gelu,
                                       bias=bgel[:, jc * 8 + e: jc * 8 + e + 1]),
                                   "ac")
                        hhi = hbb[:, jc * 256:(jc + 1) * 256]
                        hlo = hbb[:, 512 + jc * 256: 512 + (jc + 1) * 256]
                        b.wait("scalar", "ac", cge)
                        chc = b.op("scalar", lambda E, jc=jc, hhi=hhi: E.copy(
                            out=hhi, in_=pbank[5 + jc][:, 256:512]), "ac")
                        b.wait("vector", "ac", chc)
                        dv_hj[jc] = b.op("vector", lambda E, jc=jc, hhi=hhi,
                                         hlo=hlo: E.tensor_tensor(
                                             hlo, pbank[5 + jc][:, 256:512], hhi,
                                             op=ALU.subtract), "dv")
                        bank_ac[pb] = chc
                        bank_dv[pb] = dv_hj[jc]

                    cw2 = 0
                    for jc in range(2):
                        b.wait("tensor", "dv", dv_hj[jc])
                        for tl in range(2):
                            for dh in range(2):
                                pb = tl * 2 + dh
                                hhi = hbb[:, jc * 256 + tl * 128:
                                          jc * 256 + (tl + 1) * 128]
                                hlo = hbb[:, 512 + jc * 256 + tl * 128:
                                          512 + jc * 256 + (tl + 1) * 128]
                                whi = wdst[:, 4096 + jc * 1024 + dh * 512:
                                           4096 + jc * 1024 + (dh + 1) * 512]
                                wlo = wdst[:, 6144 + jc * 1024 + dh * 512:
                                           6144 + jc * 1024 + (dh + 1) * 512]
                                b.op("tensor", lambda E, hhi=hhi, whi=whi, pb=pb:
                                     E.matmul(pbank[pb][:, :], hhi, whi,
                                              start=False, stop=False), "pe")
                                b.op("tensor", lambda E, hhi=hhi, wlo=wlo, pb=pb:
                                     E.matmul(pbank[pb][:, :], hhi, wlo,
                                              start=False, stop=False), "pe")
                                cw2 = b.op("tensor", lambda E, hlo=hlo, whi=whi,
                                           pb=pb: E.matmul(
                                               pbank[pb][:, :], hlo, whi,
                                               start=False, stop=False), "pe")
                    pe_hbf[0] = cw2
                    wbuf_free[sl_cur] = ("pe", cw2)
                    if sl_next is not None:
                        sl_cur = sl_next
                pe_m1_reader = b.cnt["pe"]
                if th == 1:
                    pe_g4_free = b.cnt["pe"]
                b.wait("tensor", "dv", max(ac_af, dv_wos))
                g6b = g6[:, :].bitcast(BF16)
                for tl in range(2):
                    for dh in range(2):
                        pb = tl * 2 + dh
                        cl = th * 2 + tl
                        for hc in range(8):
                            ahi = g6b[:, 2 * (hc * 512 + cl * 128):
                                      2 * (hc * 512 + cl * 128) + 128]
                            alo = g6b[:, 2 * (hc * 512 + cl * 128) + 128:
                                      2 * (hc * 512 + cl * 128) + 256]
                            whi = wosb[:, hc * 1024 + dh * 512:
                                       hc * 1024 + (dh + 1) * 512]
                            wlo = wosb[:, 8192 + hc * 1024 + dh * 512:
                                       8192 + hc * 1024 + (dh + 1) * 512]
                            b.op("tensor", lambda E, pb=pb, ahi=ahi, whi=whi:
                                 E.matmul(pbank[pb][:, :], ahi, whi,
                                          start=False, stop=False), "pe")
                            b.op("tensor", lambda E, pb=pb, ahi=ahi, wlo=wlo:
                                 E.matmul(pbank[pb][:, :], ahi, wlo,
                                          start=False, stop=False), "pe")
                            cwo = b.op("tensor", lambda E, pb=pb, alo=alo,
                                       whi=whi, hc=hc: E.matmul(
                                           pbank[pb][:, :], alo, whi,
                                           start=False, stop=(hc == 7)), "pe")
                        grp_last[pb] = cwo
                if th == 1:
                    pe_g6_free = b.cnt["pe"]

                for tl in range(2):
                    cl = th * 2 + tl
                    dv_last = 0
                    for dh in range(2):
                        pb = tl * 2 + dh
                        b.wait("vector", "pe", grp_last[pb])
                        b.wait("vector", "lx", ldx[3])
                        if th == 0 and tl == 0 and dh == 0:
                            b.wait("vector", "st", st_out_prev)
                            if s == 0:
                                b.wait("vector", "ac", ac_b1_end)
                                b.wait("vector", "sx", sx_spill_total)
                        dvf = b.op("vector", lambda E, cl=cl, dh=dh, pb=pb:
                                   E.tensor_tensor(
                                       g2[:, cl * 1024 + dh * 512:
                                          cl * 1024 + (dh + 1) * 512],
                                       g1[:, cl * 1024 + dh * 512:
                                          cl * 1024 + (dh + 1) * 512],
                                       pbank[pb][:, :], op=ALU.add), "dv")
                        dv_fin_bank[pb] = dvf
                        dv_last = dvf
                    dv_g1_done = dv_last
                    b.wait("scalar", "dv", dv_last)
                    st_out = b.op("scalar", lambda E, s=s, cl=cl: E.dma_start(
                        out=out_h[s * 512 + cl * 128: s * 512 + (cl + 1) * 128, :],
                        in_=g2[:, cl * 1024:(cl + 1) * 1024]), "st", 16)

            pe_prev_s_end = b.cnt["pe"]
            if s == 0:
                wbc_ac_g[0] = b.cnt["ac"]

        b.wait("scalar", "st", b.cnt["st"])
        b.flush()

    return nc


_NC_CACHE = [None]


def kernel(**inputs):
    x = np.ascontiguousarray(np.asarray(inputs["L0_attn_x"]), dtype=np.float32)

    def f32c(a):
        return np.ascontiguousarray(np.asarray(a), dtype=np.float32)

    shared = {
        "wq": f32c(np.asarray(inputs["L0_attn_Wq"]).T),
        "wk": f32c(np.asarray(inputs["L0_attn_Wk"]).T),
        "wv": f32c(np.asarray(inputs["L0_attn_Wv"]).T),
        "wu": f32c(np.asarray(inputs["L0_attn_Wu"]).T),
        "wo": f32c(np.asarray(inputs["L0_attn_Wo"]).T),
        "wg": f32c(np.asarray(inputs["L0_ffn_Wg"]).T),
        "bqp": f32c(np.asarray(inputs["L0_attn_bq"]).reshape(8, 128).T),
        "bkp": f32c(np.asarray(inputs["L0_attn_bk"]).reshape(8, 128).T),
        "bvr": f32c(np.asarray(inputs["L0_attn_bv"]).reshape(1, 1024)),
        "bur": f32c(np.asarray(inputs["L0_attn_bu"]).reshape(1, 1024)),
        "bor": f32c(np.asarray(inputs["L0_attn_bo"]).reshape(1, 1024)),
        "scp": f32c(np.asarray(inputs["L0_attn_scale"]).reshape(8, 128).T),
        "lns": f32c(np.asarray(inputs["L0_ffn_ln_scale"]).T),
        "lnb": f32c(np.asarray(inputs["L0_ffn_ln_bias"]).T),
        "w1": f32c(inputs["L0_ffn_W1"]),
        "b1t": f32c(np.asarray(inputs["L0_ffn_b1"]).T),
        "w2": f32c(inputs["L0_ffn_W2"]),
        "b2": f32c(inputs["L0_ffn_b2"]),
        "ident": np.eye(128, dtype=np.float32),
        "cmask": np.triu(np.ones((128, 128), dtype=np.float32)),
    }

    in_maps = []
    for c in range(NCORES):
        bb, sl = c // 4, c % 4
        m = dict(shared)
        m["x"] = np.ascontiguousarray(x[bb, sl * T:(sl + 1) * T, :])
        m["pmask"] = np.array([[1.0 if r < sl else 0.0
                                for r in range(NCORES)]], dtype=np.float32)
        in_maps.append(m)

    if _NC_CACHE[0] is None:
        _NC_CACHE[0] = build_nc()
    nc = _NC_CACHE[0]

    res = run_bass_kernel_spmd(nc, in_maps, core_ids=list(range(NCORES)))
    out = np.empty((B, N, D), dtype=np.float32)
    for c in range(NCORES):
        bb, sl = c // 4, c % 4
        out[bb, sl * T:(sl + 1) * T, :] = res.results[c]["out"]
    return out


if __name__ == "__main__":
    build_nc()
    print("build ok")
